# version 30
# baseline (speedup 1.0000x reference)
import contextlib
import numpy as np
import concourse.bass as bass
import concourse.mybir as mybir
from concourse.bass import IndirectOffsetOnAxis
from concourse.bass_utils import run_bass_kernel_spmd

F32 = mybir.dt.float32
BF16 = mybir.dt.bfloat16
I32 = mybir.dt.int32
ALU = mybir.AluOpType
AF = mybir.ActivationFunctionType
AX = mybir.AxisListType

D = 1024
PG = 8
HID = 2048
T_OWN = 2048
HALO = 256
NSLC = 4
NE = 8
EPS = 1e-5
SCL = 1.0 / 512.0
CAP = 768
NCH = CAP // 128

C_ID = 0
C_M0 = 128
C_M2 = 256
C_ONE = 384
C_IOTA = 512
C_TOK = 1280
C_TOT = 1296

_ENG2NAME = {"tensor": "T", "vector": "V", "scalar": "A",
             "gpsimd": "G", "sync": "SY"}


class Sems:
    def __init__(self):
        self.cnt = {}
        self.sem = {}
        self.lastw = {}

    def add(self, name, handle):
        self.sem[name] = handle
        self.cnt[name] = 0

    def inc(self, instr, name, k=1):
        instr.then_inc(self.sem[name], k)
        self.cnt[name] += k
        return (name, self.cnt[name])

    def tok(self, name):
        return (name, self.cnt[name])


def build_program(dbg_skip=()):
    nc = bass.Bass("TRN2", target_bir_lowering=False)

    x_d = nc.dram_tensor("x", [HALO + T_OWN, D], F32, kind="ExternalInput").ap()
    wqkv_d = nc.dram_tensor("wqkv", [D, 3 * D], F32, kind="ExternalInput").ap()
    bqkvt_d = nc.dram_tensor("bqkvt", [128, 24], F32, kind="ExternalInput").ap()
    bqkvv_d = nc.dram_tensor("bqkvv", [1, D], F32, kind="ExternalInput").ap()
    gatet_d = nc.dram_tensor("gatet", [D, NE], F32, kind="ExternalInput").ap()
    eb1_d = nc.dram_tensor("eb1", [1, NE], F32, kind="ExternalInput").ap()
    wut_d = nc.dram_tensor("wut", [NE, D, HID], F32, kind="ExternalInput").ap()
    wdt_d = nc.dram_tensor("wdt", [NE, HID, D], F32, kind="ExternalInput").ap()
    lnw_d = nc.dram_tensor("lnw", [128, D], F32, kind="ExternalInput").ap()
    lnb_d = nc.dram_tensor("lnb", [128, D], F32, kind="ExternalInput").ap()
    cst_d = nc.dram_tensor("cst", [128, C_TOT], F32, kind="ExternalInput").ap()
    hv_d = nc.dram_tensor("hv", [128, 1], F32, kind="ExternalInput").ap()
    out_d = nc.dram_tensor("out", [T_OWN, D], F32, kind="ExternalOutput").ap()
    xns_d = nc.dram_tensor("xnscr", [T_OWN, D], F32, kind="Internal").ap()

    sm = Sems()

    with contextlib.ExitStack() as ex:
        ra_h = ex.enter_context(nc.sbuf_tensor("rarena", [128, 20480], F32))

        def rseg(off, n):
            return ra_h[:, off:off + n]

        acc = rseg(0, 4096)
        acc2 = rseg(4096, 4096)
        atk = rseg(8192, 1024)
        atkB = rseg(9216, 1024)
        wQ = rseg(10240, 4096)
        wQ2 = rseg(14336, 4096)
        xgs = rseg(0, 6144)
        yg = rseg(6144, 6144)
        wst = [rseg(12288, 4096), rseg(16384, 4096)]

        VARENA_B = 106496
        back = ex.enter_context(
            nc.sbuf_tensor("varena", [128, VARENA_B], mybir.dt.uint8))
        base_addr = nc.lookup_mloc(back).addr

        def valloc(name, cols, off, dt=F32, rows=128):
            h = nc.virtual_sbuf_tensor(name, [rows, cols], dt)
            nc._tensor(h.name, h.shape, h.dtype, type="SB")
            mloc = nc.lookup_mloc(h)
            mloc.allocated = True
            mloc.addr = base_addr + off
            return h

        xnT = valloc("xnT", 4096, 0)
        qTs = valloc("qTs", 4096, 16384)
        kTr = valloc("kTr", 4096, 32768)
        vrb = valloc("vrb", 4096, 49152)
        STb = valloc("STb", 384, 65536)
        aTb = valloc("aTb", 1024, 67072)
        lnscr = valloc("lnscr", 1024, 71168)
        atkC = valloc("atkC", 1024, 75264)
        xnT2 = valloc("xnT2", 4096, 79360)
        Epr = valloc("Epr", 1536, 71168)
        idxrow = valloc("idxrow", 768, 95744)
        wBh = [valloc("wBh0", 4096, 0, dt=BF16),
               valloc("wBh1", 4096, 16384, dt=BF16)]
        wBl = [valloc("wBl0", 4096, 8192, dt=BF16),
               valloc("wBl1", 4096, 24576, dt=BF16)]
        xgTh = valloc("xgTh", 6144, 32768, dt=BF16)
        xgTl = valloc("xgTl", 6144, 45056, dt=BF16)
        Hh = valloc("Hh", 12288, 57344, dt=BF16)
        Hl = valloc("Hl", 12288, 81920, dt=BF16)

        def sbt(name, cols, rows=128, dt=F32):
            return ex.enter_context(nc.sbuf_tensor(name, [rows, cols], dt))

        cst = sbt("cst_s", C_TOT)
        rsq = sbt("rsq", 512)
        lnwb = sbt("lnwb", 1024)
        lnbb = sbt("lnbb", 1024)
        bqt = sbt("bqt", 24)
        gwT = sbt("gwT", 64)
        ebb = sbt("ebb", 8)
        gsc = sbt("gsc", 32)
        gbi = sbt("gbi", 32)
        gt8 = sbt("gt8", 32)
        g2f = sbt("g2f", 128)
        gmf = sbt("gmf", 128)
        cmf = sbt("cmf", 128)
        stats = sbt("stats", 32)
        hv = sbt("hv_s", 1)
        cg = sbt("cgst", 32)
        idxc = sbt("idxc", NE * NCH, dt=I32)
        g2gc = sbt("g2gc", NE * NCH)
        bqv = ex.enter_context(nc.sbuf_tensor("bqv", [1, D], F32))
        eb1 = ex.enter_context(nc.sbuf_tensor("eb1s", [1, NE], F32))
        base = ex.enter_context(nc.sbuf_tensor("base8", [1, NE], F32))

        PS = [ex.enter_context(nc.psum_tensor(f"ps{i}", [128, 512], F32))
              for i in range(8)]
        for name in ("T", "V", "A", "G", "LC", "LQ", "LR", "L0", "L1", "LX",
                     "LY", "LS", "LT", "LG", "SC", "ST", "SU"):
            sm.add(name, ex.enter_context(nc.semaphore(name)))

        ident = cst[:, C_ID:C_ID + 128]
        mask0 = cst[:, C_M0:C_M0 + 128]
        mask2 = cst[:, C_M2:C_M2 + 128]
        ones1 = cst[0:1, C_ONE:C_ONE + 128]

        st = {
            "bw": [None] * 8, "br": [None] * 8,
            "t_xnT": None, "r_xnT": None,
            "t_qTs": None, "r_qTs": None,
            "t_kv": [None] * 4, "r_kv": [None] * 4,
            "r_wQ": [None, None],
            "t_ST": None, "r_ST": None,
            "t_aT": None, "r_aT": None,
            "t_atk": None, "r_atk": None, "r_atkB": None, "r_atkC": None,
            "t_accld": None, "acc_wv": None,
            "t_wA": [None, None], "r_wA": [None, None],
            "r_wh": [None, None], "r_stage": [None, None],
            "t_wB": None, "r_wB": None, "t_wlo": [None, None],
            "t_Hb": None, "r_Hb": None,
            "ln_a": None, "ln_a2": None,
        }

        with nc.Block() as block:
            def sec(kind, fn):
                wname = _ENG2NAME[kind]

                def wrapped(eng):
                    def w(tok):
                        if tok is None:
                            return
                        name, val = tok
                        if val <= 0:
                            return
                        key = (wname, name)
                        if sm.lastw.get(key, 0) >= val:
                            return
                        eng.wait_ge(sm.sem[name], val)
                        sm.lastw[key] = val
                    fn(eng, w)
                getattr(block, kind)(wrapped)

            def _(sync, w):
                for dst, src in [
                    (cst[:], cst_d[:]), (lnwb[:], lnw_d[:]), (lnbb[:], lnb_d[:]),
                    (bqt[:], bqkvt_d[:]), (bqv[:], bqkvv_d[:]),
                    (eb1[:], eb1_d[:]), (hv[:], hv_d[:]),
                    (gwT[:, 0:64].rearrange("p (g f) -> p g f", f=NE),
                     gatet_d[:, :].rearrange("(g p) f -> p g f", p=128)),
                ]:
                    sm.inc(sync.dma_start(out=dst, in_=src), "LC", 16)
                sm.inc(sync.dma_start(
                    out=acc[:, 0:2048].rearrange("p (t f) -> p t f", f=D),
                    in_=x_d[0:HALO, :].rearrange("(t p) f -> p t f", p=128)),
                    "LC", 16)
            sec("sync", _)
            ld_all = sm.tok("LC")

            def _(vector, w):
                v = vector.memset(base[:, :], 0.0)
                sm.inc(v, "V")
            sec("vector", _)
            base_tok = sm.tok("V")

            def _(tensor, w):
                w(ld_all)
                t = tensor.matmul(PS[1][:, 0:8], ones1, eb1[0:1, :],
                                  start=True, stop=True)
                sm.inc(t, "T")
            sec("tensor", _)
            st["bw"][1] = sm.tok("T")

            def _(vector, w):
                w(st["bw"][1])
                v = vector.tensor_copy(out=ebb[:], in_=PS[1][:, 0:8])
                sm.inc(v, "V")
            sec("vector", _)
            st["br"][1] = sm.tok("V")
            gwT_rdy = ld_all

            def emit_ln(src_ap, dst_ap, scr_ap, pre, so=0, akey="ln_a"):
                b6 = 8 + (so // 4) * 12
                def _(vector, w):
                    for tk in pre:
                        w(tk)
                    w(st[akey])
                    v = vector.bn_stats(out=stats[:, b6:b6 + 6],
                                        in_=src_ap[:, 0:512])
                    w(sm.inc(v, "V"))
                    v = vector.bn_stats(out=stats[:, b6 + 6:b6 + 12],
                                        in_=src_ap[:, 512:1024])
                    w(sm.inc(v, "V"))
                    v = vector.bn_aggr(out=stats[:, so:so + 2],
                                       in_=stats[:, b6:b6 + 12])
                    w(sm.inc(v, "V"))
                    v = vector.tensor_scalar(out=stats[:, so + 1:so + 2],
                                             in0=stats[:, so + 1:so + 2],
                                             scalar1=EPS, scalar2=None,
                                             op0=ALU.add)
                    sm.inc(v, "V")
                sec("vector", _)
                vtok = sm.tok("V")

                def _(scalar, w):
                    w(vtok)
                    a = scalar.activation(out=stats[:, so + 2:so + 3],
                                          in_=stats[:, so + 1:so + 2],
                                          func=AF.Sqrt)
                    sm.inc(a, "A")
                sec("scalar", _)
                st[akey] = sm.tok("A")
                a_sqrt = sm.tok("A")

                def _(vector, w):
                    for tk in pre:
                        w(tk)
                    w(a_sqrt)
                    v = vector.reciprocal(out=stats[:, so + 3:so + 4],
                                          in_=stats[:, so + 2:so + 3])
                    w(sm.inc(v, "V"))
                    v = vector.tensor_scalar(out=dst_ap, in0=src_ap,
                                             scalar1=stats[:, so:so + 1],
                                             scalar2=stats[:, so + 3:so + 4],
                                             op0=ALU.subtract, op1=ALU.mult)
                    sm.inc(v, "V")
                sec("vector", _)
                return sm.tok("V")

            def emit_tile_transposes(src_buf, dst_positions, pre_t, pre_v):
                def _(tensor, w):
                    for tk in pre_t:
                        w(tk)
                    w(st["br"][0])
                    w(st["br"][1])
                    t = None
                    for p in range(PG):
                        bank = PS[p // 4]
                        t = tensor.transpose(
                            bank[:, (p % 4) * 128:(p % 4) * 128 + 128],
                            src_buf[:, p * 128:(p + 1) * 128], ident)
                    sm.inc(t, "T")
                sec("tensor", _)
                ttok = sm.tok("T")
                st["bw"][0] = st["bw"][1] = ttok

                def _(vector, w):
                    w(ttok)
                    for tk in pre_v:
                        w(tk)
                    v = None
                    for p in range(PG):
                        bank = PS[p // 4]
                        v = vector.tensor_copy(
                            out=dst_positions[p],
                            in_=bank[:, (p % 4) * 128:(p % 4) * 128 + 128])
                    sm.inc(v, "V")
                sec("vector", _)
                vtok = sm.tok("V")
                st["br"][0] = st["br"][1] = vtok
                return ttok, vtok

            wq_state = {"p": 0}

            def wq_load(j):
                par = wq_state["p"]
                wq_state["p"] ^= 1
                buf = wQ if par == 0 else wQ2
                sname = "LQ" if par == 0 else "LR"

                def _(sync, w, _par=par, _buf=buf, _sn=sname, _j=j):
                    w(st["r_wQ"][_par])
                    sm.inc(sync.dma_start(
                        out=_buf[:, 0:4096].rearrange("p (g f) -> p g f",
                                                      f=512),
                        in_=wqkv_d[:, _j * 512:(_j + 1) * 512]
                        .rearrange("(g p) f -> p g f", p=128)), _sn, 16)
                sec("sync", _)
                return sm.tok(sname), buf, par

            def k_prod(j, src_buf, src_tok_key, tok_off, n_tok, slot0,
                       par=None):
                t_wq, wqb, wqp = wq_load(j)
                nsl = n_tok // 128
                for cb in range(4):
                    page = (j - 2) * 4 + cb
                    bank = 6 + cb % 2

                    def _(tensor, w, _cb=cb, _bank=bank, _twq=t_wq, _wq=wqb):
                        w(_twq)
                        w(st["br"][_bank])
                        w(st[src_tok_key])
                        t = None
                        for p in range(PG):
                            t = tensor.matmul(
                                PS[_bank][:, 0:n_tok],
                                _wq[:, p * 512 + _cb * 128:
                                    p * 512 + _cb * 128 + 128],
                                src_buf[:, p * 512 + tok_off:
                                        p * 512 + tok_off + n_tok],
                                start=(p == 0), stop=(p == PG - 1))
                        sm.inc(t, "T")
                    sec("tensor", _)
                    st["bw"][bank] = sm.tok("T")
                    st["r_wQ"][wqp] = sm.tok("T")
                    if par is not None:
                        st["r_xnTp"][par] = sm.tok("T")
                    else:
                        st["r_qTs"] = sm.tok("T")

                    def _(vector, w, _page=page, _bank=bank):
                        w(st["bw"][_bank])
                        for sl in range(slot0, slot0 + nsl):
                            w(st["r_kv"][sl])
                        v = vector.tensor_scalar(
                            out=kTr[:, _page * 512 + slot0 * 128:
                                    _page * 512 + slot0 * 128 + n_tok],
                            in0=PS[_bank][:, 0:n_tok],
                            scalar1=bqt[:, 8 + _page:9 + _page], scalar2=None,
                            op0=ALU.add)
                        sm.inc(v, "V")
                    sec("vector", _)
                    st["br"][bank] = sm.tok("V")
                for sl in range(slot0, slot0 + nsl):
                    st["t_kv"][sl] = sm.tok("V")

            def v_prod(j, src_buf, src_tok_key, tok_off, n_tok, slot0,
                       par=None):
                t_wq, wqb, wqp = wq_load(j)
                for i in range(n_tok // 128):
                    sl = slot0 + i
                    bank = 6 + i % 2

                    def _(tensor, w, _i=i, _bank=bank, _twq=t_wq, _wq=wqb):
                        w(_twq)
                        w(st["br"][_bank])
                        w(st[src_tok_key])
                        w(ld_all)
                        col = tok_off + _i * 128
                        for p in range(PG):
                            tensor.matmul(
                                PS[_bank][:, 0:512],
                                src_buf[:, p * 512 + col: p * 512 + col + 128],
                                _wq[:, p * 512: p * 512 + 512],
                                start=(p == 0), stop=False)
                        t = tensor.matmul(
                            PS[_bank][:, 0:512], ones1,
                            bqv[0:1, (j - 4) * 512:(j - 4) * 512 + 512],
                            start=False, stop=True)
                        sm.inc(t, "T")
                    sec("tensor", _)
                    st["bw"][bank] = sm.tok("T")
                    st["r_wQ"][wqp] = sm.tok("T")
                    if par is not None:
                        st["r_xnTp"][par] = sm.tok("T")
                    else:
                        st["r_qTs"] = sm.tok("T")

                    def _(vector, w, _sl=sl, _bank=bank, _j=j):
                        w(st["bw"][_bank])
                        w(st["r_kv"][_sl])
                        v = vector.tensor_copy(
                            out=vrb[:, _sl * 1024 + (_j - 4) * 512:
                                    _sl * 1024 + (_j - 4) * 512 + 512],
                            in_=PS[_bank][:, 0:512])
                        sm.inc(v, "V")
                    sec("vector", _)
                    st["br"][bank] = sm.tok("V")
                    st["t_kv"][sl] = sm.tok("V")

            for ht in range(2):
                lt = emit_ln(acc[:, ht * 1024:(ht + 1) * 1024], atk[:, :],
                             lnscr[:, :],
                             pre=[ld_all, st["r_atk"]])
                dsts = [qTs[:, p * 512 + ht * 128: p * 512 + ht * 128 + 128]
                        for p in range(PG)]
                tt, vt = emit_tile_transposes(atk, dsts, pre_t=[lt],
                                              pre_v=[st["r_qTs"]])
                st["r_atk"] = tt
                st["t_qTs"] = vt

            k_prod(2, qTs, "t_qTs", 0, 256, 0)
            k_prod(3, qTs, "t_qTs", 0, 256, 0)
            v_prod(4, qTs, "t_qTs", 0, 256, 0)
            v_prod(5, qTs, "t_qTs", 0, 256, 0)

            def _(vector, w):
                w(st["t_kv"][0])
                w(st["t_kv"][1])
                w(ld_all)
                kh = kTr[:, 0:4096].rearrange("p (g f) -> p g f",
                                              f=512)[:, :, 0:256]
                v = vector.tensor_scalar(out=kh, in0=kh, scalar1=hv[:, 0:1],
                                         scalar2=None, op0=ALU.mult)
                w(sm.inc(v, "V"))
                v = vector.tensor_scalar(out=vrb[:, 0:2048],
                                         in0=vrb[:, 0:2048],
                                         scalar1=hv[:, 0:1], scalar2=None,
                                         op0=ALU.mult)
                sm.inc(v, "V")
            sec("vector", _)
            st["t_kv"][0] = st["t_kv"][1] = sm.tok("V")
            halo_v_done = sm.tok("V")

            ls_cnt = 0
            st["t_xnTp"] = [None, None]
            st["r_xnTp"] = [None, None]

            def emit_front(_s):
                nonlocal ls_cnt
                _ax = [acc, acc2][_s % 2]
                _xt = [xnT, xnT2][_s % 2]

                def _(sync, w):
                    w(("ST" if _s % 2 == 0 else "SU", 16 * (_s // 2)))
                    if _s == 0:
                        w(halo_v_done)
                        w(st["r_qTs"])
                    sm.inc(sync.dma_start(
                        out=_ax[:, 0:4096].rearrange("p (t f) -> p t f", f=D),
                        in_=x_d[HALO + _s * 512: HALO + (_s + 1) * 512, :]
                        .rearrange("(t p) f -> p t f", p=128)),
                        "LX" if _s % 2 == 0 else "LY", 16)
                sec("sync", _)
                st["t_accld"] = sm.tok("LX" if _s % 2 == 0 else "LY")

                for t in range(4):
                    ab = [atk, atkB][t % 2]
                    key = "r_atk" if t % 2 == 0 else "r_atkB"
                    lsn = "LS" if t % 2 == 0 else "LT"
                    lt = emit_ln(_ax[:, t * 1024:(t + 1) * 1024], ab[:, :],
                                 lnscr[:, :],
                                 pre=[st["t_accld"], st.get(key),
                                      sm.tok(lsn)])
                    def _(scalar, w, _t=t, _lt=lt, _ab=ab, _ls=lsn):
                        w(_lt)
                        sm.inc(scalar.dma_start(
                            out=xns_d[(_s * 4 + _t) * 128:
                                      (_s * 4 + _t) * 128 + 128, :],
                            in_=_ab[:, :]), _ls, 16)
                    sec("scalar", _)
                    ls_cnt += 16
                    dsts = [_xt[:, p * 512 + t * 128: p * 512 + t * 128 + 128]
                            for p in range(PG)]
                    tt, vt = emit_tile_transposes(
                        ab, dsts, pre_t=[lt],
                        pre_v=[st["r_xnTp"][_s % 2]])
                    st[key] = tt
                st["t_xnTp"][_s % 2] = sm.tok("V")

            emit_front(0)
            for s in range(NSLC):
                g0 = s * 4
                accx = [acc, acc2][s % 2]
                xnT_s = [xnT, xnT2][s % 2]
                st["t_xnT"] = st["t_xnTp"][s % 2]

                for j in (0, 1):
                    t_wq, wqb, wqp = wq_load(j)
                    for cb in range(4):
                        page = j * 4 + cb
                        bank = 6 + cb % 2

                        def _(tensor, w, _cb=cb, _bank=bank, _twq=t_wq,
                              _wq=wqb):
                            w(_twq)
                            w(st["br"][_bank])
                            w(st["t_xnT"])
                            t = None
                            for p in range(PG):
                                t = tensor.matmul(
                                    PS[_bank][:, 0:512],
                                    _wq[:, p * 512 + _cb * 128:
                                        p * 512 + _cb * 128 + 128],
                                    xnT_s[:, p * 512: p * 512 + 512],
                                    start=(p == 0), stop=(p == PG - 1))
                            sm.inc(t, "T")
                        sec("tensor", _)
                        st["bw"][bank] = sm.tok("T")
                        st["r_wQ"][wqp] = sm.tok("T")
                        st["r_xnTp"][s % 2] = sm.tok("T")

                        def _(vector, w, _page=page, _bank=bank):
                            w(st["bw"][_bank])
                            w(st["r_qTs"])
                            v = vector.tensor_scalar(
                                out=qTs[:, _page * 512: _page * 512 + 512],
                                in0=PS[_bank][:, 0:512],
                                scalar1=bqt[:, _page:_page + 1], scalar2=None,
                                op0=ALU.add)
                            sm.inc(v, "V")
                        sec("vector", _)
                        st["br"][bank] = sm.tok("V")
                st["t_qTs"] = sm.tok("V")

                def _(tensor, w):
                    w(st["br"][2])
                    w(st["t_xnT"])
                    w(gwT_rdy)
                    t = None
                    for t4 in range(4):
                        for p in range(PG):
                            t = tensor.matmul(
                                PS[2][:, t4 * 8: t4 * 8 + 8],
                                xnT_s[:, p * 512 + t4 * 128:
                                    p * 512 + t4 * 128 + 128],
                                gwT[:, p * 8:(p + 1) * 8],
                                start=(p == 0), stop=(p == PG - 1))
                    sm.inc(t, "T")
                sec("tensor", _)
                st["bw"][2] = sm.tok("T")
                st["r_xnTp"][s % 2] = sm.tok("T")

                prev_gate_v = sm.tok("V")

                def _(scalar, w, _pg=prev_gate_v):
                    w(st["bw"][2])
                    w(_pg)
                    a = scalar.activation(out=gsc[:, 0:32], in_=PS[2][:, 0:32],
                                          func=AF.Sigmoid)
                    sm.inc(a, "A")
                sec("scalar", _)
                a_g = sm.tok("A")
                st["br"][2] = a_g

                def _(vector, w, _ag=a_g, _s=s):
                    w(_ag)
                    for t4 in range(4):
                        c = t4 * 8
                        gcol = (_s * 4 + t4) * 8
                        v = vector.tensor_tensor(out=gbi[:, c:c + 8],
                                                 in0=gsc[:, c:c + 8],
                                                 in1=ebb[:, 0:8], op=ALU.add)
                        w(sm.inc(v, "V"))
                        v = vector.max(out=gt8[:, c:c + 8], in_=gbi[:, c:c + 8])
                        w(sm.inc(v, "V"))
                        v = vector.tensor_scalar(out=gmf[:, gcol:gcol + 8],
                                                 in0=gbi[:, c:c + 8],
                                                 scalar1=gt8[:, c + 1:c + 2],
                                                 scalar2=None, op0=ALU.is_ge)
                        w(sm.inc(v, "V"))
                        v = vector.tensor_tensor(out=gbi[:, c:c + 8],
                                                 in0=gmf[:, gcol:gcol + 8],
                                                 in1=gsc[:, c:c + 8],
                                                 op=ALU.mult)
                        w(sm.inc(v, "V"))
                        v = vector.tensor_tensor(out=g2f[:, gcol:gcol + 8],
                                                 in0=gbi[:, c:c + 8],
                                                 in1=gbi[:, c:c + 8],
                                                 op=ALU.mult)
                        sm.inc(v, "V")
                sec("vector", _)

                for h in range(2):
                    gA = g0 + 2 * h
                    slot0 = (gA + 2) % 4
                    k_prod(2, xnT_s, "t_xnT", 2 * h * 128, 256, slot0, par=s % 2)
                    k_prod(3, xnT_s, "t_xnT", 2 * h * 128, 256, slot0, par=s % 2)
                    v_prod(4, xnT_s, "t_xnT", 2 * h * 128, 256, slot0, par=s % 2)
                    v_prod(5, xnT_s, "t_xnT", 2 * h * 128, 256, slot0, par=s % 2)

                    for g in (gA, gA + 1):
                        tloc = g - g0
                        slots = [((kt + 2) % 4, kt - g) for kt in
                                 (g - 2, g - 1, g)]

                        def _(tensor, w, _tloc=tloc, _slots=slots):
                            w(st["br"][2])
                            w(st["t_qTs"])
                            for sl, _d in _slots:
                                w(st["t_kv"][sl])
                            t = None
                            for si, (sl, _d) in enumerate(_slots):
                                for p in range(PG):
                                    t = tensor.matmul(
                                        PS[2][:, si * 128: si * 128 + 128],
                                        kTr[:, p * 512 + sl * 128:
                                            p * 512 + sl * 128 + 128],
                                        qTs[:, p * 512 + _tloc * 128:
                                            p * 512 + _tloc * 128 + 128],
                                        start=(p == 0), stop=(p == PG - 1))
                            sm.inc(t, "T")
                        sec("tensor", _)
                        st["bw"][2] = sm.tok("T")
                        st["r_qTs"] = sm.tok("T")
                        for sl, _d in slots:
                            st["r_kv"][sl] = sm.tok("T")

                        def _(vector, w, _slots=slots):
                            w(st["bw"][2])
                            w(st["r_ST"])
                            w(ld_all)
                            v = None
                            for si, (sl, delta) in enumerate(_slots):
                                src = PS[2][:, si * 128: si * 128 + 128]
                                dst = STb[:, si * 128: si * 128 + 128]
                                if delta == -1:
                                    v = vector.tensor_scalar(
                                        out=dst, in0=src, scalar1=SCL,
                                        scalar2=None, op0=ALU.mult)
                                else:
                                    msk = mask0 if delta == 0 else mask2
                                    v = vector.scalar_tensor_tensor(
                                        out=dst, in0=src, scalar=SCL, in1=msk,
                                        op0=ALU.mult, op1=ALU.mult)
                            sm.inc(v, "V")
                        sec("vector", _)
                        st["br"][2] = sm.tok("V")
                        st["t_ST"] = sm.tok("V")

                        def _(tensor, w, _slots=slots):
                            w(st["t_ST"])
                            w(st["br"][4])
                            w(st["br"][5])
                            for sl, _d in _slots:
                                w(st["t_kv"][sl])
                            t = None
                            for dp in range(PG):
                                bank = 4 + dp // 4
                                for si, (sl, _d) in enumerate(_slots):
                                    t = tensor.matmul(
                                        PS[bank][:, (dp % 4) * 128:
                                                 (dp % 4) * 128 + 128],
                                        vrb[:, sl * 1024 + dp * 128:
                                            sl * 1024 + dp * 128 + 128],
                                        STb[:, si * 128: si * 128 + 128],
                                        start=(si == 0), stop=(si == 2))
                            sm.inc(t, "T")
                        sec("tensor", _)
                        st["bw"][4] = st["bw"][5] = sm.tok("T")
                        st["r_ST"] = sm.tok("T")
                        for sl, _d in slots:
                            st["r_kv"][sl] = sm.tok("T")

                        def _(vector, w):
                            w(st["bw"][4])
                            w(st["r_aT"])
                            w(st["t_aT"])
                            vector.tensor_copy(out=aTb[:, 0:512],
                                               in_=PS[4][:, 0:512])
                            v = vector.tensor_copy(out=aTb[:, 512:1024],
                                                   in_=PS[5][:, 0:512])
                            sm.inc(v, "V")
                        sec("vector", _)
                        st["br"][4] = st["br"][5] = sm.tok("V")
                        st["t_aT"] = sm.tok("V")

                        dsts = [atkC[:, p * 128:(p + 1) * 128]
                                for p in range(PG)]
                        tt, vt = emit_tile_transposes(
                            aTb, dsts, pre_t=[st["t_aT"]],
                            pre_v=[st["r_atkC"]])
                        st["r_aT"] = tt
                        st["t_atk"] = vt

                        lt = emit_ln(atkC[:, :], atkC[:, :], aTb[:, :],
                                     pre=[st["t_atk"], st["r_aT"],
                                          st["t_aT"]], so=4, akey="ln_a2")

                        def _(gpsimd, w, _tloc=tloc, _lt=lt, _ax=accx):
                            w(_lt)
                            w(ld_all)
                            v = gpsimd.tensor_tensor(out=aTb[:, :],
                                                     in0=atkC[:, :],
                                                     in1=lnwb[:, :],
                                                     op=ALU.mult)
                            w(sm.inc(v, "G"))
                            v = gpsimd.tensor_tensor(out=aTb[:, :],
                                                     in0=aTb[:, :],
                                                     in1=lnbb[:, :],
                                                     op=ALU.add)
                            w(sm.inc(v, "G"))
                            if "att" in dbg_skip:
                                v = gpsimd.tensor_scalar(
                                    out=aTb[:, :], in0=aTb[:, :], scalar1=0.0,
                                    scalar2=None, op0=ALU.mult)
                                w(sm.inc(v, "G"))
                            w(st["t_accld"])
                            w(st["acc_wv"])
                            v = gpsimd.tensor_tensor(
                                out=_ax[:, _tloc * 1024:(_tloc + 1) * 1024],
                                in0=_ax[:, _tloc * 1024:(_tloc + 1) * 1024],
                                in1=aTb[:, :], op=ALU.add)
                            sm.inc(v, "G")
                        sec("gpsimd", _)
                        st["r_atkC"] = sm.tok("G")
                        st["acc_wv"] = sm.tok("G")
                        st["t_aT"] = sm.tok("G")
                        st["r_aT"] = sm.tok("G")

                if s + 1 < NSLC:
                    emit_front(s + 1)

                def _(scalar, w, _s=s, _ax=accx):
                    w(st["acc_wv"])
                    sm.inc(scalar.dma_start(
                        out=out_d[_s * 512:(_s + 1) * 512, :]
                        .rearrange("(t p) f -> p t f", p=128),
                        in_=_ax[:, 0:4096].rearrange("p (t f) -> p t f", f=D)),
                        "ST" if _s % 2 == 0 else "SU", 16)
                sec("scalar", _)

            bar = [sm.tok("T"), sm.tok("V"), sm.tok("A"), sm.tok("G"),
                   ("ST", 32), ("SU", 32), sm.tok("LS"), sm.tok("LT"),
                   sm.tok("LQ"), sm.tok("LR"), sm.tok("LX"),
                   sm.tok("LY")]

            for t in range(16):
                def _(tensor, w, _t=t):
                    for tk in bar:
                        w(tk)
                    w(st["br"][2])
                    w(base_tok)
                    tensor.matmul(PS[2][:, 0:8], mask0,
                                  gmf[:, _t * 8: _t * 8 + 8],
                                  start=True, stop=False)
                    tensor.matmul(PS[2][:, 0:8], ones1, base[0:1, :],
                                  start=False, stop=True)
                    t2 = tensor.matmul(PS[2][0:1, 8:16],
                                       cst[:, C_ONE:C_ONE + 1],
                                       gmf[:, _t * 8: _t * 8 + 8],
                                       start=True, stop=True)
                    sm.inc(t2, "T")
                sec("tensor", _)
                st["bw"][2] = sm.tok("T")

                def _(vector, w, _t=t):
                    w(st["bw"][2])
                    vector.tensor_copy(out=cmf[:, _t * 8: _t * 8 + 8],
                                       in_=PS[2][:, 0:8])
                    v = vector.tensor_tensor(out=base[:, :], in0=base[:, :],
                                             in1=PS[2][0:1, 8:16],
                                             op=ALU.add)
                    sm.inc(v, "V")
                sec("vector", _)
                st["br"][2] = sm.tok("V")
                base_tok = sm.tok("V")

            def _(vector, w):
                w(base_tok)
                v = vector.tensor_tensor(out=cmf[:, :], in0=cmf[:, :],
                                         in1=gmf[:, :], op=ALU.mult)
                sm.inc(v, "V")
            sec("vector", _)
            cmsk_tok = sm.tok("V")

            r_E = [None, None]
            r_cg = None
            t_rowv = None
            r_row = None
            for e in range(NE):
                def _(vector, w, _e=e, _rcg=r_cg):
                    w(cmsk_tok)
                    w(_rcg)
                    cgv = cg[:, 0:32].rearrange("p (t c) -> p t c", c=2)
                    vector.tensor_copy(
                        out=cgv[:, :, 0:1],
                        in_=cst[:, C_TOK:C_TOK + 16]
                        .rearrange("p (t c) -> p t c", c=1))
                    v = vector.tensor_copy(
                        out=cgv[:, :, 1:2],
                        in_=g2f[:, 0:128].rearrange("p (t c) -> p t c",
                                                    c=8)[:, :, _e:_e + 1])
                    sm.inc(v, "V")
                sec("vector", _)
                cg_rdy = sm.tok("V")

                for t in range(16):
                    ebuf = t % 2

                    def _(vector, w, _e=e, _t=t, _eb=ebuf):
                        w(cmsk_tok)
                        w(r_E[_eb])
                        v = vector.tensor_scalar(
                            out=Epr[:, _eb * 768:_eb * 768 + 768],
                            in0=cst[:, C_IOTA:C_IOTA + 768],
                            scalar1=cmf[:, _t * 8 + _e: _t * 8 + _e + 1],
                            scalar2=None, op0=ALU.is_equal)
                        sm.inc(v, "V")
                    sec("vector", _)
                    e_tok = sm.tok("V")

                    def _(tensor, w, _t=t, _eb=ebuf, _et=e_tok,
                          _cg=cg_rdy):
                        w(_et)
                        w(_cg)
                        if _t == 0:
                            w(st["br"][2])
                            w(st["br"][3])
                        tensor.matmul(
                            PS[2][0:2, 0:512], cg[:, 2 * _t: 2 * _t + 2],
                            Epr[:, _eb * 768: _eb * 768 + 512],
                            start=(_t == 0), stop=(_t == 15))
                        t2 = tensor.matmul(
                            PS[3][0:2, 0:256], cg[:, 2 * _t: 2 * _t + 2],
                            Epr[:, _eb * 768 + 512: _eb * 768 + 768],
                            start=(_t == 0), stop=(_t == 15))
                        sm.inc(t2, "T")
                    sec("tensor", _)
                    r_E[ebuf] = sm.tok("T")
                r_cg = sm.tok("T")
                st["bw"][2] = st["bw"][3] = sm.tok("T")

                def _(vector, w, _rr=r_row):
                    w(st["bw"][2])
                    w(_rr)
                    vector.tensor_copy(out=idxrow[0:2, 0:512],
                                       in_=PS[2][0:2, 0:512])
                    v = vector.tensor_copy(out=idxrow[0:2, 512:768],
                                           in_=PS[3][0:2, 0:256])
                    sm.inc(v, "V")
                sec("vector", _)
                st["br"][2] = st["br"][3] = sm.tok("V")
                t_rowv = sm.tok("V")

                def _(tensor, w, _tv=t_rowv):
                    w(_tv)
                    w(st["br"][4])
                    t2 = None
                    for ch in range(NCH):
                        t2 = tensor.transpose(
                            PS[4][:, ch * 2: ch * 2 + 2],
                            idxrow[0:2, ch * 128: ch * 128 + 128],
                            ident[0:2, 0:2])
                    sm.inc(t2, "T")
                sec("tensor", _)
                st["bw"][4] = sm.tok("T")
                r_row = sm.tok("T")

                def _(vector, w, _e=e):
                    w(st["bw"][4])
                    p4 = PS[4][:, 0:12].rearrange("p (g f) -> p g f", f=2)
                    vector.tensor_copy(
                        out=idxc[:, _e * NCH:(_e + 1) * NCH]
                        .rearrange("p (g f) -> p g f", f=1),
                        in_=p4[:, :, 0:1])
                    v = vector.tensor_copy(
                        out=g2gc[:, _e * NCH:(_e + 1) * NCH]
                        .rearrange("p (g f) -> p g f", f=1),
                        in_=p4[:, :, 1:2])
                    sm.inc(v, "V")
                sec("vector", _)
                st["br"][4] = sm.tok("V")
            idx_rdy = sm.tok("V")

            r_xgs = None
            t_xgT = None
            r_xgT = None
            t_yg = [None] * NCH
            r_yg = None
            lg_cnt = 0
            sc_cnt = 0

            def emit_gather(_e):
                nonlocal lg_cnt
                def _(gpsimd, w, _rx=r_xgs):
                    for tk in bar:
                        w(tk)
                    w(idx_rdy)
                    w(_rx)
                    for ti in range(NCH):
                        sm.inc(gpsimd.indirect_dma_start(
                            out=xgs[:, ti * 1024: ti * 1024 + 1024],
                            out_offset=None,
                            in_=xns_d[:, :],
                            in_offset=IndirectOffsetOnAxis(
                                ap=idxc[:, _e * NCH + ti: _e * NCH + ti + 1],
                                axis=0)), "LG", 16)
                sec("gpsimd", _)
                lg_cnt += 16 * NCH
                return ("LG", lg_cnt)

            def emit_scatter(_e, _tyg, _sc):
                def _(gpsimd, w):
                    for tk in _tyg:
                        w(tk)
                    for ti in range(NCH):
                        gpsimd.wait_ge(sm.sem["SC"], _sc + 16 * ti)
                        sm.inc(gpsimd.indirect_dma_start(
                            out=out_d[:, :],
                            out_offset=IndirectOffsetOnAxis(
                                ap=idxc[:, _e * NCH + ti: _e * NCH + ti + 1],
                                axis=0),
                            in_=yg[:, ti * 1024: ti * 1024 + 1024],
                            in_offset=None,
                            compute_op=ALU.add), "SC", 16)
                sec("gpsimd", _)

            def emit_xg_transposes(_tx):
                nonlocal r_xgs, r_xgT
                for p in range(PG):
                    for half in range(2):
                        n_ti = 4 if half == 0 else 2
                        bank = (p * 2 + half) % 4

                        def _(tensor, w, _p=p, _h=half, _n=n_ti, _bank=bank,
                              _tx2=_tx):
                            w(_tx2)
                            w(st["br"][_bank])
                            w(r_xgT)
                            t = None
                            for i in range(_n):
                                ti = _h * 4 + i
                                t = tensor.transpose(
                                    PS[_bank][:, i * 128: i * 128 + 128],
                                    xgs[:, ti * 1024 + _p * 128:
                                        ti * 1024 + _p * 128 + 128],
                                    ident)
                            sm.inc(t, "T")
                        sec("tensor", _)
                        st["bw"][bank] = sm.tok("T")
                        r_xgs = sm.tok("T")

                        def _(scalar, w, _p=p, _h=half, _n=n_ti, _bank=bank):
                            w(st["bw"][_bank])
                            off = _p * CAP + _h * 512
                            a = scalar.activation(
                                out=xgTh[:, off:off + _n * 128],
                                in_=PS[_bank][:, 0:_n * 128],
                                func=AF.Identity)
                            sm.inc(a, "A")
                        sec("scalar", _)
                        a_hi2 = sm.tok("A")

                        def _(vector, w, _p=p, _h=half, _n=n_ti, _bank=bank,
                              _ah=a_hi2):
                            w(st["bw"][_bank])
                            w(_ah)
                            off = _p * CAP + _h * 512
                            v = vector.scalar_tensor_tensor(
                                out=xgTl[:, off:off + _n * 128],
                                in0=PS[_bank][:, 0:_n * 128],
                                scalar=1.0, in1=xgTh[:, off:off + _n * 128],
                                op0=ALU.mult, op1=ALU.subtract)
                            sm.inc(v, "V")
                        sec("vector", _)
                        st["br"][bank] = sm.tok("V")
                return (sm.tok("A"), sm.tok("V"))

            t_xgs = emit_gather(0)
            xg_done = emit_xg_transposes(t_xgs)
            def emit_wstage(pg):
                _e, _ph = pg // 8, pg % 8
                _up = _ph < 4
                _q = _ph % 4
                _wi = pg % 2

                def _(sync, w, _e=_e, _q=_q, _up=_up, _wi=_wi):
                    for tk in bar:
                        w(tk)
                    w(st["r_stage"][_wi])
                    ln = "L0" if _wi == 0 else "L1"
                    if _up:
                        src = wut_d[_e, :, _q * 512:(_q + 1) * 512]
                    else:
                        src = wdt_d[_e, :, _q * 256:(_q + 1) * 256]
                    sm.inc(sync.dma_start(
                        out=wst[_wi][:, 0:4096].rearrange(
                            "p (c f) -> p c f", f=512 if _up else 256),
                        in_=src.rearrange("(g p) f -> p g f", p=128)),
                        ln, 16)
                sec("sync", _)
                t_dma = sm.tok("L0" if _wi == 0 else "L1")

                def _(scalar, w, _wi=_wi, _td=t_dma, _rw=st["r_wh"][_wi]):
                    w(_td)
                    w(_rw)
                    a = scalar.activation(
                        out=wBh[_wi][:, :], in_=wst[_wi][:, :],
                        func=AF.Identity)
                    sm.inc(a, "A")
                sec("scalar", _)
                t_whi = sm.tok("A")

                def _(gpsimd, w, _wi=_wi, _td=t_dma, _th=t_whi,
                      _rw=st["r_wh"][_wi]):
                    w(_td)
                    w(_th)
                    w(_rw)
                    g = gpsimd.tensor_tensor(
                        out=wBl[_wi][:, :], in0=wst[_wi][:, :],
                        in1=wBh[_wi][:, :], op=ALU.subtract)
                    sm.inc(g, "G")
                sec("gpsimd", _)
                st["t_wA"][_wi] = t_whi
                st["t_wlo"][_wi] = sm.tok("G")
                st["r_stage"][_wi] = sm.tok("G")

            emit_wstage(0)
            pend_scatter = None
            for e in range(NE):

                for ph in range(8):
                    up = ph < 4
                    q = ph % 4
                    wi = (e * 8 + ph) % 2

                    if ph == 1 and pend_scatter is not None:
                        emit_scatter(*pend_scatter)
                        pend_scatter = None
                    if ph == 5 and e + 1 < NE:
                        xg_done_next = emit_xg_transposes(emit_gather(e + 1))
                    if e * 8 + ph + 1 < NE * 8:
                        emit_wstage(e * 8 + ph + 1)

                    if up:
                        for m in range(4):
                            hp = q * 4 + m
                            for c in range(2):
                                ntk = 512 if c == 0 else 256
                                bank = 4 + (m * 2 + c) % 4

                                def _(tensor, w, _m=m, _c=c, _n=ntk,
                                      _bank=bank, _wi=wi):
                                    w(st["t_wA"][_wi])
                                    w(xg_done[0])
                                    w(xg_done[1])
                                    w(st["br"][_bank])
                                    wh = wBh[_wi][:, :]
                                    xh = xgTh[:, :]
                                    xl = xgTl[:, :]
                                    t = None
                                    first = True
                                    for p in range(PG):
                                        wcol = p * 512 + _m * 128
                                        xcol = p * CAP + _c * 512
                                        for rh in (xh, xl):
                                            t = tensor.matmul(
                                                PS[_bank][:, 0:_n],
                                                wh[:, wcol:wcol + 128],
                                                rh[:, xcol:xcol + _n],
                                                start=first, stop=False)
                                            first = False
                                    sm.inc(t, "T")
                                sec("tensor", _)

                                def _(tensor, w, _m=m, _c=c, _n=ntk,
                                      _bank=bank, _wi=wi):
                                    w(st["t_wlo"][_wi])
                                    wl = wBl[_wi][:, :]
                                    xh = xgTh[:, :]
                                    t = None
                                    for p in range(PG):
                                        wcol = p * 512 + _m * 128
                                        xcol = p * CAP + _c * 512
                                        t = tensor.matmul(
                                            PS[_bank][:, 0:_n],
                                            wl[:, wcol:wcol + 128],
                                            xh[:, xcol:xcol + _n],
                                            start=False, stop=(p == PG - 1))
                                    sm.inc(t, "T")
                                sec("tensor", _)
                                st["bw"][bank] = sm.tok("T")
                                st["r_wA"][wi] = sm.tok("T")
                                st["r_wh"][wi] = sm.tok("T")
                                r_xgT = sm.tok("T")

                                def _(vector, w, _hp=hp, _c=c, _n=ntk,
                                      _bank=bank):
                                    w(st["bw"][_bank])
                                    w(st["r_Hb"])
                                    off = _hp * CAP + _c * 512
                                    v = vector.tensor_scalar(
                                        out=rsq[:, 0:_n],
                                        in0=PS[_bank][:, 0:_n],
                                        scalar1=0.0, scalar2=None,
                                        op0=ALU.max)
                                    w(sm.inc(v, "V"))
                                    v = vector.tensor_tensor(
                                        out=PS[_bank][:, 0:_n],
                                        in0=rsq[:, 0:_n],
                                        in1=rsq[:, 0:_n],
                                        op=ALU.mult)
                                    w(sm.inc(v, "V"))
                                    v = vector.tensor_scalar(
                                        out=Hh[:, off:off + _n],
                                        in0=PS[_bank][:, 0:_n],
                                        scalar1=1.0, scalar2=None,
                                        op0=ALU.mult)
                                    w(sm.inc(v, "V"))
                                    v = vector.scalar_tensor_tensor(
                                        out=Hl[:, off:off + _n],
                                        in0=PS[_bank][:, 0:_n],
                                        scalar=1.0, in1=Hh[:, off:off + _n],
                                        op0=ALU.mult, op1=ALU.subtract)
                                    sm.inc(v, "V")
                                sec("vector", _)
                                st["br"][bank] = sm.tok("V")
                        st["t_Hb"] = sm.tok("V")
                    else:
                        for ti in range(NCH):
                            bank = 4 + ti % 4

                            def _(tensor, w, _ti=ti, _bank=bank, _wi=wi):
                                w(st["t_wA"][_wi])
                                w(st["t_Hb"])
                                w(st["br"][_bank])
                                hl = Hl[:, :]
                                wh = wBh[_wi][:, :]
                                t = None
                                first = True
                                for hp in range(16):
                                    hcol = hp * CAP + _ti * 128
                                    wcol = hp * 256
                                    t = tensor.matmul(
                                        PS[_bank][:, 0:256],
                                        hl[:, hcol:hcol + 128],
                                        wh[:, wcol:wcol + 256],
                                        start=first, stop=False)
                                    first = False
                                sm.inc(t, "T")
                            sec("tensor", _)

                            def _(tensor, w, _ti=ti, _bank=bank, _wi=wi):
                                w(st["t_wlo"][_wi])
                                hh = Hh[:, :]
                                wh = wBh[_wi][:, :]
                                wl = wBl[_wi][:, :]
                                t = None
                                for hp in range(16):
                                    hcol = hp * CAP + _ti * 128
                                    wcol = hp * 256
                                    for rh in (wh, wl):
                                        t = tensor.matmul(
                                            PS[_bank][:, 0:256],
                                            hh[:, hcol:hcol + 128],
                                            rh[:, wcol:wcol + 256],
                                            start=False,
                                            stop=(hp == 15 and rh is wl))
                                sm.inc(t, "T")
                            sec("tensor", _)
                            st["bw"][bank] = sm.tok("T")
                            st["r_wA"][wi] = sm.tok("T")
                            st["r_wh"][wi] = sm.tok("T")
                            st["r_Hb"] = sm.tok("T")

                            def _(vector, w, _ti=ti, _q=q, _e=e, _bank=bank):
                                w(st["bw"][_bank])
                                w(idx_rdy)
                                w(r_yg)
                                sc1 = (0.0 if "moe" in dbg_skip else
                                       g2gc[:, _e * NCH + _ti:
                                            _e * NCH + _ti + 1])
                                v = vector.tensor_scalar(
                                    out=yg[:, _ti * 1024 + _q * 256:
                                           _ti * 1024 + _q * 256 + 256],
                                    in0=PS[_bank][:, 0:256],
                                    scalar1=sc1, scalar2=None, op0=ALU.mult)
                                sm.inc(v, "V")
                            sec("vector", _)
                            st["br"][bank] = sm.tok("V")
                            t_yg[ti] = sm.tok("V")

                pend_scatter = (e, list(t_yg), sc_cnt)
                sc_cnt += 16 * NCH
                r_yg = ("SC", sc_cnt)
                if e + 1 < NE:
                    xg_done = xg_done_next

            if pend_scatter is not None:
                emit_scatter(*pend_scatter)

            def _(gpsimd, w):
                gpsimd.wait_ge(sm.sem["ST"], 32)
                gpsimd.wait_ge(sm.sem["SU"], 32)
                gpsimd.wait_ge(sm.sem["SC"], sc_cnt)
            sec("gpsimd", _)

    return nc


def kernel(**inputs):
    x = np.ascontiguousarray(np.asarray(inputs["L0_attn_x"], dtype=np.float32))
    wqkv = np.ascontiguousarray(np.asarray(inputs["L0_attn_W_qkv"],
                                           dtype=np.float32))
    bqkv = np.asarray(inputs["L0_attn_b_qkv"], dtype=np.float32)
    lnw = np.asarray(inputs["L0_attn_ln_w"], dtype=np.float32)
    lnb = np.asarray(inputs["L0_attn_ln_b"], dtype=np.float32)
    gatet = np.ascontiguousarray(
        np.asarray(inputs["L0_ffn_gate_w"], dtype=np.float32).T)
    wut = np.ascontiguousarray(
        np.asarray(inputs["L0_ffn_w_up"], dtype=np.float32).transpose(0, 2, 1))
    wdt = np.ascontiguousarray(
        np.asarray(inputs["L0_ffn_w_down"],
                   dtype=np.float32).transpose(0, 2, 1))
    ebias = np.asarray(inputs["L0_ffn_expert_bias"], dtype=np.float32)

    B, T, C = x.shape
    assert (B, T, C) == (2, 8192, 1024)

    bqkvt = np.ascontiguousarray(bqkv.reshape(24, 128).T)
    bqkvv = np.ascontiguousarray(bqkv[2048:].reshape(1, 1024))
    lnwb = np.ascontiguousarray(np.broadcast_to(lnw[None, :], (128, 1024)))
    lnbb = np.ascontiguousarray(np.broadcast_to(lnb[None, :], (128, 1024)))
    eb1 = np.ascontiguousarray(ebias.reshape(1, 8))

    kk = np.arange(128)[:, None]
    qq = np.arange(128)[None, :]
    cstm = np.zeros((128, C_TOT), dtype=np.float32)
    cstm[:, C_ID:C_ID + 128] = np.eye(128, dtype=np.float32)
    cstm[:, C_M0:C_M0 + 128] = (kk <= qq).astype(np.float32)
    cstm[:, C_M2:C_M2 + 128] = (kk > qq).astype(np.float32)
    cstm[:, C_ONE:C_ONE + 128] = 1.0
    cstm[:, C_IOTA:C_IOTA + CAP] = (np.arange(CAP) + 1)[None, :]
    for t in range(16):
        cstm[:, C_TOK + t] = np.arange(128) + 128 * t

    in_maps = []
    for c in range(8):
        b = c // 4
        s0 = (c % 4) * 2048
        if c % 4 == 0:
            xc = np.concatenate(
                [np.zeros((HALO, C), dtype=np.float32), x[b, 0:T_OWN]], axis=0)
            hvv = np.zeros((128, 1), dtype=np.float32)
        else:
            xc = x[b, s0 - HALO: s0 + T_OWN]
            hvv = np.ones((128, 1), dtype=np.float32)
        in_maps.append({
            "x": np.ascontiguousarray(xc),
            "wqkv": wqkv, "bqkvt": bqkvt, "bqkvv": bqkvv,
            "gatet": gatet, "eb1": eb1, "wut": wut, "wdt": wdt,
            "lnw": lnwb, "lnb": lnbb, "cst": cstm, "hv": hvv,
        })

    nc = build_program()
    res = run_bass_kernel_spmd(nc, in_maps, core_ids=list(range(8)))

    out = np.empty((2, 8192, 1024), dtype=np.float32)
    for c in range(8):
        b = c // 4
        s0 = (c % 4) * 2048
        out[b, s0:s0 + T_OWN] = res.results[c]["out"]
    return out


# version 31
# speedup vs baseline: 1.0487x; 1.0487x over previous
import contextlib
import numpy as np
import concourse.bass as bass
import concourse.mybir as mybir
from concourse.bass import IndirectOffsetOnAxis
from concourse.bass_utils import run_bass_kernel_spmd

F32 = mybir.dt.float32
BF16 = mybir.dt.bfloat16
I32 = mybir.dt.int32
ALU = mybir.AluOpType
AF = mybir.ActivationFunctionType
AX = mybir.AxisListType

D = 1024
PG = 8
HID = 2048
T_OWN = 2048
HALO = 256
NSLC = 4
NE = 8
EPS = 1e-5
SCL = 1.0 / 512.0
CAP = 768
NCH = CAP // 128

C_ID = 0
C_M0 = 128
C_M2 = 256
C_ONE = 384
C_IOTA = 512
C_TOK = 1280
C_TOT = 1296

_ENG2NAME = {"tensor": "T", "vector": "V", "scalar": "A",
             "gpsimd": "G", "sync": "SY"}


class Sems:
    def __init__(self):
        self.cnt = {}
        self.sem = {}
        self.lastw = {}

    def add(self, name, handle):
        self.sem[name] = handle
        self.cnt[name] = 0

    def inc(self, instr, name, k=1):
        instr.then_inc(self.sem[name], k)
        self.cnt[name] += k
        return (name, self.cnt[name])

    def tok(self, name):
        return (name, self.cnt[name])


def build_program(dbg_skip=()):
    nc = bass.Bass("TRN2", target_bir_lowering=False)

    x_d = nc.dram_tensor("x", [HALO + T_OWN, D], F32, kind="ExternalInput").ap()
    wqkv_d = nc.dram_tensor("wqkv", [D, 3 * D], F32, kind="ExternalInput").ap()
    bqkvt_d = nc.dram_tensor("bqkvt", [128, 24], F32, kind="ExternalInput").ap()
    bqkvv_d = nc.dram_tensor("bqkvv", [1, D], F32, kind="ExternalInput").ap()
    gatet_d = nc.dram_tensor("gatet", [D, NE], F32, kind="ExternalInput").ap()
    eb1_d = nc.dram_tensor("eb1", [1, NE], F32, kind="ExternalInput").ap()
    wut_d = nc.dram_tensor("wut", [NE, D, HID], F32, kind="ExternalInput").ap()
    wdt_d = nc.dram_tensor("wdt", [NE, HID, D], F32, kind="ExternalInput").ap()
    lnw_d = nc.dram_tensor("lnw", [128, D], F32, kind="ExternalInput").ap()
    lnb_d = nc.dram_tensor("lnb", [128, D], F32, kind="ExternalInput").ap()
    cst_d = nc.dram_tensor("cst", [128, C_TOT], F32, kind="ExternalInput").ap()
    hv_d = nc.dram_tensor("hv", [128, 1], F32, kind="ExternalInput").ap()
    out_d = nc.dram_tensor("out", [T_OWN, D], F32, kind="ExternalOutput").ap()
    xns_d = nc.dram_tensor("xnscr", [T_OWN, D], F32, kind="Internal").ap()

    sm = Sems()

    with contextlib.ExitStack() as ex:
        ra_h = ex.enter_context(nc.sbuf_tensor("rarena", [128, 20480], F32))

        def rseg(off, n):
            return ra_h[:, off:off + n]

        acc = rseg(0, 4096)
        acc2 = rseg(4096, 4096)
        atk = rseg(8192, 1024)
        atkB = rseg(9216, 1024)
        wQ = rseg(10240, 4096)
        wQ2 = rseg(14336, 4096)
        xgs = rseg(0, 6144)
        yg = rseg(6144, 6144)
        wst = [rseg(12288, 4096), rseg(16384, 4096)]

        VARENA_B = 106496
        back = ex.enter_context(
            nc.sbuf_tensor("varena", [128, VARENA_B], mybir.dt.uint8))
        base_addr = nc.lookup_mloc(back).addr

        def valloc(name, cols, off, dt=F32, rows=128):
            h = nc.virtual_sbuf_tensor(name, [rows, cols], dt)
            nc._tensor(h.name, h.shape, h.dtype, type="SB")
            mloc = nc.lookup_mloc(h)
            mloc.allocated = True
            mloc.addr = base_addr + off
            return h

        xnT = valloc("xnT", 4096, 0)
        qTs = valloc("qTs", 4096, 16384)
        kTr = valloc("kTr", 4096, 32768)
        vrb = valloc("vrb", 4096, 49152)
        STb = valloc("STb", 384, 65536)
        aTb = valloc("aTb", 1024, 67072)
        lnscr = valloc("lnscr", 1024, 71168)
        atkC = valloc("atkC", 1024, 75264)
        xnT2 = valloc("xnT2", 4096, 79360)
        Epr = valloc("Epr", 1536, 71168)
        idxrow = valloc("idxrow", 768, 95744)
        wBh = [valloc("wBh0", 4096, 0, dt=BF16),
               valloc("wBh1", 4096, 16384, dt=BF16)]
        wBl = [valloc("wBl0", 4096, 8192, dt=BF16),
               valloc("wBl1", 4096, 24576, dt=BF16)]
        xgTh = valloc("xgTh", 6144, 32768, dt=BF16)
        xgTl = valloc("xgTl", 6144, 45056, dt=BF16)
        Hh = valloc("Hh", 12288, 57344, dt=BF16)
        Hl = valloc("Hl", 12288, 81920, dt=BF16)

        def sbt(name, cols, rows=128, dt=F32):
            return ex.enter_context(nc.sbuf_tensor(name, [rows, cols], dt))

        cst = sbt("cst_s", C_TOT)
        rsq = sbt("rsq", 512)
        lnwb = sbt("lnwb", 1024)
        lnbb = sbt("lnbb", 1024)
        bqt = sbt("bqt", 24)
        gwT = sbt("gwT", 64)
        ebb = sbt("ebb", 8)
        gsc = sbt("gsc", 32)
        gbi = sbt("gbi", 32)
        gt8 = sbt("gt8", 32)
        g2f = sbt("g2f", 128)
        gmf = sbt("gmf", 128)
        cmf = sbt("cmf", 128)
        stats = sbt("stats", 32)
        hv = sbt("hv_s", 1)
        cg = sbt("cgst", 32)
        idxc = sbt("idxc", NE * NCH, dt=I32)
        g2gc = sbt("g2gc", NE * NCH)
        bqv = ex.enter_context(nc.sbuf_tensor("bqv", [1, D], F32))
        eb1 = ex.enter_context(nc.sbuf_tensor("eb1s", [1, NE], F32))
        base = ex.enter_context(nc.sbuf_tensor("base8", [1, NE], F32))

        PS = [ex.enter_context(nc.psum_tensor(f"ps{i}", [128, 512], F32))
              for i in range(8)]
        for name in ("T", "V", "A", "G", "LC", "LQ", "LR", "L0", "L1", "LX",
                     "LY", "LS", "LT", "LG", "SC", "ST", "SU"):
            sm.add(name, ex.enter_context(nc.semaphore(name)))

        ident = cst[:, C_ID:C_ID + 128]
        mask0 = cst[:, C_M0:C_M0 + 128]
        mask2 = cst[:, C_M2:C_M2 + 128]
        ones1 = cst[0:1, C_ONE:C_ONE + 128]

        st = {
            "bw": [None] * 8, "br": [None] * 8,
            "t_xnT": None, "r_xnT": None,
            "t_qTs": None, "r_qTs": None,
            "t_kv": [None] * 4, "r_kv": [None] * 4,
            "r_wQ": [None, None],
            "t_ST": None, "r_ST": None,
            "t_aT": None, "r_aT": None,
            "t_atk": None, "r_atk": None, "r_atkB": None, "r_atkC": None,
            "t_accld": None, "acc_wv": None,
            "t_wA": [None, None], "r_wA": [None, None],
            "r_wh": [None, None], "r_stage": [None, None],
            "t_wB": None, "r_wB": None, "t_wlo": [None, None],
            "t_Hb": None, "r_Hb": None,
            "ln_a": None, "ln_a2": None,
        }

        with nc.Block() as block:
            def sec(kind, fn):
                wname = _ENG2NAME[kind]

                def wrapped(eng):
                    def w(tok):
                        if tok is None:
                            return
                        name, val = tok
                        if val <= 0:
                            return
                        key = (wname, name)
                        if sm.lastw.get(key, 0) >= val:
                            return
                        eng.wait_ge(sm.sem[name], val)
                        sm.lastw[key] = val
                    fn(eng, w)
                getattr(block, kind)(wrapped)

            def _(sync, w):
                for dst, src in [
                    (cst[:], cst_d[:]), (lnwb[:], lnw_d[:]), (lnbb[:], lnb_d[:]),
                    (bqt[:], bqkvt_d[:]), (bqv[:], bqkvv_d[:]),
                    (eb1[:], eb1_d[:]), (hv[:], hv_d[:]),
                    (gwT[:, 0:64].rearrange("p (g f) -> p g f", f=NE),
                     gatet_d[:, :].rearrange("(g p) f -> p g f", p=128)),
                ]:
                    sm.inc(sync.dma_start(out=dst, in_=src), "LC", 16)
                sm.inc(sync.dma_start(
                    out=acc[:, 0:2048].rearrange("p (t f) -> p t f", f=D),
                    in_=x_d[0:HALO, :].rearrange("(t p) f -> p t f", p=128)),
                    "LC", 16)
            sec("sync", _)
            ld_all = sm.tok("LC")

            def _(vector, w):
                v = vector.memset(base[:, :], 0.0)
                sm.inc(v, "V")
            sec("vector", _)
            base_tok = sm.tok("V")

            def _(tensor, w):
                w(ld_all)
                t = tensor.matmul(PS[1][:, 0:8], ones1, eb1[0:1, :],
                                  start=True, stop=True)
                sm.inc(t, "T")
            sec("tensor", _)
            st["bw"][1] = sm.tok("T")

            def _(vector, w):
                w(st["bw"][1])
                v = vector.tensor_copy(out=ebb[:], in_=PS[1][:, 0:8])
                sm.inc(v, "V")
            sec("vector", _)
            st["br"][1] = sm.tok("V")
            gwT_rdy = ld_all

            def emit_ln(src_ap, dst_ap, scr_ap, pre, so=0, akey="ln_a"):
                b6 = 8 + (so // 4) * 12
                def _(vector, w):
                    for tk in pre:
                        w(tk)
                    w(st[akey])
                    v = vector.bn_stats(out=stats[:, b6:b6 + 6],
                                        in_=src_ap[:, 0:512])
                    w(sm.inc(v, "V"))
                    v = vector.bn_stats(out=stats[:, b6 + 6:b6 + 12],
                                        in_=src_ap[:, 512:1024])
                    w(sm.inc(v, "V"))
                    v = vector.bn_aggr(out=stats[:, so:so + 2],
                                       in_=stats[:, b6:b6 + 12])
                    w(sm.inc(v, "V"))
                    v = vector.tensor_scalar(out=stats[:, so + 1:so + 2],
                                             in0=stats[:, so + 1:so + 2],
                                             scalar1=EPS, scalar2=None,
                                             op0=ALU.add)
                    sm.inc(v, "V")
                sec("vector", _)
                vtok = sm.tok("V")

                def _(scalar, w):
                    w(vtok)
                    a = scalar.activation(out=stats[:, so + 2:so + 3],
                                          in_=stats[:, so + 1:so + 2],
                                          func=AF.Sqrt)
                    sm.inc(a, "A")
                sec("scalar", _)
                st[akey] = sm.tok("A")
                a_sqrt = sm.tok("A")

                def _(vector, w):
                    for tk in pre:
                        w(tk)
                    w(a_sqrt)
                    v = vector.reciprocal(out=stats[:, so + 3:so + 4],
                                          in_=stats[:, so + 2:so + 3])
                    w(sm.inc(v, "V"))
                    v = vector.tensor_scalar(out=dst_ap, in0=src_ap,
                                             scalar1=stats[:, so:so + 1],
                                             scalar2=stats[:, so + 3:so + 4],
                                             op0=ALU.subtract, op1=ALU.mult)
                    sm.inc(v, "V")
                sec("vector", _)
                return sm.tok("V")

            def emit_tile_transposes(src_buf, dst_positions, pre_t, pre_v):
                def _(tensor, w):
                    for tk in pre_t:
                        w(tk)
                    w(st["br"][0])
                    w(st["br"][1])
                    t = None
                    for p in range(PG):
                        bank = PS[p // 4]
                        t = tensor.transpose(
                            bank[:, (p % 4) * 128:(p % 4) * 128 + 128],
                            src_buf[:, p * 128:(p + 1) * 128], ident)
                    sm.inc(t, "T")
                sec("tensor", _)
                ttok = sm.tok("T")
                st["bw"][0] = st["bw"][1] = ttok

                def _(vector, w):
                    w(ttok)
                    for tk in pre_v:
                        w(tk)
                    v = None
                    for p in range(PG):
                        bank = PS[p // 4]
                        v = vector.tensor_copy(
                            out=dst_positions[p],
                            in_=bank[:, (p % 4) * 128:(p % 4) * 128 + 128])
                    sm.inc(v, "V")
                sec("vector", _)
                vtok = sm.tok("V")
                st["br"][0] = st["br"][1] = vtok
                return ttok, vtok

            wq_state = {"p": 0}

            def wq_load(j):
                par = wq_state["p"]
                wq_state["p"] ^= 1
                buf = wQ if par == 0 else wQ2
                sname = "LQ" if par == 0 else "LR"

                def _(sync, w, _par=par, _buf=buf, _sn=sname, _j=j):
                    w(st["r_wQ"][_par])
                    sm.inc(sync.dma_start(
                        out=_buf[:, 0:4096].rearrange("p (g f) -> p g f",
                                                      f=512),
                        in_=wqkv_d[:, _j * 512:(_j + 1) * 512]
                        .rearrange("(g p) f -> p g f", p=128)), _sn, 16)
                sec("sync", _)
                return sm.tok(sname), buf, par

            def k_prod(j, src_buf, src_tok_key, tok_off, n_tok, slot0,
                       par=None):
                t_wq, wqb, wqp = wq_load(j)
                nsl = n_tok // 128
                for cb in range(4):
                    page = (j - 2) * 4 + cb
                    bank = 6 + cb % 2

                    def _(tensor, w, _cb=cb, _bank=bank, _twq=t_wq, _wq=wqb):
                        w(_twq)
                        w(st["br"][_bank])
                        w(st[src_tok_key])
                        t = None
                        for p in range(PG):
                            t = tensor.matmul(
                                PS[_bank][:, 0:n_tok],
                                _wq[:, p * 512 + _cb * 128:
                                    p * 512 + _cb * 128 + 128],
                                src_buf[:, p * 512 + tok_off:
                                        p * 512 + tok_off + n_tok],
                                start=(p == 0), stop=(p == PG - 1))
                        sm.inc(t, "T")
                    sec("tensor", _)
                    st["bw"][bank] = sm.tok("T")
                    st["r_wQ"][wqp] = sm.tok("T")
                    if par is not None:
                        st["r_xnTp"][par] = sm.tok("T")
                    else:
                        st["r_qTs"] = sm.tok("T")

                    def _(vector, w, _page=page, _bank=bank):
                        w(st["bw"][_bank])
                        for sl in range(slot0, slot0 + nsl):
                            w(st["r_kv"][sl])
                        v = vector.tensor_scalar(
                            out=kTr[:, _page * 512 + slot0 * 128:
                                    _page * 512 + slot0 * 128 + n_tok],
                            in0=PS[_bank][:, 0:n_tok],
                            scalar1=bqt[:, 8 + _page:9 + _page], scalar2=None,
                            op0=ALU.add)
                        sm.inc(v, "V")
                    sec("vector", _)
                    st["br"][bank] = sm.tok("V")
                for sl in range(slot0, slot0 + nsl):
                    st["t_kv"][sl] = sm.tok("V")

            def v_prod(j, src_buf, src_tok_key, tok_off, n_tok, slot0,
                       par=None):
                t_wq, wqb, wqp = wq_load(j)
                for i in range(n_tok // 128):
                    sl = slot0 + i
                    bank = 6 + i % 2

                    def _(tensor, w, _i=i, _bank=bank, _twq=t_wq, _wq=wqb):
                        w(_twq)
                        w(st["br"][_bank])
                        w(st[src_tok_key])
                        w(ld_all)
                        col = tok_off + _i * 128
                        for p in range(PG):
                            tensor.matmul(
                                PS[_bank][:, 0:512],
                                src_buf[:, p * 512 + col: p * 512 + col + 128],
                                _wq[:, p * 512: p * 512 + 512],
                                start=(p == 0), stop=False)
                        t = tensor.matmul(
                            PS[_bank][:, 0:512], ones1,
                            bqv[0:1, (j - 4) * 512:(j - 4) * 512 + 512],
                            start=False, stop=True)
                        sm.inc(t, "T")
                    sec("tensor", _)
                    st["bw"][bank] = sm.tok("T")
                    st["r_wQ"][wqp] = sm.tok("T")
                    if par is not None:
                        st["r_xnTp"][par] = sm.tok("T")
                    else:
                        st["r_qTs"] = sm.tok("T")

                    def _(vector, w, _sl=sl, _bank=bank, _j=j):
                        w(st["bw"][_bank])
                        w(st["r_kv"][_sl])
                        v = vector.tensor_copy(
                            out=vrb[:, _sl * 1024 + (_j - 4) * 512:
                                    _sl * 1024 + (_j - 4) * 512 + 512],
                            in_=PS[_bank][:, 0:512])
                        sm.inc(v, "V")
                    sec("vector", _)
                    st["br"][bank] = sm.tok("V")
                    st["t_kv"][sl] = sm.tok("V")

            for ht in range(2):
                lt = emit_ln(acc[:, ht * 1024:(ht + 1) * 1024], atk[:, :],
                             lnscr[:, :],
                             pre=[ld_all, st["r_atk"]])
                dsts = [qTs[:, p * 512 + ht * 128: p * 512 + ht * 128 + 128]
                        for p in range(PG)]
                tt, vt = emit_tile_transposes(atk, dsts, pre_t=[lt],
                                              pre_v=[st["r_qTs"]])
                st["r_atk"] = tt
                st["t_qTs"] = vt

            k_prod(2, qTs, "t_qTs", 0, 256, 0)
            k_prod(3, qTs, "t_qTs", 0, 256, 0)
            v_prod(4, qTs, "t_qTs", 0, 256, 0)
            v_prod(5, qTs, "t_qTs", 0, 256, 0)

            def _(vector, w):
                w(st["t_kv"][0])
                w(st["t_kv"][1])
                w(ld_all)
                kh = kTr[:, 0:4096].rearrange("p (g f) -> p g f",
                                              f=512)[:, :, 0:256]
                v = vector.tensor_scalar(out=kh, in0=kh, scalar1=hv[:, 0:1],
                                         scalar2=None, op0=ALU.mult)
                w(sm.inc(v, "V"))
                v = vector.tensor_scalar(out=vrb[:, 0:2048],
                                         in0=vrb[:, 0:2048],
                                         scalar1=hv[:, 0:1], scalar2=None,
                                         op0=ALU.mult)
                sm.inc(v, "V")
            sec("vector", _)
            st["t_kv"][0] = st["t_kv"][1] = sm.tok("V")
            halo_v_done = sm.tok("V")

            ls_cnt = 0
            st["t_xnTp"] = [None, None]
            st["r_xnTp"] = [None, None]

            def emit_front(_s):
                nonlocal ls_cnt
                _ax = [acc, acc2][_s % 2]
                _xt = [xnT, xnT2][_s % 2]

                def _(sync, w):
                    w(("ST" if _s % 2 == 0 else "SU", 16 * (_s // 2)))
                    if _s == 0:
                        w(halo_v_done)
                        w(st["r_qTs"])
                    sm.inc(sync.dma_start(
                        out=_ax[:, 0:4096].rearrange("p (t f) -> p t f", f=D),
                        in_=x_d[HALO + _s * 512: HALO + (_s + 1) * 512, :]
                        .rearrange("(t p) f -> p t f", p=128)),
                        "LX" if _s % 2 == 0 else "LY", 16)
                sec("sync", _)
                st["t_accld"] = sm.tok("LX" if _s % 2 == 0 else "LY")

                for t in range(4):
                    ab = [atk, atkB][t % 2]
                    key = "r_atk" if t % 2 == 0 else "r_atkB"
                    lsn = "LS" if t % 2 == 0 else "LT"
                    lt = emit_ln(_ax[:, t * 1024:(t + 1) * 1024], ab[:, :],
                                 lnscr[:, :],
                                 pre=[st["t_accld"], st.get(key),
                                      sm.tok(lsn)])
                    def _(scalar, w, _t=t, _lt=lt, _ab=ab, _ls=lsn):
                        w(_lt)
                        sm.inc(scalar.dma_start(
                            out=xns_d[(_s * 4 + _t) * 128:
                                      (_s * 4 + _t) * 128 + 128, :],
                            in_=_ab[:, :]), _ls, 16)
                    sec("scalar", _)
                    ls_cnt += 16
                    dsts = [_xt[:, p * 512 + t * 128: p * 512 + t * 128 + 128]
                            for p in range(PG)]
                    tt, vt = emit_tile_transposes(
                        ab, dsts, pre_t=[lt],
                        pre_v=[st["r_xnTp"][_s % 2]])
                    st[key] = tt
                st["t_xnTp"][_s % 2] = sm.tok("V")

            emit_front(0)
            for s in range(NSLC):
                g0 = s * 4
                accx = [acc, acc2][s % 2]
                xnT_s = [xnT, xnT2][s % 2]
                st["t_xnT"] = st["t_xnTp"][s % 2]

                for j in (0, 1):
                    t_wq, wqb, wqp = wq_load(j)
                    for cb in range(4):
                        page = j * 4 + cb
                        bank = 6 + cb % 2

                        def _(tensor, w, _cb=cb, _bank=bank, _twq=t_wq,
                              _wq=wqb):
                            w(_twq)
                            w(st["br"][_bank])
                            w(st["t_xnT"])
                            t = None
                            for p in range(PG):
                                t = tensor.matmul(
                                    PS[_bank][:, 0:512],
                                    _wq[:, p * 512 + _cb * 128:
                                        p * 512 + _cb * 128 + 128],
                                    xnT_s[:, p * 512: p * 512 + 512],
                                    start=(p == 0), stop=(p == PG - 1))
                            sm.inc(t, "T")
                        sec("tensor", _)
                        st["bw"][bank] = sm.tok("T")
                        st["r_wQ"][wqp] = sm.tok("T")
                        st["r_xnTp"][s % 2] = sm.tok("T")

                        def _(vector, w, _page=page, _bank=bank):
                            w(st["bw"][_bank])
                            w(st["r_qTs"])
                            v = vector.tensor_scalar(
                                out=qTs[:, _page * 512: _page * 512 + 512],
                                in0=PS[_bank][:, 0:512],
                                scalar1=bqt[:, _page:_page + 1], scalar2=None,
                                op0=ALU.add)
                            sm.inc(v, "V")
                        sec("vector", _)
                        st["br"][bank] = sm.tok("V")
                st["t_qTs"] = sm.tok("V")

                def _(tensor, w):
                    w(st["br"][2])
                    w(st["t_xnT"])
                    w(gwT_rdy)
                    t = None
                    for t4 in range(4):
                        for p in range(PG):
                            t = tensor.matmul(
                                PS[2][:, t4 * 8: t4 * 8 + 8],
                                xnT_s[:, p * 512 + t4 * 128:
                                    p * 512 + t4 * 128 + 128],
                                gwT[:, p * 8:(p + 1) * 8],
                                start=(p == 0), stop=(p == PG - 1))
                    sm.inc(t, "T")
                sec("tensor", _)
                st["bw"][2] = sm.tok("T")
                st["r_xnTp"][s % 2] = sm.tok("T")

                prev_gate_v = sm.tok("V")

                def _(scalar, w, _pg=prev_gate_v):
                    w(st["bw"][2])
                    w(_pg)
                    a = scalar.activation(out=gsc[:, 0:32], in_=PS[2][:, 0:32],
                                          func=AF.Sigmoid)
                    sm.inc(a, "A")
                sec("scalar", _)
                a_g = sm.tok("A")
                st["br"][2] = a_g

                def _(vector, w, _ag=a_g, _s=s):
                    w(_ag)
                    for t4 in range(4):
                        c = t4 * 8
                        gcol = (_s * 4 + t4) * 8
                        v = vector.tensor_tensor(out=gbi[:, c:c + 8],
                                                 in0=gsc[:, c:c + 8],
                                                 in1=ebb[:, 0:8], op=ALU.add)
                        w(sm.inc(v, "V"))
                        v = vector.max(out=gt8[:, c:c + 8], in_=gbi[:, c:c + 8])
                        w(sm.inc(v, "V"))
                        v = vector.tensor_scalar(out=gmf[:, gcol:gcol + 8],
                                                 in0=gbi[:, c:c + 8],
                                                 scalar1=gt8[:, c + 1:c + 2],
                                                 scalar2=None, op0=ALU.is_ge)
                        w(sm.inc(v, "V"))
                        v = vector.tensor_tensor(out=gbi[:, c:c + 8],
                                                 in0=gmf[:, gcol:gcol + 8],
                                                 in1=gsc[:, c:c + 8],
                                                 op=ALU.mult)
                        w(sm.inc(v, "V"))
                        v = vector.tensor_tensor(out=g2f[:, gcol:gcol + 8],
                                                 in0=gbi[:, c:c + 8],
                                                 in1=gbi[:, c:c + 8],
                                                 op=ALU.mult)
                        sm.inc(v, "V")
                sec("vector", _)

                for h in range(2):
                    if h == 1 and s + 1 < NSLC:
                        emit_front(s + 1)
                    gA = g0 + 2 * h
                    slot0 = (gA + 2) % 4
                    k_prod(2, xnT_s, "t_xnT", 2 * h * 128, 256, slot0, par=s % 2)
                    k_prod(3, xnT_s, "t_xnT", 2 * h * 128, 256, slot0, par=s % 2)
                    v_prod(4, xnT_s, "t_xnT", 2 * h * 128, 256, slot0, par=s % 2)
                    v_prod(5, xnT_s, "t_xnT", 2 * h * 128, 256, slot0, par=s % 2)

                    for g in (gA, gA + 1):
                        tloc = g - g0
                        slots = [((kt + 2) % 4, kt - g) for kt in
                                 (g - 2, g - 1, g)]

                        def _(tensor, w, _tloc=tloc, _slots=slots):
                            w(st["br"][2])
                            w(st["t_qTs"])
                            for sl, _d in _slots:
                                w(st["t_kv"][sl])
                            t = None
                            for si, (sl, _d) in enumerate(_slots):
                                for p in range(PG):
                                    t = tensor.matmul(
                                        PS[2][:, si * 128: si * 128 + 128],
                                        kTr[:, p * 512 + sl * 128:
                                            p * 512 + sl * 128 + 128],
                                        qTs[:, p * 512 + _tloc * 128:
                                            p * 512 + _tloc * 128 + 128],
                                        start=(p == 0), stop=(p == PG - 1))
                            sm.inc(t, "T")
                        sec("tensor", _)
                        st["bw"][2] = sm.tok("T")
                        st["r_qTs"] = sm.tok("T")
                        for sl, _d in slots:
                            st["r_kv"][sl] = sm.tok("T")

                        def _(vector, w, _slots=slots):
                            w(st["bw"][2])
                            w(st["r_ST"])
                            w(ld_all)
                            v = None
                            for si, (sl, delta) in enumerate(_slots):
                                src = PS[2][:, si * 128: si * 128 + 128]
                                dst = STb[:, si * 128: si * 128 + 128]
                                if delta == -1:
                                    v = vector.tensor_scalar(
                                        out=dst, in0=src, scalar1=SCL,
                                        scalar2=None, op0=ALU.mult)
                                else:
                                    msk = mask0 if delta == 0 else mask2
                                    v = vector.scalar_tensor_tensor(
                                        out=dst, in0=src, scalar=SCL, in1=msk,
                                        op0=ALU.mult, op1=ALU.mult)
                            sm.inc(v, "V")
                        sec("vector", _)
                        st["br"][2] = sm.tok("V")
                        st["t_ST"] = sm.tok("V")

                        def _(tensor, w, _slots=slots):
                            w(st["t_ST"])
                            w(st["br"][4])
                            w(st["br"][5])
                            for sl, _d in _slots:
                                w(st["t_kv"][sl])
                            t = None
                            for dp in range(PG):
                                bank = 4 + dp // 4
                                for si, (sl, _d) in enumerate(_slots):
                                    t = tensor.matmul(
                                        PS[bank][:, (dp % 4) * 128:
                                                 (dp % 4) * 128 + 128],
                                        vrb[:, sl * 1024 + dp * 128:
                                            sl * 1024 + dp * 128 + 128],
                                        STb[:, si * 128: si * 128 + 128],
                                        start=(si == 0), stop=(si == 2))
                            sm.inc(t, "T")
                        sec("tensor", _)
                        st["bw"][4] = st["bw"][5] = sm.tok("T")
                        st["r_ST"] = sm.tok("T")
                        for sl, _d in slots:
                            st["r_kv"][sl] = sm.tok("T")

                        def _(vector, w):
                            w(st["bw"][4])
                            w(st["r_aT"])
                            w(st["t_aT"])
                            vector.tensor_copy(out=aTb[:, 0:512],
                                               in_=PS[4][:, 0:512])
                            v = vector.tensor_copy(out=aTb[:, 512:1024],
                                                   in_=PS[5][:, 0:512])
                            sm.inc(v, "V")
                        sec("vector", _)
                        st["br"][4] = st["br"][5] = sm.tok("V")
                        st["t_aT"] = sm.tok("V")

                        dsts = [atkC[:, p * 128:(p + 1) * 128]
                                for p in range(PG)]
                        tt, vt = emit_tile_transposes(
                            aTb, dsts, pre_t=[st["t_aT"]],
                            pre_v=[st["r_atkC"]])
                        st["r_aT"] = tt
                        st["t_atk"] = vt

                        lt = emit_ln(atkC[:, :], atkC[:, :], aTb[:, :],
                                     pre=[st["t_atk"], st["r_aT"],
                                          st["t_aT"]], so=4, akey="ln_a2")

                        def _(gpsimd, w, _tloc=tloc, _lt=lt, _ax=accx):
                            w(_lt)
                            w(ld_all)
                            v = gpsimd.tensor_tensor(out=aTb[:, :],
                                                     in0=atkC[:, :],
                                                     in1=lnwb[:, :],
                                                     op=ALU.mult)
                            w(sm.inc(v, "G"))
                            v = gpsimd.tensor_tensor(out=aTb[:, :],
                                                     in0=aTb[:, :],
                                                     in1=lnbb[:, :],
                                                     op=ALU.add)
                            w(sm.inc(v, "G"))
                            if "att" in dbg_skip:
                                v = gpsimd.tensor_scalar(
                                    out=aTb[:, :], in0=aTb[:, :], scalar1=0.0,
                                    scalar2=None, op0=ALU.mult)
                                w(sm.inc(v, "G"))
                            w(st["t_accld"])
                            w(st["acc_wv"])
                            v = gpsimd.tensor_tensor(
                                out=_ax[:, _tloc * 1024:(_tloc + 1) * 1024],
                                in0=_ax[:, _tloc * 1024:(_tloc + 1) * 1024],
                                in1=aTb[:, :], op=ALU.add)
                            sm.inc(v, "G")
                        sec("gpsimd", _)
                        st["r_atkC"] = sm.tok("G")
                        st["acc_wv"] = sm.tok("G")
                        st["t_aT"] = sm.tok("G")
                        st["r_aT"] = sm.tok("G")

                def _(scalar, w, _s=s, _ax=accx):
                    w(st["acc_wv"])
                    sm.inc(scalar.dma_start(
                        out=out_d[_s * 512:(_s + 1) * 512, :]
                        .rearrange("(t p) f -> p t f", p=128),
                        in_=_ax[:, 0:4096].rearrange("p (t f) -> p t f", f=D)),
                        "ST" if _s % 2 == 0 else "SU", 16)
                sec("scalar", _)

            bar = [sm.tok("T"), sm.tok("V"), sm.tok("A"), sm.tok("G"),
                   ("ST", 32), ("SU", 32), sm.tok("LS"), sm.tok("LT"),
                   sm.tok("LQ"), sm.tok("LR"), sm.tok("LX"),
                   sm.tok("LY")]

            for t in range(16):
                def _(tensor, w, _t=t):
                    for tk in bar:
                        w(tk)
                    w(st["br"][2])
                    w(base_tok)
                    tensor.matmul(PS[2][:, 0:8], mask0,
                                  gmf[:, _t * 8: _t * 8 + 8],
                                  start=True, stop=False)
                    tensor.matmul(PS[2][:, 0:8], ones1, base[0:1, :],
                                  start=False, stop=True)
                    t2 = tensor.matmul(PS[2][0:1, 8:16],
                                       cst[:, C_ONE:C_ONE + 1],
                                       gmf[:, _t * 8: _t * 8 + 8],
                                       start=True, stop=True)
                    sm.inc(t2, "T")
                sec("tensor", _)
                st["bw"][2] = sm.tok("T")

                def _(vector, w, _t=t):
                    w(st["bw"][2])
                    vector.tensor_copy(out=cmf[:, _t * 8: _t * 8 + 8],
                                       in_=PS[2][:, 0:8])
                    v = vector.tensor_tensor(out=base[:, :], in0=base[:, :],
                                             in1=PS[2][0:1, 8:16],
                                             op=ALU.add)
                    sm.inc(v, "V")
                sec("vector", _)
                st["br"][2] = sm.tok("V")
                base_tok = sm.tok("V")

            def _(vector, w):
                w(base_tok)
                v = vector.tensor_tensor(out=cmf[:, :], in0=cmf[:, :],
                                         in1=gmf[:, :], op=ALU.mult)
                sm.inc(v, "V")
            sec("vector", _)
            cmsk_tok = sm.tok("V")

            r_E = [None, None]
            r_cg = None
            t_rowv = None
            r_row = None
            for e in range(NE):
                def _(vector, w, _e=e, _rcg=r_cg):
                    w(cmsk_tok)
                    w(_rcg)
                    cgv = cg[:, 0:32].rearrange("p (t c) -> p t c", c=2)
                    vector.tensor_copy(
                        out=cgv[:, :, 0:1],
                        in_=cst[:, C_TOK:C_TOK + 16]
                        .rearrange("p (t c) -> p t c", c=1))
                    v = vector.tensor_copy(
                        out=cgv[:, :, 1:2],
                        in_=g2f[:, 0:128].rearrange("p (t c) -> p t c",
                                                    c=8)[:, :, _e:_e + 1])
                    sm.inc(v, "V")
                sec("vector", _)
                cg_rdy = sm.tok("V")

                for t in range(16):
                    ebuf = t % 2

                    def _(vector, w, _e=e, _t=t, _eb=ebuf):
                        w(cmsk_tok)
                        w(r_E[_eb])
                        v = vector.tensor_scalar(
                            out=Epr[:, _eb * 768:_eb * 768 + 768],
                            in0=cst[:, C_IOTA:C_IOTA + 768],
                            scalar1=cmf[:, _t * 8 + _e: _t * 8 + _e + 1],
                            scalar2=None, op0=ALU.is_equal)
                        sm.inc(v, "V")
                    sec("vector", _)
                    e_tok = sm.tok("V")

                    def _(tensor, w, _t=t, _eb=ebuf, _et=e_tok,
                          _cg=cg_rdy):
                        w(_et)
                        w(_cg)
                        if _t == 0:
                            w(st["br"][2])
                            w(st["br"][3])
                        tensor.matmul(
                            PS[2][0:2, 0:512], cg[:, 2 * _t: 2 * _t + 2],
                            Epr[:, _eb * 768: _eb * 768 + 512],
                            start=(_t == 0), stop=(_t == 15))
                        t2 = tensor.matmul(
                            PS[3][0:2, 0:256], cg[:, 2 * _t: 2 * _t + 2],
                            Epr[:, _eb * 768 + 512: _eb * 768 + 768],
                            start=(_t == 0), stop=(_t == 15))
                        sm.inc(t2, "T")
                    sec("tensor", _)
                    r_E[ebuf] = sm.tok("T")
                r_cg = sm.tok("T")
                st["bw"][2] = st["bw"][3] = sm.tok("T")

                def _(vector, w, _rr=r_row):
                    w(st["bw"][2])
                    w(_rr)
                    vector.tensor_copy(out=idxrow[0:2, 0:512],
                                       in_=PS[2][0:2, 0:512])
                    v = vector.tensor_copy(out=idxrow[0:2, 512:768],
                                           in_=PS[3][0:2, 0:256])
                    sm.inc(v, "V")
                sec("vector", _)
                st["br"][2] = st["br"][3] = sm.tok("V")
                t_rowv = sm.tok("V")

                def _(tensor, w, _tv=t_rowv):
                    w(_tv)
                    w(st["br"][4])
                    t2 = None
                    for ch in range(NCH):
                        t2 = tensor.transpose(
                            PS[4][:, ch * 2: ch * 2 + 2],
                            idxrow[0:2, ch * 128: ch * 128 + 128],
                            ident[0:2, 0:2])
                    sm.inc(t2, "T")
                sec("tensor", _)
                st["bw"][4] = sm.tok("T")
                r_row = sm.tok("T")

                def _(vector, w, _e=e):
                    w(st["bw"][4])
                    p4 = PS[4][:, 0:12].rearrange("p (g f) -> p g f", f=2)
                    vector.tensor_copy(
                        out=idxc[:, _e * NCH:(_e + 1) * NCH]
                        .rearrange("p (g f) -> p g f", f=1),
                        in_=p4[:, :, 0:1])
                    v = vector.tensor_copy(
                        out=g2gc[:, _e * NCH:(_e + 1) * NCH]
                        .rearrange("p (g f) -> p g f", f=1),
                        in_=p4[:, :, 1:2])
                    sm.inc(v, "V")
                sec("vector", _)
                st["br"][4] = sm.tok("V")
            idx_rdy = sm.tok("V")

            r_xgs = None
            t_xgT = None
            r_xgT = None
            t_yg = [None] * NCH
            r_yg = None
            lg_cnt = 0
            sc_cnt = 0

            def emit_gather(_e):
                nonlocal lg_cnt
                def _(gpsimd, w, _rx=r_xgs):
                    for tk in bar:
                        w(tk)
                    w(idx_rdy)
                    w(_rx)
                    for ti in range(NCH):
                        sm.inc(gpsimd.indirect_dma_start(
                            out=xgs[:, ti * 1024: ti * 1024 + 1024],
                            out_offset=None,
                            in_=xns_d[:, :],
                            in_offset=IndirectOffsetOnAxis(
                                ap=idxc[:, _e * NCH + ti: _e * NCH + ti + 1],
                                axis=0)), "LG", 16)
                sec("gpsimd", _)
                lg_cnt += 16 * NCH
                return ("LG", lg_cnt)

            def emit_scatter(_e, _tyg, _sc):
                def _(gpsimd, w):
                    for tk in _tyg:
                        w(tk)
                    for ti in range(NCH):
                        gpsimd.wait_ge(sm.sem["SC"], _sc + 16 * ti)
                        sm.inc(gpsimd.indirect_dma_start(
                            out=out_d[:, :],
                            out_offset=IndirectOffsetOnAxis(
                                ap=idxc[:, _e * NCH + ti: _e * NCH + ti + 1],
                                axis=0),
                            in_=yg[:, ti * 1024: ti * 1024 + 1024],
                            in_offset=None,
                            compute_op=ALU.add), "SC", 16)
                sec("gpsimd", _)

            def emit_xg_transposes(_tx):
                nonlocal r_xgs, r_xgT
                for p in range(PG):
                    for half in range(2):
                        n_ti = 4 if half == 0 else 2
                        bank = (p * 2 + half) % 4

                        def _(tensor, w, _p=p, _h=half, _n=n_ti, _bank=bank,
                              _tx2=_tx):
                            w(_tx2)
                            w(st["br"][_bank])
                            w(r_xgT)
                            t = None
                            for i in range(_n):
                                ti = _h * 4 + i
                                t = tensor.transpose(
                                    PS[_bank][:, i * 128: i * 128 + 128],
                                    xgs[:, ti * 1024 + _p * 128:
                                        ti * 1024 + _p * 128 + 128],
                                    ident)
                            sm.inc(t, "T")
                        sec("tensor", _)
                        st["bw"][bank] = sm.tok("T")
                        r_xgs = sm.tok("T")

                        def _(scalar, w, _p=p, _h=half, _n=n_ti, _bank=bank):
                            w(st["bw"][_bank])
                            off = _p * CAP + _h * 512
                            a = scalar.activation(
                                out=xgTh[:, off:off + _n * 128],
                                in_=PS[_bank][:, 0:_n * 128],
                                func=AF.Identity)
                            sm.inc(a, "A")
                        sec("scalar", _)
                        a_hi2 = sm.tok("A")

                        def _(vector, w, _p=p, _h=half, _n=n_ti, _bank=bank,
                              _ah=a_hi2):
                            w(st["bw"][_bank])
                            w(_ah)
                            off = _p * CAP + _h * 512
                            v = vector.scalar_tensor_tensor(
                                out=xgTl[:, off:off + _n * 128],
                                in0=PS[_bank][:, 0:_n * 128],
                                scalar=1.0, in1=xgTh[:, off:off + _n * 128],
                                op0=ALU.mult, op1=ALU.subtract)
                            sm.inc(v, "V")
                        sec("vector", _)
                        st["br"][bank] = sm.tok("V")
                return (sm.tok("A"), sm.tok("V"))

            t_xgs = emit_gather(0)
            xg_done = emit_xg_transposes(t_xgs)
            def emit_wstage(pg):
                _e, _ph = pg // 8, pg % 8
                _up = _ph < 4
                _q = _ph % 4
                _wi = pg % 2

                def _(sync, w, _e=_e, _q=_q, _up=_up, _wi=_wi):
                    for tk in bar:
                        w(tk)
                    w(st["r_stage"][_wi])
                    ln = "L0" if _wi == 0 else "L1"
                    if _up:
                        src = wut_d[_e, :, _q * 512:(_q + 1) * 512]
                    else:
                        src = wdt_d[_e, :, _q * 256:(_q + 1) * 256]
                    sm.inc(sync.dma_start(
                        out=wst[_wi][:, 0:4096].rearrange(
                            "p (c f) -> p c f", f=512 if _up else 256),
                        in_=src.rearrange("(g p) f -> p g f", p=128)),
                        ln, 16)
                sec("sync", _)
                t_dma = sm.tok("L0" if _wi == 0 else "L1")

                def _(scalar, w, _wi=_wi, _td=t_dma, _rw=st["r_wh"][_wi]):
                    w(_td)
                    w(_rw)
                    a = scalar.activation(
                        out=wBh[_wi][:, :], in_=wst[_wi][:, :],
                        func=AF.Identity)
                    sm.inc(a, "A")
                sec("scalar", _)
                t_whi = sm.tok("A")

                def _(gpsimd, w, _wi=_wi, _td=t_dma, _th=t_whi,
                      _rw=st["r_wh"][_wi]):
                    w(_td)
                    w(_th)
                    w(_rw)
                    g = gpsimd.tensor_tensor(
                        out=wBl[_wi][:, :], in0=wst[_wi][:, :],
                        in1=wBh[_wi][:, :], op=ALU.subtract)
                    sm.inc(g, "G")
                sec("gpsimd", _)
                st["t_wA"][_wi] = t_whi
                st["t_wlo"][_wi] = sm.tok("G")
                st["r_stage"][_wi] = sm.tok("G")

            emit_wstage(0)
            pend_scatter = None
            for e in range(NE):

                for ph in range(8):
                    up = ph < 4
                    q = ph % 4
                    wi = (e * 8 + ph) % 2

                    if ph == 1 and pend_scatter is not None:
                        emit_scatter(*pend_scatter)
                        pend_scatter = None
                    if ph == 5 and e + 1 < NE:
                        xg_done_next = emit_xg_transposes(emit_gather(e + 1))
                    if e * 8 + ph + 1 < NE * 8:
                        emit_wstage(e * 8 + ph + 1)

                    if up:
                        for m in range(4):
                            hp = q * 4 + m
                            for c in range(2):
                                ntk = 512 if c == 0 else 256
                                bank = 4 + (m * 2 + c) % 4

                                def _(tensor, w, _m=m, _c=c, _n=ntk,
                                      _bank=bank, _wi=wi):
                                    w(st["t_wA"][_wi])
                                    w(xg_done[0])
                                    w(xg_done[1])
                                    w(st["br"][_bank])
                                    wh = wBh[_wi][:, :]
                                    xh = xgTh[:, :]
                                    xl = xgTl[:, :]
                                    t = None
                                    first = True
                                    for p in range(PG):
                                        wcol = p * 512 + _m * 128
                                        xcol = p * CAP + _c * 512
                                        for rh in (xh, xl):
                                            t = tensor.matmul(
                                                PS[_bank][:, 0:_n],
                                                wh[:, wcol:wcol + 128],
                                                rh[:, xcol:xcol + _n],
                                                start=first, stop=False)
                                            first = False
                                    sm.inc(t, "T")
                                sec("tensor", _)

                                def _(tensor, w, _m=m, _c=c, _n=ntk,
                                      _bank=bank, _wi=wi):
                                    w(st["t_wlo"][_wi])
                                    wl = wBl[_wi][:, :]
                                    xh = xgTh[:, :]
                                    t = None
                                    for p in range(PG):
                                        wcol = p * 512 + _m * 128
                                        xcol = p * CAP + _c * 512
                                        t = tensor.matmul(
                                            PS[_bank][:, 0:_n],
                                            wl[:, wcol:wcol + 128],
                                            xh[:, xcol:xcol + _n],
                                            start=False, stop=(p == PG - 1))
                                    sm.inc(t, "T")
                                sec("tensor", _)
                                st["bw"][bank] = sm.tok("T")
                                st["r_wA"][wi] = sm.tok("T")
                                st["r_wh"][wi] = sm.tok("T")
                                r_xgT = sm.tok("T")

                                def _(vector, w, _hp=hp, _c=c, _n=ntk,
                                      _bank=bank):
                                    w(st["bw"][_bank])
                                    w(st["r_Hb"])
                                    off = _hp * CAP + _c * 512
                                    v = vector.tensor_scalar(
                                        out=rsq[:, 0:_n],
                                        in0=PS[_bank][:, 0:_n],
                                        scalar1=0.0, scalar2=None,
                                        op0=ALU.max)
                                    w(sm.inc(v, "V"))
                                    v = vector.tensor_tensor(
                                        out=PS[_bank][:, 0:_n],
                                        in0=rsq[:, 0:_n],
                                        in1=rsq[:, 0:_n],
                                        op=ALU.mult)
                                    w(sm.inc(v, "V"))
                                    v = vector.tensor_scalar(
                                        out=Hh[:, off:off + _n],
                                        in0=PS[_bank][:, 0:_n],
                                        scalar1=1.0, scalar2=None,
                                        op0=ALU.mult)
                                    w(sm.inc(v, "V"))
                                    v = vector.scalar_tensor_tensor(
                                        out=Hl[:, off:off + _n],
                                        in0=PS[_bank][:, 0:_n],
                                        scalar=1.0, in1=Hh[:, off:off + _n],
                                        op0=ALU.mult, op1=ALU.subtract)
                                    sm.inc(v, "V")
                                sec("vector", _)
                                st["br"][bank] = sm.tok("V")
                        st["t_Hb"] = sm.tok("V")
                    else:
                        for ti in range(NCH):
                            bank = 4 + ti % 4

                            def _(tensor, w, _ti=ti, _bank=bank, _wi=wi):
                                w(st["t_wA"][_wi])
                                w(st["t_Hb"])
                                w(st["br"][_bank])
                                hl = Hl[:, :]
                                wh = wBh[_wi][:, :]
                                t = None
                                first = True
                                for hp in range(16):
                                    hcol = hp * CAP + _ti * 128
                                    wcol = hp * 256
                                    t = tensor.matmul(
                                        PS[_bank][:, 0:256],
                                        hl[:, hcol:hcol + 128],
                                        wh[:, wcol:wcol + 256],
                                        start=first, stop=False)
                                    first = False
                                sm.inc(t, "T")
                            sec("tensor", _)

                            def _(tensor, w, _ti=ti, _bank=bank, _wi=wi):
                                w(st["t_wlo"][_wi])
                                hh = Hh[:, :]
                                wh = wBh[_wi][:, :]
                                wl = wBl[_wi][:, :]
                                t = None
                                for hp in range(16):
                                    hcol = hp * CAP + _ti * 128
                                    wcol = hp * 256
                                    for rh in (wh, wl):
                                        t = tensor.matmul(
                                            PS[_bank][:, 0:256],
                                            hh[:, hcol:hcol + 128],
                                            rh[:, wcol:wcol + 256],
                                            start=False,
                                            stop=(hp == 15 and rh is wl))
                                sm.inc(t, "T")
                            sec("tensor", _)
                            st["bw"][bank] = sm.tok("T")
                            st["r_wA"][wi] = sm.tok("T")
                            st["r_wh"][wi] = sm.tok("T")
                            st["r_Hb"] = sm.tok("T")

                            def _(vector, w, _ti=ti, _q=q, _e=e, _bank=bank):
                                w(st["bw"][_bank])
                                w(idx_rdy)
                                w(r_yg)
                                sc1 = (0.0 if "moe" in dbg_skip else
                                       g2gc[:, _e * NCH + _ti:
                                            _e * NCH + _ti + 1])
                                v = vector.tensor_scalar(
                                    out=yg[:, _ti * 1024 + _q * 256:
                                           _ti * 1024 + _q * 256 + 256],
                                    in0=PS[_bank][:, 0:256],
                                    scalar1=sc1, scalar2=None, op0=ALU.mult)
                                sm.inc(v, "V")
                            sec("vector", _)
                            st["br"][bank] = sm.tok("V")
                            t_yg[ti] = sm.tok("V")

                pend_scatter = (e, list(t_yg), sc_cnt)
                sc_cnt += 16 * NCH
                r_yg = ("SC", sc_cnt)
                if e + 1 < NE:
                    xg_done = xg_done_next

            if pend_scatter is not None:
                emit_scatter(*pend_scatter)

            def _(gpsimd, w):
                gpsimd.wait_ge(sm.sem["ST"], 32)
                gpsimd.wait_ge(sm.sem["SU"], 32)
                gpsimd.wait_ge(sm.sem["SC"], sc_cnt)
            sec("gpsimd", _)

    return nc


def kernel(**inputs):
    x = np.ascontiguousarray(np.asarray(inputs["L0_attn_x"], dtype=np.float32))
    wqkv = np.ascontiguousarray(np.asarray(inputs["L0_attn_W_qkv"],
                                           dtype=np.float32))
    bqkv = np.asarray(inputs["L0_attn_b_qkv"], dtype=np.float32)
    lnw = np.asarray(inputs["L0_attn_ln_w"], dtype=np.float32)
    lnb = np.asarray(inputs["L0_attn_ln_b"], dtype=np.float32)
    gatet = np.ascontiguousarray(
        np.asarray(inputs["L0_ffn_gate_w"], dtype=np.float32).T)
    wut = np.ascontiguousarray(
        np.asarray(inputs["L0_ffn_w_up"], dtype=np.float32).transpose(0, 2, 1))
    wdt = np.ascontiguousarray(
        np.asarray(inputs["L0_ffn_w_down"],
                   dtype=np.float32).transpose(0, 2, 1))
    ebias = np.asarray(inputs["L0_ffn_expert_bias"], dtype=np.float32)

    B, T, C = x.shape
    assert (B, T, C) == (2, 8192, 1024)

    bqkvt = np.ascontiguousarray(bqkv.reshape(24, 128).T)
    bqkvv = np.ascontiguousarray(bqkv[2048:].reshape(1, 1024))
    lnwb = np.ascontiguousarray(np.broadcast_to(lnw[None, :], (128, 1024)))
    lnbb = np.ascontiguousarray(np.broadcast_to(lnb[None, :], (128, 1024)))
    eb1 = np.ascontiguousarray(ebias.reshape(1, 8))

    kk = np.arange(128)[:, None]
    qq = np.arange(128)[None, :]
    cstm = np.zeros((128, C_TOT), dtype=np.float32)
    cstm[:, C_ID:C_ID + 128] = np.eye(128, dtype=np.float32)
    cstm[:, C_M0:C_M0 + 128] = (kk <= qq).astype(np.float32)
    cstm[:, C_M2:C_M2 + 128] = (kk > qq).astype(np.float32)
    cstm[:, C_ONE:C_ONE + 128] = 1.0
    cstm[:, C_IOTA:C_IOTA + CAP] = (np.arange(CAP) + 1)[None, :]
    for t in range(16):
        cstm[:, C_TOK + t] = np.arange(128) + 128 * t

    in_maps = []
    for c in range(8):
        b = c // 4
        s0 = (c % 4) * 2048
        if c % 4 == 0:
            xc = np.concatenate(
                [np.zeros((HALO, C), dtype=np.float32), x[b, 0:T_OWN]], axis=0)
            hvv = np.zeros((128, 1), dtype=np.float32)
        else:
            xc = x[b, s0 - HALO: s0 + T_OWN]
            hvv = np.ones((128, 1), dtype=np.float32)
        in_maps.append({
            "x": np.ascontiguousarray(xc),
            "wqkv": wqkv, "bqkvt": bqkvt, "bqkvv": bqkvv,
            "gatet": gatet, "eb1": eb1, "wut": wut, "wdt": wdt,
            "lnw": lnwb, "lnb": lnbb, "cst": cstm, "hv": hvv,
        })

    nc = build_program()
    res = run_bass_kernel_spmd(nc, in_maps, core_ids=list(range(8)))

    out = np.empty((2, 8192, 1024), dtype=np.float32)
    for c in range(8):
        b = c // 4
        s0 = (c % 4) * 2048
        out[b, s0:s0 + T_OWN] = res.results[c]["out"]
    return out


# version 32
# speedup vs baseline: 1.0785x; 1.0284x over previous
import contextlib
import numpy as np
import concourse.bass as bass
import concourse.mybir as mybir
from concourse.bass import IndirectOffsetOnAxis
from concourse.bass_utils import run_bass_kernel_spmd

F32 = mybir.dt.float32
BF16 = mybir.dt.bfloat16
I32 = mybir.dt.int32
ALU = mybir.AluOpType
AF = mybir.ActivationFunctionType
AX = mybir.AxisListType

D = 1024
PG = 8
HID = 2048
T_OWN = 2048
HALO = 256
NSLC = 4
NE = 8
EPS = 1e-5
SCL = 1.0 / 512.0
CAP = 768
NCH = CAP // 128

C_ID = 0
C_M0 = 128
C_M2 = 256
C_ONE = 384
C_IOTA = 512
C_TOK = 1280
C_TOT = 1296

_ENG2NAME = {"tensor": "T", "vector": "V", "scalar": "A",
             "gpsimd": "G", "sync": "SY"}


class Sems:
    def __init__(self):
        self.cnt = {}
        self.sem = {}
        self.lastw = {}

    def add(self, name, handle):
        self.sem[name] = handle
        self.cnt[name] = 0

    def inc(self, instr, name, k=1):
        instr.then_inc(self.sem[name], k)
        self.cnt[name] += k
        return (name, self.cnt[name])

    def tok(self, name):
        return (name, self.cnt[name])


def build_program(dbg_skip=()):
    nc = bass.Bass("TRN2", target_bir_lowering=False)

    x_d = nc.dram_tensor("x", [HALO + T_OWN, D], F32, kind="ExternalInput").ap()
    wqkv_d = nc.dram_tensor("wqkv", [D, 3 * D], F32, kind="ExternalInput").ap()
    bqkvt_d = nc.dram_tensor("bqkvt", [128, 24], F32, kind="ExternalInput").ap()
    bqkvv_d = nc.dram_tensor("bqkvv", [1, D], F32, kind="ExternalInput").ap()
    gatet_d = nc.dram_tensor("gatet", [D, NE], F32, kind="ExternalInput").ap()
    eb1_d = nc.dram_tensor("eb1", [1, NE], F32, kind="ExternalInput").ap()
    wut_d = nc.dram_tensor("wut", [NE, D, HID], F32, kind="ExternalInput").ap()
    wdt_d = nc.dram_tensor("wdt", [NE, HID, D], F32, kind="ExternalInput").ap()
    lnw_d = nc.dram_tensor("lnw", [128, D], F32, kind="ExternalInput").ap()
    lnb_d = nc.dram_tensor("lnb", [128, D], F32, kind="ExternalInput").ap()
    cst_d = nc.dram_tensor("cst", [128, C_TOT], F32, kind="ExternalInput").ap()
    hv_d = nc.dram_tensor("hv", [128, 1], F32, kind="ExternalInput").ap()
    out_d = nc.dram_tensor("out", [T_OWN, D], F32, kind="ExternalOutput").ap()
    xns_d = nc.dram_tensor("xnscr", [T_OWN, D], F32, kind="Internal").ap()

    sm = Sems()

    with contextlib.ExitStack() as ex:
        ra_h = ex.enter_context(nc.sbuf_tensor("rarena", [128, 20480], F32))

        def rseg(off, n):
            return ra_h[:, off:off + n]

        acc = rseg(0, 4096)
        acc2 = rseg(4096, 4096)
        atk = rseg(8192, 1024)
        atkB = rseg(9216, 1024)
        wQ = rseg(10240, 4096)
        wQ2 = rseg(14336, 4096)
        xgs = rseg(0, 6144)
        yg = rseg(6144, 6144)
        wst = [rseg(12288, 4096), rseg(16384, 4096)]

        VARENA_B = 106496
        back = ex.enter_context(
            nc.sbuf_tensor("varena", [128, VARENA_B], mybir.dt.uint8))
        base_addr = nc.lookup_mloc(back).addr

        def valloc(name, cols, off, dt=F32, rows=128):
            h = nc.virtual_sbuf_tensor(name, [rows, cols], dt)
            nc._tensor(h.name, h.shape, h.dtype, type="SB")
            mloc = nc.lookup_mloc(h)
            mloc.allocated = True
            mloc.addr = base_addr + off
            return h

        xnT = valloc("xnT", 4096, 0)
        qTs = valloc("qTs", 4096, 16384)
        kTr = valloc("kTr", 4096, 32768)
        vrb = valloc("vrb", 4096, 49152)
        STb = valloc("STb", 384, 65536)
        aTb = valloc("aTb", 1024, 67072)
        lnscr = valloc("lnscr", 1024, 71168)
        atkC = valloc("atkC", 1024, 75264)
        xnT2 = valloc("xnT2", 4096, 79360)
        Epr = valloc("Epr", 1536, 71168)
        idxrow = valloc("idxrow", 768, 95744)
        wBh = [valloc("wBh0", 4096, 0, dt=BF16),
               valloc("wBh1", 4096, 16384, dt=BF16)]
        wBl = [valloc("wBl0", 4096, 8192, dt=BF16),
               valloc("wBl1", 4096, 24576, dt=BF16)]
        xgTh = valloc("xgTh", 6144, 32768, dt=BF16)
        xgTl = valloc("xgTl", 6144, 45056, dt=BF16)
        Hh = valloc("Hh", 12288, 57344, dt=BF16)
        Hl = valloc("Hl", 12288, 81920, dt=BF16)

        def sbt(name, cols, rows=128, dt=F32):
            return ex.enter_context(nc.sbuf_tensor(name, [rows, cols], dt))

        cst = sbt("cst_s", C_TOT)
        rsq = sbt("rsq", 512)
        lnwb = sbt("lnwb", 1024)
        lnbb = sbt("lnbb", 1024)
        bqt = sbt("bqt", 24)
        gwT = sbt("gwT", 64)
        ebb = sbt("ebb", 8)
        gsc = sbt("gsc", 32)
        gbi = sbt("gbi", 32)
        gt8 = sbt("gt8", 32)
        g2f = sbt("g2f", 128)
        gmf = sbt("gmf", 128)
        cmf = sbt("cmf", 128)
        stats = sbt("stats", 32)
        hv = sbt("hv_s", 1)
        cg = sbt("cgst", 32)
        idxc = sbt("idxc", NE * NCH, dt=I32)
        g2gc = sbt("g2gc", NE * NCH)
        bqv = ex.enter_context(nc.sbuf_tensor("bqv", [1, D], F32))
        eb1 = ex.enter_context(nc.sbuf_tensor("eb1s", [1, NE], F32))
        base = ex.enter_context(nc.sbuf_tensor("base8", [1, NE], F32))

        PS = [ex.enter_context(nc.psum_tensor(f"ps{i}", [128, 512], F32))
              for i in range(8)]
        for name in ("T", "V", "A", "G", "LC", "LQ", "LR", "L0", "L1", "LX",
                     "LY", "LS", "LT", "LG", "SC", "ST", "SU"):
            sm.add(name, ex.enter_context(nc.semaphore(name)))

        ident = cst[:, C_ID:C_ID + 128]
        mask0 = cst[:, C_M0:C_M0 + 128]
        mask2 = cst[:, C_M2:C_M2 + 128]
        ones1 = cst[0:1, C_ONE:C_ONE + 128]

        st = {
            "bw": [None] * 8, "br": [None] * 8,
            "t_xnT": None, "r_xnT": None,
            "t_qTs": None, "r_qTs": None,
            "t_kv": [None] * 4, "r_kv": [None] * 4,
            "r_wQ": [None, None],
            "t_ST": None, "r_ST": None,
            "t_aT": None, "r_aT": None,
            "t_atk": None, "r_atk": None, "r_atkB": None, "r_atkC": None,
            "t_accld": None, "acc_wv": None,
            "t_wA": [None, None], "r_wA": [None, None],
            "r_wh": [None, None], "r_stage": [None, None],
            "t_wB": None, "r_wB": None, "t_wlo": [None, None],
            "t_Hb": None, "r_Hb": None,
            "ln_a": None, "ln_a2": None,
        }

        with nc.Block() as block:
            def sec(kind, fn):
                wname = _ENG2NAME[kind]

                def wrapped(eng):
                    def w(tok):
                        if tok is None:
                            return
                        name, val = tok
                        if val <= 0:
                            return
                        key = (wname, name)
                        if sm.lastw.get(key, 0) >= val:
                            return
                        eng.wait_ge(sm.sem[name], val)
                        sm.lastw[key] = val
                    fn(eng, w)
                getattr(block, kind)(wrapped)

            def _(sync, w):
                for dst, src in [
                    (cst[:], cst_d[:]), (lnwb[:], lnw_d[:]), (lnbb[:], lnb_d[:]),
                    (bqt[:], bqkvt_d[:]), (bqv[:], bqkvv_d[:]),
                    (eb1[:], eb1_d[:]), (hv[:], hv_d[:]),
                    (gwT[:, 0:64].rearrange("p (g f) -> p g f", f=NE),
                     gatet_d[:, :].rearrange("(g p) f -> p g f", p=128)),
                ]:
                    sm.inc(sync.dma_start(out=dst, in_=src), "LC", 16)
                sm.inc(sync.dma_start(
                    out=acc[:, 0:2048].rearrange("p (t f) -> p t f", f=D),
                    in_=x_d[0:HALO, :].rearrange("(t p) f -> p t f", p=128)),
                    "LC", 16)
            sec("sync", _)
            ld_all = sm.tok("LC")

            def _(vector, w):
                v = vector.memset(base[:, :], 0.0)
                sm.inc(v, "V")
            sec("vector", _)
            base_tok = sm.tok("V")

            def _(tensor, w):
                w(ld_all)
                t = tensor.matmul(PS[1][:, 0:8], ones1, eb1[0:1, :],
                                  start=True, stop=True)
                sm.inc(t, "T")
            sec("tensor", _)
            st["bw"][1] = sm.tok("T")

            def _(vector, w):
                w(st["bw"][1])
                v = vector.tensor_copy(out=ebb[:], in_=PS[1][:, 0:8])
                sm.inc(v, "V")
            sec("vector", _)
            st["br"][1] = sm.tok("V")
            gwT_rdy = ld_all

            def emit_ln(src_ap, dst_ap, scr_ap, pre, so=0, akey="ln_a"):
                b6 = 8 + (so // 4) * 12
                def _(vector, w):
                    for tk in pre:
                        w(tk)
                    w(st[akey])
                    v = vector.bn_stats(out=stats[:, b6:b6 + 6],
                                        in_=src_ap[:, 0:512])
                    w(sm.inc(v, "V"))
                    v = vector.bn_stats(out=stats[:, b6 + 6:b6 + 12],
                                        in_=src_ap[:, 512:1024])
                    w(sm.inc(v, "V"))
                    v = vector.bn_aggr(out=stats[:, so:so + 2],
                                       in_=stats[:, b6:b6 + 12])
                    w(sm.inc(v, "V"))
                    v = vector.tensor_scalar(out=stats[:, so + 1:so + 2],
                                             in0=stats[:, so + 1:so + 2],
                                             scalar1=EPS, scalar2=None,
                                             op0=ALU.add)
                    sm.inc(v, "V")
                sec("vector", _)
                vtok = sm.tok("V")

                def _(scalar, w):
                    w(vtok)
                    a = scalar.activation(out=stats[:, so + 2:so + 3],
                                          in_=stats[:, so + 1:so + 2],
                                          func=AF.Sqrt)
                    sm.inc(a, "A")
                sec("scalar", _)
                st[akey] = sm.tok("A")
                a_sqrt = sm.tok("A")

                def _(vector, w):
                    for tk in pre:
                        w(tk)
                    w(a_sqrt)
                    v = vector.reciprocal(out=stats[:, so + 3:so + 4],
                                          in_=stats[:, so + 2:so + 3])
                    w(sm.inc(v, "V"))
                    v = vector.tensor_scalar(out=dst_ap, in0=src_ap,
                                             scalar1=stats[:, so:so + 1],
                                             scalar2=stats[:, so + 3:so + 4],
                                             op0=ALU.subtract, op1=ALU.mult)
                    sm.inc(v, "V")
                sec("vector", _)
                return sm.tok("V")

            def emit_tile_transposes(src_buf, dst_positions, pre_t, pre_v):
                def _(tensor, w):
                    for tk in pre_t:
                        w(tk)
                    w(st["br"][0])
                    w(st["br"][1])
                    t = None
                    for p in range(PG):
                        bank = PS[p // 4]
                        t = tensor.transpose(
                            bank[:, (p % 4) * 128:(p % 4) * 128 + 128],
                            src_buf[:, p * 128:(p + 1) * 128], ident)
                    sm.inc(t, "T")
                sec("tensor", _)
                ttok = sm.tok("T")
                st["bw"][0] = st["bw"][1] = ttok

                def _(vector, w):
                    w(ttok)
                    for tk in pre_v:
                        w(tk)
                    v = None
                    for p in range(PG):
                        bank = PS[p // 4]
                        v = vector.tensor_copy(
                            out=dst_positions[p],
                            in_=bank[:, (p % 4) * 128:(p % 4) * 128 + 128])
                    sm.inc(v, "V")
                sec("vector", _)
                vtok = sm.tok("V")
                st["br"][0] = st["br"][1] = vtok
                return ttok, vtok

            wq_state = {"p": 0}

            def wq_load(j):
                par = wq_state["p"]
                wq_state["p"] ^= 1
                buf = wQ if par == 0 else wQ2
                sname = "LQ" if par == 0 else "LR"

                def _(sync, w, _par=par, _buf=buf, _sn=sname, _j=j):
                    w(st["r_wQ"][_par])
                    sm.inc(sync.dma_start(
                        out=_buf[:, 0:4096].rearrange("p (g f) -> p g f",
                                                      f=512),
                        in_=wqkv_d[:, _j * 512:(_j + 1) * 512]
                        .rearrange("(g p) f -> p g f", p=128)), _sn, 16)
                sec("sync", _)
                return sm.tok(sname), buf, par

            def k_prod(j, src_buf, src_tok_key, tok_off, n_tok, slot0,
                       par=None):
                t_wq, wqb, wqp = wq_load(j)
                nsl = n_tok // 128
                for cb in range(4):
                    page = (j - 2) * 4 + cb
                    bank = 6 + cb % 2

                    def _(tensor, w, _cb=cb, _bank=bank, _twq=t_wq, _wq=wqb):
                        w(_twq)
                        w(st["br"][_bank])
                        w(st[src_tok_key])
                        t = None
                        for p in range(PG):
                            t = tensor.matmul(
                                PS[_bank][:, 0:n_tok],
                                _wq[:, p * 512 + _cb * 128:
                                    p * 512 + _cb * 128 + 128],
                                src_buf[:, p * 512 + tok_off:
                                        p * 512 + tok_off + n_tok],
                                start=(p == 0), stop=(p == PG - 1))
                        sm.inc(t, "T")
                    sec("tensor", _)
                    st["bw"][bank] = sm.tok("T")
                    st["r_wQ"][wqp] = sm.tok("T")
                    if par is not None:
                        st["r_xnTp"][par] = sm.tok("T")
                    else:
                        st["r_qTs"] = sm.tok("T")

                    def _(vector, w, _page=page, _bank=bank):
                        w(st["bw"][_bank])
                        for sl in range(slot0, slot0 + nsl):
                            w(st["r_kv"][sl])
                        v = vector.tensor_scalar(
                            out=kTr[:, _page * 512 + slot0 * 128:
                                    _page * 512 + slot0 * 128 + n_tok],
                            in0=PS[_bank][:, 0:n_tok],
                            scalar1=bqt[:, 8 + _page:9 + _page], scalar2=None,
                            op0=ALU.add)
                        sm.inc(v, "V")
                    sec("vector", _)
                    st["br"][bank] = sm.tok("V")
                for sl in range(slot0, slot0 + nsl):
                    st["t_kv"][sl] = sm.tok("V")

            def v_prod(j, src_buf, src_tok_key, tok_off, n_tok, slot0,
                       par=None):
                t_wq, wqb, wqp = wq_load(j)
                for i in range(n_tok // 128):
                    sl = slot0 + i
                    bank = 6 + i % 2

                    def _(tensor, w, _i=i, _bank=bank, _twq=t_wq, _wq=wqb):
                        w(_twq)
                        w(st["br"][_bank])
                        w(st[src_tok_key])
                        w(ld_all)
                        col = tok_off + _i * 128
                        for p in range(PG):
                            tensor.matmul(
                                PS[_bank][:, 0:512],
                                src_buf[:, p * 512 + col: p * 512 + col + 128],
                                _wq[:, p * 512: p * 512 + 512],
                                start=(p == 0), stop=False)
                        t = tensor.matmul(
                            PS[_bank][:, 0:512], ones1,
                            bqv[0:1, (j - 4) * 512:(j - 4) * 512 + 512],
                            start=False, stop=True)
                        sm.inc(t, "T")
                    sec("tensor", _)
                    st["bw"][bank] = sm.tok("T")
                    st["r_wQ"][wqp] = sm.tok("T")
                    if par is not None:
                        st["r_xnTp"][par] = sm.tok("T")
                    else:
                        st["r_qTs"] = sm.tok("T")

                    def _(vector, w, _sl=sl, _bank=bank, _j=j):
                        w(st["bw"][_bank])
                        w(st["r_kv"][_sl])
                        v = vector.tensor_copy(
                            out=vrb[:, _sl * 1024 + (_j - 4) * 512:
                                    _sl * 1024 + (_j - 4) * 512 + 512],
                            in_=PS[_bank][:, 0:512])
                        sm.inc(v, "V")
                    sec("vector", _)
                    st["br"][bank] = sm.tok("V")
                    st["t_kv"][sl] = sm.tok("V")

            for ht in range(2):
                lt = emit_ln(acc[:, ht * 1024:(ht + 1) * 1024], atk[:, :],
                             lnscr[:, :],
                             pre=[ld_all, st["r_atk"]])
                dsts = [qTs[:, p * 512 + ht * 128: p * 512 + ht * 128 + 128]
                        for p in range(PG)]
                tt, vt = emit_tile_transposes(atk, dsts, pre_t=[lt],
                                              pre_v=[st["r_qTs"]])
                st["r_atk"] = tt
                st["t_qTs"] = vt

            k_prod(2, qTs, "t_qTs", 0, 256, 0)
            k_prod(3, qTs, "t_qTs", 0, 256, 0)
            v_prod(4, qTs, "t_qTs", 0, 256, 0)
            v_prod(5, qTs, "t_qTs", 0, 256, 0)

            def _(vector, w):
                w(st["t_kv"][0])
                w(st["t_kv"][1])
                w(ld_all)
                kh = kTr[:, 0:4096].rearrange("p (g f) -> p g f",
                                              f=512)[:, :, 0:256]
                v = vector.tensor_scalar(out=kh, in0=kh, scalar1=hv[:, 0:1],
                                         scalar2=None, op0=ALU.mult)
                w(sm.inc(v, "V"))
                v = vector.tensor_scalar(out=vrb[:, 0:2048],
                                         in0=vrb[:, 0:2048],
                                         scalar1=hv[:, 0:1], scalar2=None,
                                         op0=ALU.mult)
                sm.inc(v, "V")
            sec("vector", _)
            st["t_kv"][0] = st["t_kv"][1] = sm.tok("V")
            halo_v_done = sm.tok("V")

            ls_cnt = 0
            st["t_xnTp"] = [None, None]
            st["r_xnTp"] = [None, None]

            def emit_front(_s):
                nonlocal ls_cnt
                _ax = [acc, acc2][_s % 2]
                _xt = [xnT, xnT2][_s % 2]

                def _(sync, w):
                    w(("ST" if _s % 2 == 0 else "SU", 16 * (_s // 2)))
                    if _s == 0:
                        w(halo_v_done)
                        w(st["r_qTs"])
                    sm.inc(sync.dma_start(
                        out=_ax[:, 0:4096].rearrange("p (t f) -> p t f", f=D),
                        in_=x_d[HALO + _s * 512: HALO + (_s + 1) * 512, :]
                        .rearrange("(t p) f -> p t f", p=128)),
                        "LX" if _s % 2 == 0 else "LY", 16)
                sec("sync", _)
                st["t_accld"] = sm.tok("LX" if _s % 2 == 0 else "LY")

                for t in range(4):
                    ab = [atk, atkB][t % 2]
                    key = "r_atk" if t % 2 == 0 else "r_atkB"
                    lsn = "LS" if t % 2 == 0 else "LT"
                    lt = emit_ln(_ax[:, t * 1024:(t + 1) * 1024], ab[:, :],
                                 lnscr[:, :],
                                 pre=[st["t_accld"], st.get(key),
                                      sm.tok(lsn)])
                    def _(scalar, w, _t=t, _lt=lt, _ab=ab, _ls=lsn):
                        w(_lt)
                        sm.inc(scalar.dma_start(
                            out=xns_d[(_s * 4 + _t) * 128:
                                      (_s * 4 + _t) * 128 + 128, :],
                            in_=_ab[:, :]), _ls, 16)
                    sec("scalar", _)
                    ls_cnt += 16
                    dsts = [_xt[:, p * 512 + t * 128: p * 512 + t * 128 + 128]
                            for p in range(PG)]
                    tt, vt = emit_tile_transposes(
                        ab, dsts, pre_t=[lt],
                        pre_v=[st["r_xnTp"][_s % 2]])
                    st[key] = tt
                st["t_xnTp"][_s % 2] = sm.tok("V")

            emit_front(0)
            for s in range(NSLC):
                g0 = s * 4
                accx = [acc, acc2][s % 2]
                xnT_s = [xnT, xnT2][s % 2]
                st["t_xnT"] = st["t_xnTp"][s % 2]

                for j in (0, 1):
                    t_wq, wqb, wqp = wq_load(j)
                    for cb in range(4):
                        page = j * 4 + cb
                        bank = 6 + cb % 2

                        def _(tensor, w, _cb=cb, _bank=bank, _twq=t_wq,
                              _wq=wqb):
                            w(_twq)
                            w(st["br"][_bank])
                            w(st["t_xnT"])
                            t = None
                            for p in range(PG):
                                t = tensor.matmul(
                                    PS[_bank][:, 0:512],
                                    _wq[:, p * 512 + _cb * 128:
                                        p * 512 + _cb * 128 + 128],
                                    xnT_s[:, p * 512: p * 512 + 512],
                                    start=(p == 0), stop=(p == PG - 1))
                            sm.inc(t, "T")
                        sec("tensor", _)
                        st["bw"][bank] = sm.tok("T")
                        st["r_wQ"][wqp] = sm.tok("T")
                        st["r_xnTp"][s % 2] = sm.tok("T")

                        def _(vector, w, _page=page, _bank=bank):
                            w(st["bw"][_bank])
                            w(st["r_qTs"])
                            v = vector.tensor_scalar(
                                out=qTs[:, _page * 512: _page * 512 + 512],
                                in0=PS[_bank][:, 0:512],
                                scalar1=bqt[:, _page:_page + 1], scalar2=None,
                                op0=ALU.add)
                            sm.inc(v, "V")
                        sec("vector", _)
                        st["br"][bank] = sm.tok("V")
                st["t_qTs"] = sm.tok("V")

                def _(tensor, w):
                    w(st["br"][2])
                    w(st["t_xnT"])
                    w(gwT_rdy)
                    t = None
                    for t4 in range(4):
                        for p in range(PG):
                            t = tensor.matmul(
                                PS[2][:, t4 * 8: t4 * 8 + 8],
                                xnT_s[:, p * 512 + t4 * 128:
                                    p * 512 + t4 * 128 + 128],
                                gwT[:, p * 8:(p + 1) * 8],
                                start=(p == 0), stop=(p == PG - 1))
                    sm.inc(t, "T")
                sec("tensor", _)
                st["bw"][2] = sm.tok("T")
                st["r_xnTp"][s % 2] = sm.tok("T")

                prev_gate_v = sm.tok("V")

                def _(scalar, w, _pg=prev_gate_v):
                    w(st["bw"][2])
                    w(_pg)
                    a = scalar.activation(out=gsc[:, 0:32], in_=PS[2][:, 0:32],
                                          func=AF.Sigmoid)
                    sm.inc(a, "A")
                sec("scalar", _)
                a_g = sm.tok("A")
                st["br"][2] = a_g

                def _(vector, w, _ag=a_g, _s=s):
                    w(_ag)
                    for t4 in range(4):
                        c = t4 * 8
                        gcol = (_s * 4 + t4) * 8
                        v = vector.tensor_tensor(out=gbi[:, c:c + 8],
                                                 in0=gsc[:, c:c + 8],
                                                 in1=ebb[:, 0:8], op=ALU.add)
                        w(sm.inc(v, "V"))
                        v = vector.max(out=gt8[:, c:c + 8], in_=gbi[:, c:c + 8])
                        w(sm.inc(v, "V"))
                        v = vector.tensor_scalar(out=gmf[:, gcol:gcol + 8],
                                                 in0=gbi[:, c:c + 8],
                                                 scalar1=gt8[:, c + 1:c + 2],
                                                 scalar2=None, op0=ALU.is_ge)
                        w(sm.inc(v, "V"))
                        v = vector.tensor_tensor(out=gbi[:, c:c + 8],
                                                 in0=gmf[:, gcol:gcol + 8],
                                                 in1=gsc[:, c:c + 8],
                                                 op=ALU.mult)
                        w(sm.inc(v, "V"))
                        v = vector.tensor_tensor(out=g2f[:, gcol:gcol + 8],
                                                 in0=gbi[:, c:c + 8],
                                                 in1=gbi[:, c:c + 8],
                                                 op=ALU.mult)
                        sm.inc(v, "V")
                sec("vector", _)

                if s + 1 < NSLC:
                    emit_front(s + 1)

                for h in range(2):
                    gA = g0 + 2 * h
                    slot0 = (gA + 2) % 4
                    k_prod(2, xnT_s, "t_xnT", 2 * h * 128, 256, slot0, par=s % 2)
                    k_prod(3, xnT_s, "t_xnT", 2 * h * 128, 256, slot0, par=s % 2)
                    v_prod(4, xnT_s, "t_xnT", 2 * h * 128, 256, slot0, par=s % 2)
                    v_prod(5, xnT_s, "t_xnT", 2 * h * 128, 256, slot0, par=s % 2)

                    for g in (gA, gA + 1):
                        tloc = g - g0
                        slots = [((kt + 2) % 4, kt - g) for kt in
                                 (g - 2, g - 1, g)]

                        def _(tensor, w, _tloc=tloc, _slots=slots):
                            w(st["br"][2])
                            w(st["t_qTs"])
                            for sl, _d in _slots:
                                w(st["t_kv"][sl])
                            t = None
                            for si, (sl, _d) in enumerate(_slots):
                                for p in range(PG):
                                    t = tensor.matmul(
                                        PS[2][:, si * 128: si * 128 + 128],
                                        kTr[:, p * 512 + sl * 128:
                                            p * 512 + sl * 128 + 128],
                                        qTs[:, p * 512 + _tloc * 128:
                                            p * 512 + _tloc * 128 + 128],
                                        start=(p == 0), stop=(p == PG - 1))
                            sm.inc(t, "T")
                        sec("tensor", _)
                        st["bw"][2] = sm.tok("T")
                        st["r_qTs"] = sm.tok("T")
                        for sl, _d in slots:
                            st["r_kv"][sl] = sm.tok("T")

                        def _(vector, w, _slots=slots):
                            w(st["bw"][2])
                            w(st["r_ST"])
                            w(ld_all)
                            v = None
                            for si, (sl, delta) in enumerate(_slots):
                                src = PS[2][:, si * 128: si * 128 + 128]
                                dst = STb[:, si * 128: si * 128 + 128]
                                if delta == -1:
                                    v = vector.tensor_scalar(
                                        out=dst, in0=src, scalar1=SCL,
                                        scalar2=None, op0=ALU.mult)
                                else:
                                    msk = mask0 if delta == 0 else mask2
                                    v = vector.scalar_tensor_tensor(
                                        out=dst, in0=src, scalar=SCL, in1=msk,
                                        op0=ALU.mult, op1=ALU.mult)
                            sm.inc(v, "V")
                        sec("vector", _)
                        st["br"][2] = sm.tok("V")
                        st["t_ST"] = sm.tok("V")

                        def _(tensor, w, _slots=slots):
                            w(st["t_ST"])
                            w(st["br"][4])
                            w(st["br"][5])
                            for sl, _d in _slots:
                                w(st["t_kv"][sl])
                            t = None
                            for dp in range(PG):
                                bank = 4 + dp // 4
                                for si, (sl, _d) in enumerate(_slots):
                                    t = tensor.matmul(
                                        PS[bank][:, (dp % 4) * 128:
                                                 (dp % 4) * 128 + 128],
                                        vrb[:, sl * 1024 + dp * 128:
                                            sl * 1024 + dp * 128 + 128],
                                        STb[:, si * 128: si * 128 + 128],
                                        start=(si == 0), stop=(si == 2))
                            sm.inc(t, "T")
                        sec("tensor", _)
                        st["bw"][4] = st["bw"][5] = sm.tok("T")
                        st["r_ST"] = sm.tok("T")
                        for sl, _d in slots:
                            st["r_kv"][sl] = sm.tok("T")

                        def _(vector, w):
                            w(st["bw"][4])
                            w(st["r_aT"])
                            w(st["t_aT"])
                            vector.tensor_copy(out=aTb[:, 0:512],
                                               in_=PS[4][:, 0:512])
                            v = vector.tensor_copy(out=aTb[:, 512:1024],
                                                   in_=PS[5][:, 0:512])
                            sm.inc(v, "V")
                        sec("vector", _)
                        st["br"][4] = st["br"][5] = sm.tok("V")
                        st["t_aT"] = sm.tok("V")

                        dsts = [atkC[:, p * 128:(p + 1) * 128]
                                for p in range(PG)]
                        tt, vt = emit_tile_transposes(
                            aTb, dsts, pre_t=[st["t_aT"]],
                            pre_v=[st["r_atkC"]])
                        st["r_aT"] = tt
                        st["t_atk"] = vt

                        lt = emit_ln(atkC[:, :], atkC[:, :], aTb[:, :],
                                     pre=[st["t_atk"], st["r_aT"],
                                          st["t_aT"]], so=4, akey="ln_a2")

                        def _(gpsimd, w, _tloc=tloc, _lt=lt, _ax=accx):
                            w(_lt)
                            w(ld_all)
                            v = gpsimd.tensor_tensor(out=aTb[:, :],
                                                     in0=atkC[:, :],
                                                     in1=lnwb[:, :],
                                                     op=ALU.mult)
                            w(sm.inc(v, "G"))
                            v = gpsimd.tensor_tensor(out=aTb[:, :],
                                                     in0=aTb[:, :],
                                                     in1=lnbb[:, :],
                                                     op=ALU.add)
                            w(sm.inc(v, "G"))
                            if "att" in dbg_skip:
                                v = gpsimd.tensor_scalar(
                                    out=aTb[:, :], in0=aTb[:, :], scalar1=0.0,
                                    scalar2=None, op0=ALU.mult)
                                w(sm.inc(v, "G"))
                            w(st["t_accld"])
                            w(st["acc_wv"])
                            v = gpsimd.tensor_tensor(
                                out=_ax[:, _tloc * 1024:(_tloc + 1) * 1024],
                                in0=_ax[:, _tloc * 1024:(_tloc + 1) * 1024],
                                in1=aTb[:, :], op=ALU.add)
                            sm.inc(v, "G")
                        sec("gpsimd", _)
                        st["r_atkC"] = sm.tok("G")
                        st["acc_wv"] = sm.tok("G")
                        st["t_aT"] = sm.tok("G")
                        st["r_aT"] = sm.tok("G")

                def _(scalar, w, _s=s, _ax=accx):
                    w(st["acc_wv"])
                    sm.inc(scalar.dma_start(
                        out=out_d[_s * 512:(_s + 1) * 512, :]
                        .rearrange("(t p) f -> p t f", p=128),
                        in_=_ax[:, 0:4096].rearrange("p (t f) -> p t f", f=D)),
                        "ST" if _s % 2 == 0 else "SU", 16)
                sec("scalar", _)

            bar = [sm.tok("T"), sm.tok("V"), sm.tok("A"), sm.tok("G"),
                   ("ST", 32), ("SU", 32), sm.tok("LS"), sm.tok("LT"),
                   sm.tok("LQ"), sm.tok("LR"), sm.tok("LX"),
                   sm.tok("LY")]

            for t in range(16):
                def _(tensor, w, _t=t):
                    for tk in bar:
                        w(tk)
                    w(st["br"][2])
                    w(base_tok)
                    tensor.matmul(PS[2][:, 0:8], mask0,
                                  gmf[:, _t * 8: _t * 8 + 8],
                                  start=True, stop=False)
                    tensor.matmul(PS[2][:, 0:8], ones1, base[0:1, :],
                                  start=False, stop=True)
                    t2 = tensor.matmul(PS[2][0:1, 8:16],
                                       cst[:, C_ONE:C_ONE + 1],
                                       gmf[:, _t * 8: _t * 8 + 8],
                                       start=True, stop=True)
                    sm.inc(t2, "T")
                sec("tensor", _)
                st["bw"][2] = sm.tok("T")

                def _(vector, w, _t=t):
                    w(st["bw"][2])
                    vector.tensor_copy(out=cmf[:, _t * 8: _t * 8 + 8],
                                       in_=PS[2][:, 0:8])
                    v = vector.tensor_tensor(out=base[:, :], in0=base[:, :],
                                             in1=PS[2][0:1, 8:16],
                                             op=ALU.add)
                    sm.inc(v, "V")
                sec("vector", _)
                st["br"][2] = sm.tok("V")
                base_tok = sm.tok("V")

            def _(vector, w):
                w(base_tok)
                v = vector.tensor_tensor(out=cmf[:, :], in0=cmf[:, :],
                                         in1=gmf[:, :], op=ALU.mult)
                sm.inc(v, "V")
            sec("vector", _)
            cmsk_tok = sm.tok("V")

            r_E = [None, None]
            r_cg = None
            t_rowv = None
            r_row = None
            for e in range(NE):
                def _(vector, w, _e=e, _rcg=r_cg):
                    w(cmsk_tok)
                    w(_rcg)
                    cgv = cg[:, 0:32].rearrange("p (t c) -> p t c", c=2)
                    vector.tensor_copy(
                        out=cgv[:, :, 0:1],
                        in_=cst[:, C_TOK:C_TOK + 16]
                        .rearrange("p (t c) -> p t c", c=1))
                    v = vector.tensor_copy(
                        out=cgv[:, :, 1:2],
                        in_=g2f[:, 0:128].rearrange("p (t c) -> p t c",
                                                    c=8)[:, :, _e:_e + 1])
                    sm.inc(v, "V")
                sec("vector", _)
                cg_rdy = sm.tok("V")

                for t in range(16):
                    ebuf = t % 2

                    def _(vector, w, _e=e, _t=t, _eb=ebuf):
                        w(cmsk_tok)
                        w(r_E[_eb])
                        v = vector.tensor_scalar(
                            out=Epr[:, _eb * 768:_eb * 768 + 768],
                            in0=cst[:, C_IOTA:C_IOTA + 768],
                            scalar1=cmf[:, _t * 8 + _e: _t * 8 + _e + 1],
                            scalar2=None, op0=ALU.is_equal)
                        sm.inc(v, "V")
                    sec("vector", _)
                    e_tok = sm.tok("V")

                    def _(tensor, w, _t=t, _eb=ebuf, _et=e_tok,
                          _cg=cg_rdy):
                        w(_et)
                        w(_cg)
                        if _t == 0:
                            w(st["br"][2])
                            w(st["br"][3])
                        tensor.matmul(
                            PS[2][0:2, 0:512], cg[:, 2 * _t: 2 * _t + 2],
                            Epr[:, _eb * 768: _eb * 768 + 512],
                            start=(_t == 0), stop=(_t == 15))
                        t2 = tensor.matmul(
                            PS[3][0:2, 0:256], cg[:, 2 * _t: 2 * _t + 2],
                            Epr[:, _eb * 768 + 512: _eb * 768 + 768],
                            start=(_t == 0), stop=(_t == 15))
                        sm.inc(t2, "T")
                    sec("tensor", _)
                    r_E[ebuf] = sm.tok("T")
                r_cg = sm.tok("T")
                st["bw"][2] = st["bw"][3] = sm.tok("T")

                def _(vector, w, _rr=r_row):
                    w(st["bw"][2])
                    w(_rr)
                    vector.tensor_copy(out=idxrow[0:2, 0:512],
                                       in_=PS[2][0:2, 0:512])
                    v = vector.tensor_copy(out=idxrow[0:2, 512:768],
                                           in_=PS[3][0:2, 0:256])
                    sm.inc(v, "V")
                sec("vector", _)
                st["br"][2] = st["br"][3] = sm.tok("V")
                t_rowv = sm.tok("V")

                def _(tensor, w, _tv=t_rowv):
                    w(_tv)
                    w(st["br"][4])
                    t2 = None
                    for ch in range(NCH):
                        t2 = tensor.transpose(
                            PS[4][:, ch * 2: ch * 2 + 2],
                            idxrow[0:2, ch * 128: ch * 128 + 128],
                            ident[0:2, 0:2])
                    sm.inc(t2, "T")
                sec("tensor", _)
                st["bw"][4] = sm.tok("T")
                r_row = sm.tok("T")

                def _(vector, w, _e=e):
                    w(st["bw"][4])
                    p4 = PS[4][:, 0:12].rearrange("p (g f) -> p g f", f=2)
                    vector.tensor_copy(
                        out=idxc[:, _e * NCH:(_e + 1) * NCH]
                        .rearrange("p (g f) -> p g f", f=1),
                        in_=p4[:, :, 0:1])
                    v = vector.tensor_copy(
                        out=g2gc[:, _e * NCH:(_e + 1) * NCH]
                        .rearrange("p (g f) -> p g f", f=1),
                        in_=p4[:, :, 1:2])
                    sm.inc(v, "V")
                sec("vector", _)
                st["br"][4] = sm.tok("V")
            idx_rdy = sm.tok("V")

            r_xgs = None
            t_xgT = None
            r_xgT = None
            t_yg = [None] * NCH
            r_yg = None
            lg_cnt = 0
            sc_cnt = 0

            def emit_gather(_e):
                nonlocal lg_cnt
                def _(gpsimd, w, _rx=r_xgs):
                    for tk in bar:
                        w(tk)
                    w(idx_rdy)
                    w(_rx)
                    for ti in range(NCH):
                        sm.inc(gpsimd.indirect_dma_start(
                            out=xgs[:, ti * 1024: ti * 1024 + 1024],
                            out_offset=None,
                            in_=xns_d[:, :],
                            in_offset=IndirectOffsetOnAxis(
                                ap=idxc[:, _e * NCH + ti: _e * NCH + ti + 1],
                                axis=0)), "LG", 16)
                sec("gpsimd", _)
                lg_cnt += 16 * NCH
                return ("LG", lg_cnt)

            def emit_scatter(_e, _tyg, _sc):
                def _(gpsimd, w):
                    for tk in _tyg:
                        w(tk)
                    for ti in range(NCH):
                        gpsimd.wait_ge(sm.sem["SC"], _sc + 16 * ti)
                        sm.inc(gpsimd.indirect_dma_start(
                            out=out_d[:, :],
                            out_offset=IndirectOffsetOnAxis(
                                ap=idxc[:, _e * NCH + ti: _e * NCH + ti + 1],
                                axis=0),
                            in_=yg[:, ti * 1024: ti * 1024 + 1024],
                            in_offset=None,
                            compute_op=ALU.add), "SC", 16)
                sec("gpsimd", _)

            def emit_xg_transposes(_tx):
                nonlocal r_xgs, r_xgT
                for p in range(PG):
                    for half in range(2):
                        n_ti = 4 if half == 0 else 2
                        bank = (p * 2 + half) % 4

                        def _(tensor, w, _p=p, _h=half, _n=n_ti, _bank=bank,
                              _tx2=_tx):
                            w(_tx2)
                            w(st["br"][_bank])
                            w(r_xgT)
                            t = None
                            for i in range(_n):
                                ti = _h * 4 + i
                                t = tensor.transpose(
                                    PS[_bank][:, i * 128: i * 128 + 128],
                                    xgs[:, ti * 1024 + _p * 128:
                                        ti * 1024 + _p * 128 + 128],
                                    ident)
                            sm.inc(t, "T")
                        sec("tensor", _)
                        st["bw"][bank] = sm.tok("T")
                        r_xgs = sm.tok("T")

                        def _(vector, w, _p=p, _h=half, _n=n_ti, _bank=bank):
                            w(st["bw"][_bank])
                            off = _p * CAP + _h * 512
                            v = vector.tensor_scalar(
                                out=xgTh[:, off:off + _n * 128],
                                in0=PS[_bank][:, 0:_n * 128],
                                scalar1=1.0, scalar2=None, op0=ALU.mult)
                            w(sm.inc(v, "V"))
                            v = vector.scalar_tensor_tensor(
                                out=xgTl[:, off:off + _n * 128],
                                in0=PS[_bank][:, 0:_n * 128],
                                scalar=1.0, in1=xgTh[:, off:off + _n * 128],
                                op0=ALU.mult, op1=ALU.subtract)
                            sm.inc(v, "V")
                        sec("vector", _)
                        st["br"][bank] = sm.tok("V")
                return (sm.tok("V"), sm.tok("V"))

            t_xgs = emit_gather(0)
            xg_done = emit_xg_transposes(t_xgs)
            def emit_wstage(pg):
                _e, _ph = pg // 8, pg % 8
                _up = _ph < 4
                _q = _ph % 4
                _wi = pg % 2

                def _(sync, w, _e=_e, _q=_q, _up=_up, _wi=_wi):
                    for tk in bar:
                        w(tk)
                    w(st["r_stage"][_wi])
                    ln = "L0" if _wi == 0 else "L1"
                    if _up:
                        src = wut_d[_e, :, _q * 512:(_q + 1) * 512]
                    else:
                        src = wdt_d[_e, :, _q * 256:(_q + 1) * 256]
                    sm.inc(sync.dma_start(
                        out=wst[_wi][:, 0:4096].rearrange(
                            "p (c f) -> p c f", f=512 if _up else 256),
                        in_=src.rearrange("(g p) f -> p g f", p=128)),
                        ln, 16)
                sec("sync", _)
                t_dma = sm.tok("L0" if _wi == 0 else "L1")

                def _(scalar, w, _wi=_wi, _td=t_dma, _rw=st["r_wh"][_wi]):
                    w(_td)
                    w(_rw)
                    a = scalar.activation(
                        out=wBh[_wi][:, :], in_=wst[_wi][:, :],
                        func=AF.Identity)
                    sm.inc(a, "A")
                sec("scalar", _)
                t_whi = sm.tok("A")

                def _(gpsimd, w, _wi=_wi, _td=t_dma, _th=t_whi,
                      _rw=st["r_wh"][_wi]):
                    w(_td)
                    w(_th)
                    w(_rw)
                    g = gpsimd.tensor_tensor(
                        out=wBl[_wi][:, :], in0=wst[_wi][:, :],
                        in1=wBh[_wi][:, :], op=ALU.subtract)
                    sm.inc(g, "G")
                sec("gpsimd", _)
                st["t_wA"][_wi] = t_whi
                st["t_wlo"][_wi] = sm.tok("G")
                st["r_stage"][_wi] = sm.tok("G")

            emit_wstage(0)
            pend_scatter = None
            for e in range(NE):

                for ph in range(8):
                    up = ph < 4
                    q = ph % 4
                    wi = (e * 8 + ph) % 2

                    if ph == 1 and pend_scatter is not None:
                        emit_scatter(*pend_scatter)
                        pend_scatter = None
                    if ph == 5 and e + 1 < NE:
                        xg_done_next = emit_xg_transposes(emit_gather(e + 1))
                    if e * 8 + ph + 1 < NE * 8:
                        emit_wstage(e * 8 + ph + 1)

                    if up:
                        for m in range(4):
                            hp = q * 4 + m
                            for c in range(2):
                                ntk = 512 if c == 0 else 256
                                bank = 4 + (m * 2 + c) % 4

                                def _(tensor, w, _m=m, _c=c, _n=ntk,
                                      _bank=bank, _wi=wi):
                                    w(st["t_wA"][_wi])
                                    w(xg_done[0])
                                    w(xg_done[1])
                                    w(st["br"][_bank])
                                    wh = wBh[_wi][:, :]
                                    xh = xgTh[:, :]
                                    xl = xgTl[:, :]
                                    t = None
                                    first = True
                                    for p in range(PG):
                                        wcol = p * 512 + _m * 128
                                        xcol = p * CAP + _c * 512
                                        for rh in (xh, xl):
                                            t = tensor.matmul(
                                                PS[_bank][:, 0:_n],
                                                wh[:, wcol:wcol + 128],
                                                rh[:, xcol:xcol + _n],
                                                start=first, stop=False)
                                            first = False
                                    sm.inc(t, "T")
                                sec("tensor", _)

                                def _(tensor, w, _m=m, _c=c, _n=ntk,
                                      _bank=bank, _wi=wi):
                                    w(st["t_wlo"][_wi])
                                    wl = wBl[_wi][:, :]
                                    xh = xgTh[:, :]
                                    t = None
                                    for p in range(PG):
                                        wcol = p * 512 + _m * 128
                                        xcol = p * CAP + _c * 512
                                        t = tensor.matmul(
                                            PS[_bank][:, 0:_n],
                                            wl[:, wcol:wcol + 128],
                                            xh[:, xcol:xcol + _n],
                                            start=False, stop=(p == PG - 1))
                                    sm.inc(t, "T")
                                sec("tensor", _)
                                st["bw"][bank] = sm.tok("T")
                                st["r_wA"][wi] = sm.tok("T")
                                st["r_wh"][wi] = sm.tok("T")
                                r_xgT = sm.tok("T")

                                def _(vector, w, _hp=hp, _c=c, _n=ntk,
                                      _bank=bank):
                                    w(st["bw"][_bank])
                                    w(st["r_Hb"])
                                    off = _hp * CAP + _c * 512
                                    v = vector.tensor_scalar(
                                        out=rsq[:, 0:_n],
                                        in0=PS[_bank][:, 0:_n],
                                        scalar1=0.0, scalar2=None,
                                        op0=ALU.max)
                                    w(sm.inc(v, "V"))
                                    v = vector.tensor_tensor(
                                        out=PS[_bank][:, 0:_n],
                                        in0=rsq[:, 0:_n],
                                        in1=rsq[:, 0:_n],
                                        op=ALU.mult)
                                    w(sm.inc(v, "V"))
                                    v = vector.tensor_scalar(
                                        out=Hh[:, off:off + _n],
                                        in0=PS[_bank][:, 0:_n],
                                        scalar1=1.0, scalar2=None,
                                        op0=ALU.mult)
                                    w(sm.inc(v, "V"))
                                    v = vector.scalar_tensor_tensor(
                                        out=Hl[:, off:off + _n],
                                        in0=PS[_bank][:, 0:_n],
                                        scalar=1.0, in1=Hh[:, off:off + _n],
                                        op0=ALU.mult, op1=ALU.subtract)
                                    sm.inc(v, "V")
                                sec("vector", _)
                                st["br"][bank] = sm.tok("V")
                        st["t_Hb"] = sm.tok("V")
                    else:
                        for ti in range(NCH):
                            bank = 4 + ti % 4

                            def _(tensor, w, _ti=ti, _bank=bank, _wi=wi):
                                w(st["t_wA"][_wi])
                                w(st["t_Hb"])
                                w(st["br"][_bank])
                                hl = Hl[:, :]
                                wh = wBh[_wi][:, :]
                                t = None
                                first = True
                                for hp in range(16):
                                    hcol = hp * CAP + _ti * 128
                                    wcol = hp * 256
                                    t = tensor.matmul(
                                        PS[_bank][:, 0:256],
                                        hl[:, hcol:hcol + 128],
                                        wh[:, wcol:wcol + 256],
                                        start=first, stop=False)
                                    first = False
                                sm.inc(t, "T")
                            sec("tensor", _)

                            def _(tensor, w, _ti=ti, _bank=bank, _wi=wi):
                                w(st["t_wlo"][_wi])
                                hh = Hh[:, :]
                                wh = wBh[_wi][:, :]
                                wl = wBl[_wi][:, :]
                                t = None
                                for hp in range(16):
                                    hcol = hp * CAP + _ti * 128
                                    wcol = hp * 256
                                    for rh in (wh, wl):
                                        t = tensor.matmul(
                                            PS[_bank][:, 0:256],
                                            hh[:, hcol:hcol + 128],
                                            rh[:, wcol:wcol + 256],
                                            start=False,
                                            stop=(hp == 15 and rh is wl))
                                sm.inc(t, "T")
                            sec("tensor", _)
                            st["bw"][bank] = sm.tok("T")
                            st["r_wA"][wi] = sm.tok("T")
                            st["r_wh"][wi] = sm.tok("T")
                            st["r_Hb"] = sm.tok("T")

                            def _(vector, w, _ti=ti, _q=q, _e=e, _bank=bank):
                                w(st["bw"][_bank])
                                w(idx_rdy)
                                w(r_yg)
                                sc1 = (0.0 if "moe" in dbg_skip else
                                       g2gc[:, _e * NCH + _ti:
                                            _e * NCH + _ti + 1])
                                v = vector.tensor_scalar(
                                    out=yg[:, _ti * 1024 + _q * 256:
                                           _ti * 1024 + _q * 256 + 256],
                                    in0=PS[_bank][:, 0:256],
                                    scalar1=sc1, scalar2=None, op0=ALU.mult)
                                sm.inc(v, "V")
                            sec("vector", _)
                            st["br"][bank] = sm.tok("V")
                            t_yg[ti] = sm.tok("V")

                pend_scatter = (e, list(t_yg), sc_cnt)
                sc_cnt += 16 * NCH
                r_yg = ("SC", sc_cnt)
                if e + 1 < NE:
                    xg_done = xg_done_next

            if pend_scatter is not None:
                emit_scatter(*pend_scatter)

            def _(gpsimd, w):
                gpsimd.wait_ge(sm.sem["ST"], 32)
                gpsimd.wait_ge(sm.sem["SU"], 32)
                gpsimd.wait_ge(sm.sem["SC"], sc_cnt)
            sec("gpsimd", _)

    return nc


def kernel(**inputs):
    x = np.ascontiguousarray(np.asarray(inputs["L0_attn_x"], dtype=np.float32))
    wqkv = np.ascontiguousarray(np.asarray(inputs["L0_attn_W_qkv"],
                                           dtype=np.float32))
    bqkv = np.asarray(inputs["L0_attn_b_qkv"], dtype=np.float32)
    lnw = np.asarray(inputs["L0_attn_ln_w"], dtype=np.float32)
    lnb = np.asarray(inputs["L0_attn_ln_b"], dtype=np.float32)
    gatet = np.ascontiguousarray(
        np.asarray(inputs["L0_ffn_gate_w"], dtype=np.float32).T)
    wut = np.ascontiguousarray(
        np.asarray(inputs["L0_ffn_w_up"], dtype=np.float32).transpose(0, 2, 1))
    wdt = np.ascontiguousarray(
        np.asarray(inputs["L0_ffn_w_down"],
                   dtype=np.float32).transpose(0, 2, 1))
    ebias = np.asarray(inputs["L0_ffn_expert_bias"], dtype=np.float32)

    B, T, C = x.shape
    assert (B, T, C) == (2, 8192, 1024)

    bqkvt = np.ascontiguousarray(bqkv.reshape(24, 128).T)
    bqkvv = np.ascontiguousarray(bqkv[2048:].reshape(1, 1024))
    lnwb = np.ascontiguousarray(np.broadcast_to(lnw[None, :], (128, 1024)))
    lnbb = np.ascontiguousarray(np.broadcast_to(lnb[None, :], (128, 1024)))
    eb1 = np.ascontiguousarray(ebias.reshape(1, 8))

    kk = np.arange(128)[:, None]
    qq = np.arange(128)[None, :]
    cstm = np.zeros((128, C_TOT), dtype=np.float32)
    cstm[:, C_ID:C_ID + 128] = np.eye(128, dtype=np.float32)
    cstm[:, C_M0:C_M0 + 128] = (kk <= qq).astype(np.float32)
    cstm[:, C_M2:C_M2 + 128] = (kk > qq).astype(np.float32)
    cstm[:, C_ONE:C_ONE + 128] = 1.0
    cstm[:, C_IOTA:C_IOTA + CAP] = (np.arange(CAP) + 1)[None, :]
    for t in range(16):
        cstm[:, C_TOK + t] = np.arange(128) + 128 * t

    in_maps = []
    for c in range(8):
        b = c // 4
        s0 = (c % 4) * 2048
        if c % 4 == 0:
            xc = np.concatenate(
                [np.zeros((HALO, C), dtype=np.float32), x[b, 0:T_OWN]], axis=0)
            hvv = np.zeros((128, 1), dtype=np.float32)
        else:
            xc = x[b, s0 - HALO: s0 + T_OWN]
            hvv = np.ones((128, 1), dtype=np.float32)
        in_maps.append({
            "x": np.ascontiguousarray(xc),
            "wqkv": wqkv, "bqkvt": bqkvt, "bqkvv": bqkvv,
            "gatet": gatet, "eb1": eb1, "wut": wut, "wdt": wdt,
            "lnw": lnwb, "lnb": lnbb, "cst": cstm, "hv": hvv,
        })

    nc = build_program()
    res = run_bass_kernel_spmd(nc, in_maps, core_ids=list(range(8)))

    out = np.empty((2, 8192, 1024), dtype=np.float32)
    for c in range(8):
        b = c // 4
        s0 = (c % 4) * 2048
        out[b, s0:s0 + T_OWN] = res.results[c]["out"]
    return out


# version 33 (final)
# speedup vs baseline: 1.0792x; 1.0007x over previous
import contextlib
import numpy as np
import concourse.bass as bass
import concourse.mybir as mybir
from concourse.bass import IndirectOffsetOnAxis
from concourse.bass_utils import run_bass_kernel_spmd

F32 = mybir.dt.float32
BF16 = mybir.dt.bfloat16
I32 = mybir.dt.int32
ALU = mybir.AluOpType
AF = mybir.ActivationFunctionType
AX = mybir.AxisListType

D = 1024
PG = 8
HID = 2048
T_OWN = 2048
HALO = 256
NSLC = 4
NE = 8
EPS = 1e-5
SCL = 1.0 / 512.0
CAP = 768
NCH = CAP // 128

C_ID = 0
C_M0 = 128
C_M2 = 256
C_ONE = 384
C_IOTA = 512
C_TOK = 1280
C_TOT = 1296

_ENG2NAME = {"tensor": "T", "vector": "V", "scalar": "A",
             "gpsimd": "G", "sync": "SY"}


class Sems:
    def __init__(self):
        self.cnt = {}
        self.sem = {}
        self.lastw = {}

    def add(self, name, handle):
        self.sem[name] = handle
        self.cnt[name] = 0

    def inc(self, instr, name, k=1):
        instr.then_inc(self.sem[name], k)
        self.cnt[name] += k
        return (name, self.cnt[name])

    def tok(self, name):
        return (name, self.cnt[name])


def build_program(dbg_skip=()):
    nc = bass.Bass("TRN2", target_bir_lowering=False)

    x_d = nc.dram_tensor("x", [HALO + T_OWN, D], F32, kind="ExternalInput").ap()
    wqkv_d = nc.dram_tensor("wqkv", [D, 3 * D], F32, kind="ExternalInput").ap()
    bqkvt_d = nc.dram_tensor("bqkvt", [128, 24], F32, kind="ExternalInput").ap()
    bqkvv_d = nc.dram_tensor("bqkvv", [1, D], F32, kind="ExternalInput").ap()
    gatet_d = nc.dram_tensor("gatet", [D, NE], F32, kind="ExternalInput").ap()
    eb1_d = nc.dram_tensor("eb1", [1, NE], F32, kind="ExternalInput").ap()
    wut_d = nc.dram_tensor("wut", [NE, D, HID], F32, kind="ExternalInput").ap()
    wdt_d = nc.dram_tensor("wdt", [NE, HID, D], F32, kind="ExternalInput").ap()
    lnw_d = nc.dram_tensor("lnw", [128, D], F32, kind="ExternalInput").ap()
    lnb_d = nc.dram_tensor("lnb", [128, D], F32, kind="ExternalInput").ap()
    cst_d = nc.dram_tensor("cst", [128, C_TOT], F32, kind="ExternalInput").ap()
    hv_d = nc.dram_tensor("hv", [128, 1], F32, kind="ExternalInput").ap()
    out_d = nc.dram_tensor("out", [T_OWN, D], F32, kind="ExternalOutput").ap()
    xns_d = nc.dram_tensor("xnscr", [T_OWN, D], F32, kind="Internal").ap()

    sm = Sems()

    with contextlib.ExitStack() as ex:
        ra_h = ex.enter_context(nc.sbuf_tensor("rarena", [128, 20480], F32))

        def rseg(off, n):
            return ra_h[:, off:off + n]

        acc = rseg(0, 4096)
        acc2 = rseg(4096, 4096)
        atk = rseg(8192, 1024)
        atkB = rseg(9216, 1024)
        wQ = rseg(10240, 4096)
        wQ2 = rseg(14336, 4096)
        xgs = rseg(0, 6144)
        yg = rseg(6144, 6144)
        wst = [rseg(12288, 4096), rseg(16384, 4096)]

        VARENA_B = 106496
        back = ex.enter_context(
            nc.sbuf_tensor("varena", [128, VARENA_B], mybir.dt.uint8))
        base_addr = nc.lookup_mloc(back).addr

        def valloc(name, cols, off, dt=F32, rows=128):
            h = nc.virtual_sbuf_tensor(name, [rows, cols], dt)
            nc._tensor(h.name, h.shape, h.dtype, type="SB")
            mloc = nc.lookup_mloc(h)
            mloc.allocated = True
            mloc.addr = base_addr + off
            return h

        xnT = valloc("xnT", 4096, 0)
        qTs = valloc("qTs", 4096, 16384)
        kTr = valloc("kTr", 4096, 32768)
        vrb = valloc("vrb", 4096, 49152)
        STb = valloc("STb", 384, 65536)
        aTb = valloc("aTb", 1024, 67072)
        lnscr = valloc("lnscr", 1024, 71168)
        atkC = valloc("atkC", 1024, 75264)
        xnT2 = valloc("xnT2", 4096, 79360)
        Epr = valloc("Epr", 1536, 71168)
        idxrow = valloc("idxrow", 768, 95744)
        wBh = [valloc("wBh0", 4096, 0, dt=BF16),
               valloc("wBh1", 4096, 16384, dt=BF16)]
        wBl = [valloc("wBl0", 4096, 8192, dt=BF16),
               valloc("wBl1", 4096, 24576, dt=BF16)]
        xgTh = valloc("xgTh", 6144, 32768, dt=BF16)
        xgTl = valloc("xgTl", 6144, 45056, dt=BF16)
        Hh = valloc("Hh", 12288, 57344, dt=BF16)
        Hl = valloc("Hl", 12288, 81920, dt=BF16)

        def sbt(name, cols, rows=128, dt=F32):
            return ex.enter_context(nc.sbuf_tensor(name, [rows, cols], dt))

        cst = sbt("cst_s", C_TOT)
        rsq = sbt("rsq", 512)
        lnwb = sbt("lnwb", 1024)
        lnbb = sbt("lnbb", 1024)
        bqt = sbt("bqt", 24)
        gwT = sbt("gwT", 64)
        ebb = sbt("ebb", 8)
        gsc = sbt("gsc", 32)
        gbi = sbt("gbi", 32)
        gt8 = sbt("gt8", 32)
        g2f = sbt("g2f", 128)
        gmf = sbt("gmf", 128)
        cmf = sbt("cmf", 128)
        stats = sbt("stats", 32)
        hv = sbt("hv_s", 1)
        cg = sbt("cgst", 32)
        idxc = sbt("idxc", NE * NCH, dt=I32)
        g2gc = sbt("g2gc", NE * NCH)
        bqv = ex.enter_context(nc.sbuf_tensor("bqv", [1, D], F32))
        eb1 = ex.enter_context(nc.sbuf_tensor("eb1s", [1, NE], F32))
        base = ex.enter_context(nc.sbuf_tensor("base8", [1, NE], F32))

        PS = [ex.enter_context(nc.psum_tensor(f"ps{i}", [128, 512], F32))
              for i in range(8)]
        for name in ("T", "V", "A", "G", "LC", "LQ", "LR", "L0", "L1", "LX",
                     "LY", "LS", "LT", "LG", "SC", "ST", "SU"):
            sm.add(name, ex.enter_context(nc.semaphore(name)))

        ident = cst[:, C_ID:C_ID + 128]
        mask0 = cst[:, C_M0:C_M0 + 128]
        mask2 = cst[:, C_M2:C_M2 + 128]
        ones1 = cst[0:1, C_ONE:C_ONE + 128]

        st = {
            "bw": [None] * 8, "br": [None] * 8,
            "t_xnT": None, "r_xnT": None,
            "t_qTs": None, "r_qTs": None,
            "t_kv": [None] * 4, "r_kv": [None] * 4,
            "r_wQ": [None, None],
            "t_ST": None, "r_ST": None,
            "t_aT": None, "r_aT": None,
            "t_atk": None, "r_atk": None, "r_atkB": None, "r_atkC": None,
            "t_accld": None, "acc_wv": None,
            "t_wA": [None, None], "r_wA": [None, None],
            "r_wh": [None, None], "r_stage": [None, None],
            "t_wB": None, "r_wB": None, "t_wlo": [None, None],
            "t_Hb": None, "r_Hb": None,
            "ln_a": None, "ln_a2": None,
        }

        with nc.Block() as block:
            def sec(kind, fn):
                wname = _ENG2NAME[kind]

                def wrapped(eng):
                    def w(tok):
                        if tok is None:
                            return
                        name, val = tok
                        if val <= 0:
                            return
                        key = (wname, name)
                        if sm.lastw.get(key, 0) >= val:
                            return
                        eng.wait_ge(sm.sem[name], val)
                        sm.lastw[key] = val
                    fn(eng, w)
                getattr(block, kind)(wrapped)

            def _(sync, w):
                for dst, src in [
                    (cst[:], cst_d[:]), (lnwb[:], lnw_d[:]), (lnbb[:], lnb_d[:]),
                    (bqt[:], bqkvt_d[:]), (bqv[:], bqkvv_d[:]),
                    (eb1[:], eb1_d[:]), (hv[:], hv_d[:]),
                    (gwT[:, 0:64].rearrange("p (g f) -> p g f", f=NE),
                     gatet_d[:, :].rearrange("(g p) f -> p g f", p=128)),
                ]:
                    sm.inc(sync.dma_start(out=dst, in_=src), "LC", 16)
                sm.inc(sync.dma_start(
                    out=acc[:, 0:2048].rearrange("p (t f) -> p t f", f=D),
                    in_=x_d[0:HALO, :].rearrange("(t p) f -> p t f", p=128)),
                    "LC", 16)
            sec("sync", _)
            ld_all = sm.tok("LC")

            def _(vector, w):
                v = vector.memset(base[:, :], 0.0)
                sm.inc(v, "V")
            sec("vector", _)
            base_tok = sm.tok("V")

            def _(tensor, w):
                w(ld_all)
                t = tensor.matmul(PS[1][:, 0:8], ones1, eb1[0:1, :],
                                  start=True, stop=True)
                sm.inc(t, "T")
            sec("tensor", _)
            st["bw"][1] = sm.tok("T")

            def _(vector, w):
                w(st["bw"][1])
                v = vector.tensor_copy(out=ebb[:], in_=PS[1][:, 0:8])
                sm.inc(v, "V")
            sec("vector", _)
            st["br"][1] = sm.tok("V")
            gwT_rdy = ld_all

            def emit_ln(src_ap, dst_ap, scr_ap, pre, so=0, akey="ln_a"):
                b6 = 8 + (so // 4) * 12
                def _(vector, w):
                    for tk in pre:
                        w(tk)
                    w(st[akey])
                    v = vector.bn_stats(out=stats[:, b6:b6 + 6],
                                        in_=src_ap[:, 0:512])
                    w(sm.inc(v, "V"))
                    v = vector.bn_stats(out=stats[:, b6 + 6:b6 + 12],
                                        in_=src_ap[:, 512:1024])
                    w(sm.inc(v, "V"))
                    v = vector.bn_aggr(out=stats[:, so:so + 2],
                                       in_=stats[:, b6:b6 + 12])
                    w(sm.inc(v, "V"))
                    v = vector.tensor_scalar(out=stats[:, so + 1:so + 2],
                                             in0=stats[:, so + 1:so + 2],
                                             scalar1=EPS, scalar2=None,
                                             op0=ALU.add)
                    sm.inc(v, "V")
                sec("vector", _)
                vtok = sm.tok("V")

                def _(scalar, w):
                    w(vtok)
                    a = scalar.activation(out=stats[:, so + 2:so + 3],
                                          in_=stats[:, so + 1:so + 2],
                                          func=AF.Sqrt)
                    sm.inc(a, "A")
                sec("scalar", _)
                st[akey] = sm.tok("A")
                a_sqrt = sm.tok("A")

                def _(vector, w):
                    for tk in pre:
                        w(tk)
                    w(a_sqrt)
                    v = vector.reciprocal(out=stats[:, so + 3:so + 4],
                                          in_=stats[:, so + 2:so + 3])
                    w(sm.inc(v, "V"))
                    v = vector.tensor_scalar(out=dst_ap, in0=src_ap,
                                             scalar1=stats[:, so:so + 1],
                                             scalar2=stats[:, so + 3:so + 4],
                                             op0=ALU.subtract, op1=ALU.mult)
                    sm.inc(v, "V")
                sec("vector", _)
                return sm.tok("V")

            def emit_tile_transposes(src_buf, dst_positions, pre_t, pre_v):
                def _(tensor, w):
                    for tk in pre_t:
                        w(tk)
                    w(st["br"][0])
                    w(st["br"][1])
                    t = None
                    for p in range(PG):
                        bank = PS[p // 4]
                        t = tensor.transpose(
                            bank[:, (p % 4) * 128:(p % 4) * 128 + 128],
                            src_buf[:, p * 128:(p + 1) * 128], ident)
                    sm.inc(t, "T")
                sec("tensor", _)
                ttok = sm.tok("T")
                st["bw"][0] = st["bw"][1] = ttok

                def _(vector, w):
                    w(ttok)
                    for tk in pre_v:
                        w(tk)
                    v = None
                    for p in range(PG):
                        bank = PS[p // 4]
                        v = vector.tensor_copy(
                            out=dst_positions[p],
                            in_=bank[:, (p % 4) * 128:(p % 4) * 128 + 128])
                    sm.inc(v, "V")
                sec("vector", _)
                vtok = sm.tok("V")
                st["br"][0] = st["br"][1] = vtok
                return ttok, vtok

            wq_state = {"p": 0}

            def wq_load(j):
                par = wq_state["p"]
                wq_state["p"] ^= 1
                buf = wQ if par == 0 else wQ2
                sname = "LQ" if par == 0 else "LR"

                def _(sync, w, _par=par, _buf=buf, _sn=sname, _j=j):
                    w(st["r_wQ"][_par])
                    sm.inc(sync.dma_start(
                        out=_buf[:, 0:4096].rearrange("p (g f) -> p g f",
                                                      f=512),
                        in_=wqkv_d[:, _j * 512:(_j + 1) * 512]
                        .rearrange("(g p) f -> p g f", p=128)), _sn, 16)
                sec("sync", _)
                return sm.tok(sname), buf, par

            def k_prod(j, src_buf, src_tok_key, tok_off, n_tok, slot0,
                       par=None):
                t_wq, wqb, wqp = wq_load(j)
                nsl = n_tok // 128
                for cb in range(4):
                    page = (j - 2) * 4 + cb
                    bank = 6 + cb % 2

                    def _(tensor, w, _cb=cb, _bank=bank, _twq=t_wq, _wq=wqb):
                        w(_twq)
                        w(st["br"][_bank])
                        w(st[src_tok_key])
                        t = None
                        for p in range(PG):
                            t = tensor.matmul(
                                PS[_bank][:, 0:n_tok],
                                _wq[:, p * 512 + _cb * 128:
                                    p * 512 + _cb * 128 + 128],
                                src_buf[:, p * 512 + tok_off:
                                        p * 512 + tok_off + n_tok],
                                start=(p == 0), stop=(p == PG - 1))
                        sm.inc(t, "T")
                    sec("tensor", _)
                    st["bw"][bank] = sm.tok("T")
                    st["r_wQ"][wqp] = sm.tok("T")
                    if par is not None:
                        st["r_xnTp"][par] = sm.tok("T")
                    else:
                        st["r_qTs"] = sm.tok("T")

                    def _(vector, w, _page=page, _bank=bank):
                        w(st["bw"][_bank])
                        for sl in range(slot0, slot0 + nsl):
                            w(st["r_kv"][sl])
                        v = vector.tensor_scalar(
                            out=kTr[:, _page * 512 + slot0 * 128:
                                    _page * 512 + slot0 * 128 + n_tok],
                            in0=PS[_bank][:, 0:n_tok],
                            scalar1=bqt[:, 8 + _page:9 + _page], scalar2=None,
                            op0=ALU.add)
                        sm.inc(v, "V")
                    sec("vector", _)
                    st["br"][bank] = sm.tok("V")
                for sl in range(slot0, slot0 + nsl):
                    st["t_kv"][sl] = sm.tok("V")

            def v_prod(j, src_buf, src_tok_key, tok_off, n_tok, slot0,
                       par=None):
                t_wq, wqb, wqp = wq_load(j)
                for i in range(n_tok // 128):
                    sl = slot0 + i
                    bank = 6 + i % 2

                    def _(tensor, w, _i=i, _bank=bank, _twq=t_wq, _wq=wqb):
                        w(_twq)
                        w(st["br"][_bank])
                        w(st[src_tok_key])
                        w(ld_all)
                        col = tok_off + _i * 128
                        for p in range(PG):
                            tensor.matmul(
                                PS[_bank][:, 0:512],
                                src_buf[:, p * 512 + col: p * 512 + col + 128],
                                _wq[:, p * 512: p * 512 + 512],
                                start=(p == 0), stop=False)
                        t = tensor.matmul(
                            PS[_bank][:, 0:512], ones1,
                            bqv[0:1, (j - 4) * 512:(j - 4) * 512 + 512],
                            start=False, stop=True)
                        sm.inc(t, "T")
                    sec("tensor", _)
                    st["bw"][bank] = sm.tok("T")
                    st["r_wQ"][wqp] = sm.tok("T")
                    if par is not None:
                        st["r_xnTp"][par] = sm.tok("T")
                    else:
                        st["r_qTs"] = sm.tok("T")

                    def _(vector, w, _sl=sl, _bank=bank, _j=j):
                        w(st["bw"][_bank])
                        w(st["r_kv"][_sl])
                        v = vector.tensor_copy(
                            out=vrb[:, _sl * 1024 + (_j - 4) * 512:
                                    _sl * 1024 + (_j - 4) * 512 + 512],
                            in_=PS[_bank][:, 0:512])
                        sm.inc(v, "V")
                    sec("vector", _)
                    st["br"][bank] = sm.tok("V")
                    st["t_kv"][sl] = sm.tok("V")

            for ht in range(2):
                lt = emit_ln(acc[:, ht * 1024:(ht + 1) * 1024], atk[:, :],
                             lnscr[:, :],
                             pre=[ld_all, st["r_atk"]])
                dsts = [qTs[:, p * 512 + ht * 128: p * 512 + ht * 128 + 128]
                        for p in range(PG)]
                tt, vt = emit_tile_transposes(atk, dsts, pre_t=[lt],
                                              pre_v=[st["r_qTs"]])
                st["r_atk"] = tt
                st["t_qTs"] = vt

            k_prod(2, qTs, "t_qTs", 0, 256, 0)
            k_prod(3, qTs, "t_qTs", 0, 256, 0)
            v_prod(4, qTs, "t_qTs", 0, 256, 0)
            v_prod(5, qTs, "t_qTs", 0, 256, 0)

            def _(vector, w):
                w(st["t_kv"][0])
                w(st["t_kv"][1])
                w(ld_all)
                kh = kTr[:, 0:4096].rearrange("p (g f) -> p g f",
                                              f=512)[:, :, 0:256]
                v = vector.tensor_scalar(out=kh, in0=kh, scalar1=hv[:, 0:1],
                                         scalar2=None, op0=ALU.mult)
                w(sm.inc(v, "V"))
                v = vector.tensor_scalar(out=vrb[:, 0:2048],
                                         in0=vrb[:, 0:2048],
                                         scalar1=hv[:, 0:1], scalar2=None,
                                         op0=ALU.mult)
                sm.inc(v, "V")
            sec("vector", _)
            st["t_kv"][0] = st["t_kv"][1] = sm.tok("V")
            halo_v_done = sm.tok("V")

            ls_cnt = 0
            st["t_xnTp"] = [None, None]
            st["r_xnTp"] = [None, None]

            def emit_front_dma(_s):
                _ax = [acc, acc2][_s % 2]

                def _(sync, w):
                    w(("ST" if _s % 2 == 0 else "SU", 16 * (_s // 2)))
                    if _s == 0:
                        w(halo_v_done)
                        w(st["r_qTs"])
                    sm.inc(sync.dma_start(
                        out=_ax[:, 0:4096].rearrange("p (t f) -> p t f", f=D),
                        in_=x_d[HALO + _s * 512: HALO + (_s + 1) * 512, :]
                        .rearrange("(t p) f -> p t f", p=128)),
                        "LX" if _s % 2 == 0 else "LY", 16)
                sec("sync", _)
                st["t_accld"] = sm.tok("LX" if _s % 2 == 0 else "LY")

            def emit_front_rest(_s):
                nonlocal ls_cnt
                _ax = [acc, acc2][_s % 2]
                _xt = [xnT, xnT2][_s % 2]
                for t in range(4):
                    ab = [atk, atkB][t % 2]
                    key = "r_atk" if t % 2 == 0 else "r_atkB"
                    lsn = "LS" if t % 2 == 0 else "LT"
                    lt = emit_ln(_ax[:, t * 1024:(t + 1) * 1024], ab[:, :],
                                 lnscr[:, :],
                                 pre=[st["t_accld"], st.get(key),
                                      sm.tok(lsn)])
                    def _(scalar, w, _t=t, _lt=lt, _ab=ab, _ls=lsn):
                        w(_lt)
                        sm.inc(scalar.dma_start(
                            out=xns_d[(_s * 4 + _t) * 128:
                                      (_s * 4 + _t) * 128 + 128, :],
                            in_=_ab[:, :]), _ls, 16)
                    sec("scalar", _)
                    ls_cnt += 16
                    dsts = [_xt[:, p * 512 + t * 128: p * 512 + t * 128 + 128]
                            for p in range(PG)]
                    tt, vt = emit_tile_transposes(
                        ab, dsts, pre_t=[lt],
                        pre_v=[st["r_xnTp"][_s % 2]])
                    st[key] = tt
                st["t_xnTp"][_s % 2] = sm.tok("V")

            def emit_front(_s):
                emit_front_dma(_s)
                emit_front_rest(_s)

            emit_front(0)
            for s in range(NSLC):
                g0 = s * 4
                accx = [acc, acc2][s % 2]
                xnT_s = [xnT, xnT2][s % 2]
                st["t_xnT"] = st["t_xnTp"][s % 2]

                for j in (0, 1):
                    t_wq, wqb, wqp = wq_load(j)
                    for cb in range(4):
                        page = j * 4 + cb
                        bank = 6 + cb % 2

                        def _(tensor, w, _cb=cb, _bank=bank, _twq=t_wq,
                              _wq=wqb):
                            w(_twq)
                            w(st["br"][_bank])
                            w(st["t_xnT"])
                            t = None
                            for p in range(PG):
                                t = tensor.matmul(
                                    PS[_bank][:, 0:512],
                                    _wq[:, p * 512 + _cb * 128:
                                        p * 512 + _cb * 128 + 128],
                                    xnT_s[:, p * 512: p * 512 + 512],
                                    start=(p == 0), stop=(p == PG - 1))
                            sm.inc(t, "T")
                        sec("tensor", _)
                        st["bw"][bank] = sm.tok("T")
                        st["r_wQ"][wqp] = sm.tok("T")
                        st["r_xnTp"][s % 2] = sm.tok("T")

                        def _(vector, w, _page=page, _bank=bank):
                            w(st["bw"][_bank])
                            w(st["r_qTs"])
                            v = vector.tensor_scalar(
                                out=qTs[:, _page * 512: _page * 512 + 512],
                                in0=PS[_bank][:, 0:512],
                                scalar1=bqt[:, _page:_page + 1], scalar2=None,
                                op0=ALU.add)
                            sm.inc(v, "V")
                        sec("vector", _)
                        st["br"][bank] = sm.tok("V")
                st["t_qTs"] = sm.tok("V")

                def _(tensor, w):
                    w(st["br"][2])
                    w(st["t_xnT"])
                    w(gwT_rdy)
                    t = None
                    for t4 in range(4):
                        for p in range(PG):
                            t = tensor.matmul(
                                PS[2][:, t4 * 8: t4 * 8 + 8],
                                xnT_s[:, p * 512 + t4 * 128:
                                    p * 512 + t4 * 128 + 128],
                                gwT[:, p * 8:(p + 1) * 8],
                                start=(p == 0), stop=(p == PG - 1))
                    sm.inc(t, "T")
                sec("tensor", _)
                st["bw"][2] = sm.tok("T")
                st["r_xnTp"][s % 2] = sm.tok("T")

                prev_gate_v = sm.tok("V")

                def _(scalar, w, _pg=prev_gate_v):
                    w(st["bw"][2])
                    w(_pg)
                    a = scalar.activation(out=gsc[:, 0:32], in_=PS[2][:, 0:32],
                                          func=AF.Sigmoid)
                    sm.inc(a, "A")
                sec("scalar", _)
                a_g = sm.tok("A")
                st["br"][2] = a_g

                def _(vector, w, _ag=a_g, _s=s):
                    w(_ag)
                    for t4 in range(4):
                        c = t4 * 8
                        gcol = (_s * 4 + t4) * 8
                        v = vector.tensor_tensor(out=gbi[:, c:c + 8],
                                                 in0=gsc[:, c:c + 8],
                                                 in1=ebb[:, 0:8], op=ALU.add)
                        w(sm.inc(v, "V"))
                        v = vector.max(out=gt8[:, c:c + 8], in_=gbi[:, c:c + 8])
                        w(sm.inc(v, "V"))
                        v = vector.tensor_scalar(out=gmf[:, gcol:gcol + 8],
                                                 in0=gbi[:, c:c + 8],
                                                 scalar1=gt8[:, c + 1:c + 2],
                                                 scalar2=None, op0=ALU.is_ge)
                        w(sm.inc(v, "V"))
                        v = vector.tensor_tensor(out=gbi[:, c:c + 8],
                                                 in0=gmf[:, gcol:gcol + 8],
                                                 in1=gsc[:, c:c + 8],
                                                 op=ALU.mult)
                        w(sm.inc(v, "V"))
                        v = vector.tensor_tensor(out=g2f[:, gcol:gcol + 8],
                                                 in0=gbi[:, c:c + 8],
                                                 in1=gbi[:, c:c + 8],
                                                 op=ALU.mult)
                        sm.inc(v, "V")
                sec("vector", _)

                if s + 1 < NSLC:
                    emit_front_dma(s + 1)

                for h in range(2):
                    if h == 1 and s + 1 < NSLC:
                        emit_front_rest(s + 1)
                    gA = g0 + 2 * h
                    slot0 = (gA + 2) % 4
                    k_prod(2, xnT_s, "t_xnT", 2 * h * 128, 256, slot0, par=s % 2)
                    k_prod(3, xnT_s, "t_xnT", 2 * h * 128, 256, slot0, par=s % 2)
                    v_prod(4, xnT_s, "t_xnT", 2 * h * 128, 256, slot0, par=s % 2)
                    v_prod(5, xnT_s, "t_xnT", 2 * h * 128, 256, slot0, par=s % 2)

                    for g in (gA, gA + 1):
                        tloc = g - g0
                        slots = [((kt + 2) % 4, kt - g) for kt in
                                 (g - 2, g - 1, g)]

                        def _(tensor, w, _tloc=tloc, _slots=slots):
                            w(st["br"][2])
                            w(st["t_qTs"])
                            for sl, _d in _slots:
                                w(st["t_kv"][sl])
                            t = None
                            for si, (sl, _d) in enumerate(_slots):
                                for p in range(PG):
                                    t = tensor.matmul(
                                        PS[2][:, si * 128: si * 128 + 128],
                                        kTr[:, p * 512 + sl * 128:
                                            p * 512 + sl * 128 + 128],
                                        qTs[:, p * 512 + _tloc * 128:
                                            p * 512 + _tloc * 128 + 128],
                                        start=(p == 0), stop=(p == PG - 1))
                            sm.inc(t, "T")
                        sec("tensor", _)
                        st["bw"][2] = sm.tok("T")
                        st["r_qTs"] = sm.tok("T")
                        for sl, _d in slots:
                            st["r_kv"][sl] = sm.tok("T")

                        def _(vector, w, _slots=slots):
                            w(st["bw"][2])
                            w(st["r_ST"])
                            w(ld_all)
                            v = None
                            for si, (sl, delta) in enumerate(_slots):
                                src = PS[2][:, si * 128: si * 128 + 128]
                                dst = STb[:, si * 128: si * 128 + 128]
                                if delta == -1:
                                    v = vector.tensor_scalar(
                                        out=dst, in0=src, scalar1=SCL,
                                        scalar2=None, op0=ALU.mult)
                                else:
                                    msk = mask0 if delta == 0 else mask2
                                    v = vector.scalar_tensor_tensor(
                                        out=dst, in0=src, scalar=SCL, in1=msk,
                                        op0=ALU.mult, op1=ALU.mult)
                            sm.inc(v, "V")
                        sec("vector", _)
                        st["br"][2] = sm.tok("V")
                        st["t_ST"] = sm.tok("V")

                        def _(tensor, w, _slots=slots):
                            w(st["t_ST"])
                            w(st["br"][4])
                            w(st["br"][5])
                            for sl, _d in _slots:
                                w(st["t_kv"][sl])
                            t = None
                            for dp in range(PG):
                                bank = 4 + dp // 4
                                for si, (sl, _d) in enumerate(_slots):
                                    t = tensor.matmul(
                                        PS[bank][:, (dp % 4) * 128:
                                                 (dp % 4) * 128 + 128],
                                        vrb[:, sl * 1024 + dp * 128:
                                            sl * 1024 + dp * 128 + 128],
                                        STb[:, si * 128: si * 128 + 128],
                                        start=(si == 0), stop=(si == 2))
                            sm.inc(t, "T")
                        sec("tensor", _)
                        st["bw"][4] = st["bw"][5] = sm.tok("T")
                        st["r_ST"] = sm.tok("T")
                        for sl, _d in slots:
                            st["r_kv"][sl] = sm.tok("T")

                        def _(vector, w):
                            w(st["bw"][4])
                            w(st["r_aT"])
                            w(st["t_aT"])
                            vector.tensor_copy(out=aTb[:, 0:512],
                                               in_=PS[4][:, 0:512])
                            v = vector.tensor_copy(out=aTb[:, 512:1024],
                                                   in_=PS[5][:, 0:512])
                            sm.inc(v, "V")
                        sec("vector", _)
                        st["br"][4] = st["br"][5] = sm.tok("V")
                        st["t_aT"] = sm.tok("V")

                        dsts = [atkC[:, p * 128:(p + 1) * 128]
                                for p in range(PG)]
                        tt, vt = emit_tile_transposes(
                            aTb, dsts, pre_t=[st["t_aT"]],
                            pre_v=[st["r_atkC"]])
                        st["r_aT"] = tt
                        st["t_atk"] = vt

                        lt = emit_ln(atkC[:, :], atkC[:, :], aTb[:, :],
                                     pre=[st["t_atk"], st["r_aT"],
                                          st["t_aT"]], so=4, akey="ln_a2")

                        def _(gpsimd, w, _tloc=tloc, _lt=lt, _ax=accx):
                            w(_lt)
                            w(ld_all)
                            v = gpsimd.tensor_tensor(out=aTb[:, :],
                                                     in0=atkC[:, :],
                                                     in1=lnwb[:, :],
                                                     op=ALU.mult)
                            w(sm.inc(v, "G"))
                            v = gpsimd.tensor_tensor(out=aTb[:, :],
                                                     in0=aTb[:, :],
                                                     in1=lnbb[:, :],
                                                     op=ALU.add)
                            w(sm.inc(v, "G"))
                            if "att" in dbg_skip:
                                v = gpsimd.tensor_scalar(
                                    out=aTb[:, :], in0=aTb[:, :], scalar1=0.0,
                                    scalar2=None, op0=ALU.mult)
                                w(sm.inc(v, "G"))
                            w(st["t_accld"])
                            w(st["acc_wv"])
                            v = gpsimd.tensor_tensor(
                                out=_ax[:, _tloc * 1024:(_tloc + 1) * 1024],
                                in0=_ax[:, _tloc * 1024:(_tloc + 1) * 1024],
                                in1=aTb[:, :], op=ALU.add)
                            sm.inc(v, "G")
                        sec("gpsimd", _)
                        st["r_atkC"] = sm.tok("G")
                        st["acc_wv"] = sm.tok("G")
                        st["t_aT"] = sm.tok("G")
                        st["r_aT"] = sm.tok("G")

                def _(scalar, w, _s=s, _ax=accx):
                    w(st["acc_wv"])
                    sm.inc(scalar.dma_start(
                        out=out_d[_s * 512:(_s + 1) * 512, :]
                        .rearrange("(t p) f -> p t f", p=128),
                        in_=_ax[:, 0:4096].rearrange("p (t f) -> p t f", f=D)),
                        "ST" if _s % 2 == 0 else "SU", 16)
                sec("scalar", _)

            bar = [sm.tok("T"), sm.tok("V"), sm.tok("A"), sm.tok("G"),
                   ("ST", 32), ("SU", 32), sm.tok("LS"), sm.tok("LT"),
                   sm.tok("LQ"), sm.tok("LR"), sm.tok("LX"),
                   sm.tok("LY")]

            for t in range(16):
                def _(tensor, w, _t=t):
                    for tk in bar:
                        w(tk)
                    w(st["br"][2])
                    w(base_tok)
                    tensor.matmul(PS[2][:, 0:8], mask0,
                                  gmf[:, _t * 8: _t * 8 + 8],
                                  start=True, stop=False)
                    tensor.matmul(PS[2][:, 0:8], ones1, base[0:1, :],
                                  start=False, stop=True)
                    t2 = tensor.matmul(PS[2][0:1, 8:16],
                                       cst[:, C_ONE:C_ONE + 1],
                                       gmf[:, _t * 8: _t * 8 + 8],
                                       start=True, stop=True)
                    sm.inc(t2, "T")
                sec("tensor", _)
                st["bw"][2] = sm.tok("T")

                def _(vector, w, _t=t):
                    w(st["bw"][2])
                    vector.tensor_copy(out=cmf[:, _t * 8: _t * 8 + 8],
                                       in_=PS[2][:, 0:8])
                    v = vector.tensor_tensor(out=base[:, :], in0=base[:, :],
                                             in1=PS[2][0:1, 8:16],
                                             op=ALU.add)
                    sm.inc(v, "V")
                sec("vector", _)
                st["br"][2] = sm.tok("V")
                base_tok = sm.tok("V")

            def _(vector, w):
                w(base_tok)
                v = vector.tensor_tensor(out=cmf[:, :], in0=cmf[:, :],
                                         in1=gmf[:, :], op=ALU.mult)
                sm.inc(v, "V")
            sec("vector", _)
            cmsk_tok = sm.tok("V")

            r_E = [None, None]
            r_cg = None
            t_rowv = None
            r_row = None
            for e in range(NE):
                def _(vector, w, _e=e, _rcg=r_cg):
                    w(cmsk_tok)
                    w(_rcg)
                    cgv = cg[:, 0:32].rearrange("p (t c) -> p t c", c=2)
                    vector.tensor_copy(
                        out=cgv[:, :, 0:1],
                        in_=cst[:, C_TOK:C_TOK + 16]
                        .rearrange("p (t c) -> p t c", c=1))
                    v = vector.tensor_copy(
                        out=cgv[:, :, 1:2],
                        in_=g2f[:, 0:128].rearrange("p (t c) -> p t c",
                                                    c=8)[:, :, _e:_e + 1])
                    sm.inc(v, "V")
                sec("vector", _)
                cg_rdy = sm.tok("V")

                for t in range(16):
                    ebuf = t % 2

                    def _(vector, w, _e=e, _t=t, _eb=ebuf):
                        w(cmsk_tok)
                        w(r_E[_eb])
                        v = vector.tensor_scalar(
                            out=Epr[:, _eb * 768:_eb * 768 + 768],
                            in0=cst[:, C_IOTA:C_IOTA + 768],
                            scalar1=cmf[:, _t * 8 + _e: _t * 8 + _e + 1],
                            scalar2=None, op0=ALU.is_equal)
                        sm.inc(v, "V")
                    sec("vector", _)
                    e_tok = sm.tok("V")

                    def _(tensor, w, _t=t, _eb=ebuf, _et=e_tok,
                          _cg=cg_rdy):
                        w(_et)
                        w(_cg)
                        if _t == 0:
                            w(st["br"][2])
                            w(st["br"][3])
                        tensor.matmul(
                            PS[2][0:2, 0:512], cg[:, 2 * _t: 2 * _t + 2],
                            Epr[:, _eb * 768: _eb * 768 + 512],
                            start=(_t == 0), stop=(_t == 15))
                        t2 = tensor.matmul(
                            PS[3][0:2, 0:256], cg[:, 2 * _t: 2 * _t + 2],
                            Epr[:, _eb * 768 + 512: _eb * 768 + 768],
                            start=(_t == 0), stop=(_t == 15))
                        sm.inc(t2, "T")
                    sec("tensor", _)
                    r_E[ebuf] = sm.tok("T")
                r_cg = sm.tok("T")
                st["bw"][2] = st["bw"][3] = sm.tok("T")

                def _(vector, w, _rr=r_row):
                    w(st["bw"][2])
                    w(_rr)
                    vector.tensor_copy(out=idxrow[0:2, 0:512],
                                       in_=PS[2][0:2, 0:512])
                    v = vector.tensor_copy(out=idxrow[0:2, 512:768],
                                           in_=PS[3][0:2, 0:256])
                    sm.inc(v, "V")
                sec("vector", _)
                st["br"][2] = st["br"][3] = sm.tok("V")
                t_rowv = sm.tok("V")

                def _(tensor, w, _tv=t_rowv):
                    w(_tv)
                    w(st["br"][4])
                    t2 = None
                    for ch in range(NCH):
                        t2 = tensor.transpose(
                            PS[4][:, ch * 2: ch * 2 + 2],
                            idxrow[0:2, ch * 128: ch * 128 + 128],
                            ident[0:2, 0:2])
                    sm.inc(t2, "T")
                sec("tensor", _)
                st["bw"][4] = sm.tok("T")
                r_row = sm.tok("T")

                def _(vector, w, _e=e):
                    w(st["bw"][4])
                    p4 = PS[4][:, 0:12].rearrange("p (g f) -> p g f", f=2)
                    vector.tensor_copy(
                        out=idxc[:, _e * NCH:(_e + 1) * NCH]
                        .rearrange("p (g f) -> p g f", f=1),
                        in_=p4[:, :, 0:1])
                    v = vector.tensor_copy(
                        out=g2gc[:, _e * NCH:(_e + 1) * NCH]
                        .rearrange("p (g f) -> p g f", f=1),
                        in_=p4[:, :, 1:2])
                    sm.inc(v, "V")
                sec("vector", _)
                st["br"][4] = sm.tok("V")
            idx_rdy = sm.tok("V")

            r_xgs = None
            t_xgT = None
            r_xgT = None
            t_yg = [None] * NCH
            r_yg = None
            lg_cnt = 0
            sc_cnt = 0

            def emit_gather(_e):
                nonlocal lg_cnt
                def _(gpsimd, w, _rx=r_xgs):
                    for tk in bar:
                        w(tk)
                    w(idx_rdy)
                    w(_rx)
                    for ti in range(NCH):
                        sm.inc(gpsimd.indirect_dma_start(
                            out=xgs[:, ti * 1024: ti * 1024 + 1024],
                            out_offset=None,
                            in_=xns_d[:, :],
                            in_offset=IndirectOffsetOnAxis(
                                ap=idxc[:, _e * NCH + ti: _e * NCH + ti + 1],
                                axis=0)), "LG", 16)
                sec("gpsimd", _)
                lg_cnt += 16 * NCH
                return ("LG", lg_cnt)

            def emit_scatter(_e, _tyg, _sc):
                def _(gpsimd, w):
                    for tk in _tyg:
                        w(tk)
                    for ti in range(NCH):
                        gpsimd.wait_ge(sm.sem["SC"], _sc + 16 * ti)
                        sm.inc(gpsimd.indirect_dma_start(
                            out=out_d[:, :],
                            out_offset=IndirectOffsetOnAxis(
                                ap=idxc[:, _e * NCH + ti: _e * NCH + ti + 1],
                                axis=0),
                            in_=yg[:, ti * 1024: ti * 1024 + 1024],
                            in_offset=None,
                            compute_op=ALU.add), "SC", 16)
                sec("gpsimd", _)

            def emit_xg_transposes(_tx):
                nonlocal r_xgs, r_xgT
                for p in range(PG):
                    for half in range(2):
                        n_ti = 4 if half == 0 else 2
                        bank = (p * 2 + half) % 4

                        def _(tensor, w, _p=p, _h=half, _n=n_ti, _bank=bank,
                              _tx2=_tx):
                            w(_tx2)
                            w(st["br"][_bank])
                            w(r_xgT)
                            t = None
                            for i in range(_n):
                                ti = _h * 4 + i
                                t = tensor.transpose(
                                    PS[_bank][:, i * 128: i * 128 + 128],
                                    xgs[:, ti * 1024 + _p * 128:
                                        ti * 1024 + _p * 128 + 128],
                                    ident)
                            sm.inc(t, "T")
                        sec("tensor", _)
                        st["bw"][bank] = sm.tok("T")
                        r_xgs = sm.tok("T")

                        def _(vector, w, _p=p, _h=half, _n=n_ti, _bank=bank):
                            w(st["bw"][_bank])
                            off = _p * CAP + _h * 512
                            v = vector.tensor_scalar(
                                out=xgTh[:, off:off + _n * 128],
                                in0=PS[_bank][:, 0:_n * 128],
                                scalar1=1.0, scalar2=None, op0=ALU.mult)
                            w(sm.inc(v, "V"))
                            v = vector.scalar_tensor_tensor(
                                out=xgTl[:, off:off + _n * 128],
                                in0=PS[_bank][:, 0:_n * 128],
                                scalar=1.0, in1=xgTh[:, off:off + _n * 128],
                                op0=ALU.mult, op1=ALU.subtract)
                            sm.inc(v, "V")
                        sec("vector", _)
                        st["br"][bank] = sm.tok("V")
                return (sm.tok("V"), sm.tok("V"))

            t_xgs = emit_gather(0)
            xg_done = emit_xg_transposes(t_xgs)
            def emit_wstage(pg):
                _e, _ph = pg // 8, pg % 8
                _up = _ph < 4
                _q = _ph % 4
                _wi = pg % 2

                def _(sync, w, _e=_e, _q=_q, _up=_up, _wi=_wi):
                    for tk in bar:
                        w(tk)
                    w(st["r_stage"][_wi])
                    ln = "L0" if _wi == 0 else "L1"
                    if _up:
                        src = wut_d[_e, :, _q * 512:(_q + 1) * 512]
                    else:
                        src = wdt_d[_e, :, _q * 256:(_q + 1) * 256]
                    sm.inc(sync.dma_start(
                        out=wst[_wi][:, 0:4096].rearrange(
                            "p (c f) -> p c f", f=512 if _up else 256),
                        in_=src.rearrange("(g p) f -> p g f", p=128)),
                        ln, 16)
                sec("sync", _)
                t_dma = sm.tok("L0" if _wi == 0 else "L1")

                def _(scalar, w, _wi=_wi, _td=t_dma, _rw=st["r_wh"][_wi]):
                    w(_td)
                    w(_rw)
                    a = scalar.activation(
                        out=wBh[_wi][:, :], in_=wst[_wi][:, :],
                        func=AF.Identity)
                    sm.inc(a, "A")
                sec("scalar", _)
                t_whi = sm.tok("A")

                def _(gpsimd, w, _wi=_wi, _td=t_dma, _th=t_whi,
                      _rw=st["r_wh"][_wi]):
                    w(_td)
                    w(_th)
                    w(_rw)
                    g = gpsimd.tensor_tensor(
                        out=wBl[_wi][:, :], in0=wst[_wi][:, :],
                        in1=wBh[_wi][:, :], op=ALU.subtract)
                    sm.inc(g, "G")
                sec("gpsimd", _)
                st["t_wA"][_wi] = t_whi
                st["t_wlo"][_wi] = sm.tok("G")
                st["r_stage"][_wi] = sm.tok("G")

            emit_wstage(0)
            pend_scatter = None
            for e in range(NE):

                for ph in range(8):
                    up = ph < 4
                    q = ph % 4
                    wi = (e * 8 + ph) % 2

                    if ph == 1 and pend_scatter is not None:
                        emit_scatter(*pend_scatter)
                        pend_scatter = None
                    if ph == 5 and e + 1 < NE:
                        xg_done_next = emit_xg_transposes(emit_gather(e + 1))
                    if e * 8 + ph + 1 < NE * 8:
                        emit_wstage(e * 8 + ph + 1)

                    if up:
                        for m in range(4):
                            hp = q * 4 + m
                            for c in range(2):
                                ntk = 512 if c == 0 else 256
                                bank = 4 + (m * 2 + c) % 4

                                def _(tensor, w, _m=m, _c=c, _n=ntk,
                                      _bank=bank, _wi=wi):
                                    w(st["t_wA"][_wi])
                                    w(xg_done[0])
                                    w(xg_done[1])
                                    w(st["br"][_bank])
                                    wh = wBh[_wi][:, :]
                                    xh = xgTh[:, :]
                                    xl = xgTl[:, :]
                                    t = None
                                    first = True
                                    for p in range(PG):
                                        wcol = p * 512 + _m * 128
                                        xcol = p * CAP + _c * 512
                                        for rh in (xh, xl):
                                            t = tensor.matmul(
                                                PS[_bank][:, 0:_n],
                                                wh[:, wcol:wcol + 128],
                                                rh[:, xcol:xcol + _n],
                                                start=first, stop=False)
                                            first = False
                                    sm.inc(t, "T")
                                sec("tensor", _)

                                def _(tensor, w, _m=m, _c=c, _n=ntk,
                                      _bank=bank, _wi=wi):
                                    w(st["t_wlo"][_wi])
                                    wl = wBl[_wi][:, :]
                                    xh = xgTh[:, :]
                                    t = None
                                    for p in range(PG):
                                        wcol = p * 512 + _m * 128
                                        xcol = p * CAP + _c * 512
                                        t = tensor.matmul(
                                            PS[_bank][:, 0:_n],
                                            wl[:, wcol:wcol + 128],
                                            xh[:, xcol:xcol + _n],
                                            start=False, stop=(p == PG - 1))
                                    sm.inc(t, "T")
                                sec("tensor", _)
                                st["bw"][bank] = sm.tok("T")
                                st["r_wA"][wi] = sm.tok("T")
                                st["r_wh"][wi] = sm.tok("T")
                                r_xgT = sm.tok("T")

                                def _(vector, w, _hp=hp, _c=c, _n=ntk,
                                      _bank=bank):
                                    w(st["bw"][_bank])
                                    w(st["r_Hb"])
                                    off = _hp * CAP + _c * 512
                                    v = vector.tensor_scalar(
                                        out=rsq[:, 0:_n],
                                        in0=PS[_bank][:, 0:_n],
                                        scalar1=0.0, scalar2=None,
                                        op0=ALU.max)
                                    w(sm.inc(v, "V"))
                                    v = vector.tensor_tensor(
                                        out=PS[_bank][:, 0:_n],
                                        in0=rsq[:, 0:_n],
                                        in1=rsq[:, 0:_n],
                                        op=ALU.mult)
                                    w(sm.inc(v, "V"))
                                    v = vector.tensor_scalar(
                                        out=Hh[:, off:off + _n],
                                        in0=PS[_bank][:, 0:_n],
                                        scalar1=1.0, scalar2=None,
                                        op0=ALU.mult)
                                    w(sm.inc(v, "V"))
                                    v = vector.scalar_tensor_tensor(
                                        out=Hl[:, off:off + _n],
                                        in0=PS[_bank][:, 0:_n],
                                        scalar=1.0, in1=Hh[:, off:off + _n],
                                        op0=ALU.mult, op1=ALU.subtract)
                                    sm.inc(v, "V")
                                sec("vector", _)
                                st["br"][bank] = sm.tok("V")
                        st["t_Hb"] = sm.tok("V")
                    else:
                        for ti in range(NCH):
                            bank = 4 + ti % 4

                            def _(tensor, w, _ti=ti, _bank=bank, _wi=wi):
                                w(st["t_wA"][_wi])
                                w(st["t_Hb"])
                                w(st["br"][_bank])
                                hl = Hl[:, :]
                                wh = wBh[_wi][:, :]
                                t = None
                                first = True
                                for hp in range(16):
                                    hcol = hp * CAP + _ti * 128
                                    wcol = hp * 256
                                    t = tensor.matmul(
                                        PS[_bank][:, 0:256],
                                        hl[:, hcol:hcol + 128],
                                        wh[:, wcol:wcol + 256],
                                        start=first, stop=False)
                                    first = False
                                sm.inc(t, "T")
                            sec("tensor", _)

                            def _(tensor, w, _ti=ti, _bank=bank, _wi=wi):
                                w(st["t_wlo"][_wi])
                                hh = Hh[:, :]
                                wh = wBh[_wi][:, :]
                                wl = wBl[_wi][:, :]
                                t = None
                                for hp in range(16):
                                    hcol = hp * CAP + _ti * 128
                                    wcol = hp * 256
                                    for rh in (wh, wl):
                                        t = tensor.matmul(
                                            PS[_bank][:, 0:256],
                                            hh[:, hcol:hcol + 128],
                                            rh[:, wcol:wcol + 256],
                                            start=False,
                                            stop=(hp == 15 and rh is wl))
                                sm.inc(t, "T")
                            sec("tensor", _)
                            st["bw"][bank] = sm.tok("T")
                            st["r_wA"][wi] = sm.tok("T")
                            st["r_wh"][wi] = sm.tok("T")
                            st["r_Hb"] = sm.tok("T")

                            def _(vector, w, _ti=ti, _q=q, _e=e, _bank=bank):
                                w(st["bw"][_bank])
                                w(idx_rdy)
                                w(r_yg)
                                sc1 = (0.0 if "moe" in dbg_skip else
                                       g2gc[:, _e * NCH + _ti:
                                            _e * NCH + _ti + 1])
                                v = vector.tensor_scalar(
                                    out=yg[:, _ti * 1024 + _q * 256:
                                           _ti * 1024 + _q * 256 + 256],
                                    in0=PS[_bank][:, 0:256],
                                    scalar1=sc1, scalar2=None, op0=ALU.mult)
                                sm.inc(v, "V")
                            sec("vector", _)
                            st["br"][bank] = sm.tok("V")
                            t_yg[ti] = sm.tok("V")

                pend_scatter = (e, list(t_yg), sc_cnt)
                sc_cnt += 16 * NCH
                r_yg = ("SC", sc_cnt)
                if e + 1 < NE:
                    xg_done = xg_done_next

            if pend_scatter is not None:
                emit_scatter(*pend_scatter)

            def _(gpsimd, w):
                gpsimd.wait_ge(sm.sem["ST"], 32)
                gpsimd.wait_ge(sm.sem["SU"], 32)
                gpsimd.wait_ge(sm.sem["SC"], sc_cnt)
            sec("gpsimd", _)

    return nc


def kernel(**inputs):
    x = np.ascontiguousarray(np.asarray(inputs["L0_attn_x"], dtype=np.float32))
    wqkv = np.ascontiguousarray(np.asarray(inputs["L0_attn_W_qkv"],
                                           dtype=np.float32))
    bqkv = np.asarray(inputs["L0_attn_b_qkv"], dtype=np.float32)
    lnw = np.asarray(inputs["L0_attn_ln_w"], dtype=np.float32)
    lnb = np.asarray(inputs["L0_attn_ln_b"], dtype=np.float32)
    gatet = np.ascontiguousarray(
        np.asarray(inputs["L0_ffn_gate_w"], dtype=np.float32).T)
    wut = np.ascontiguousarray(
        np.asarray(inputs["L0_ffn_w_up"], dtype=np.float32).transpose(0, 2, 1))
    wdt = np.ascontiguousarray(
        np.asarray(inputs["L0_ffn_w_down"],
                   dtype=np.float32).transpose(0, 2, 1))
    ebias = np.asarray(inputs["L0_ffn_expert_bias"], dtype=np.float32)

    B, T, C = x.shape
    assert (B, T, C) == (2, 8192, 1024)

    bqkvt = np.ascontiguousarray(bqkv.reshape(24, 128).T)
    bqkvv = np.ascontiguousarray(bqkv[2048:].reshape(1, 1024))
    lnwb = np.ascontiguousarray(np.broadcast_to(lnw[None, :], (128, 1024)))
    lnbb = np.ascontiguousarray(np.broadcast_to(lnb[None, :], (128, 1024)))
    eb1 = np.ascontiguousarray(ebias.reshape(1, 8))

    kk = np.arange(128)[:, None]
    qq = np.arange(128)[None, :]
    cstm = np.zeros((128, C_TOT), dtype=np.float32)
    cstm[:, C_ID:C_ID + 128] = np.eye(128, dtype=np.float32)
    cstm[:, C_M0:C_M0 + 128] = (kk <= qq).astype(np.float32)
    cstm[:, C_M2:C_M2 + 128] = (kk > qq).astype(np.float32)
    cstm[:, C_ONE:C_ONE + 128] = 1.0
    cstm[:, C_IOTA:C_IOTA + CAP] = (np.arange(CAP) + 1)[None, :]
    for t in range(16):
        cstm[:, C_TOK + t] = np.arange(128) + 128 * t

    in_maps = []
    for c in range(8):
        b = c // 4
        s0 = (c % 4) * 2048
        if c % 4 == 0:
            xc = np.concatenate(
                [np.zeros((HALO, C), dtype=np.float32), x[b, 0:T_OWN]], axis=0)
            hvv = np.zeros((128, 1), dtype=np.float32)
        else:
            xc = x[b, s0 - HALO: s0 + T_OWN]
            hvv = np.ones((128, 1), dtype=np.float32)
        in_maps.append({
            "x": np.ascontiguousarray(xc),
            "wqkv": wqkv, "bqkvt": bqkvt, "bqkvv": bqkvv,
            "gatet": gatet, "eb1": eb1, "wut": wut, "wdt": wdt,
            "lnw": lnwb, "lnb": lnbb, "cst": cstm, "hv": hvv,
        })

    nc = build_program()
    res = run_bass_kernel_spmd(nc, in_maps, core_ids=list(range(8)))

    out = np.empty((2, 8192, 1024), dtype=np.float32)
    for c in range(8):
        b = c // 4
        s0 = (c % 4) * 2048
        out[b, s0:s0 + T_OWN] = res.results[c]["out"]
    return out
